# version 10
# baseline (speedup 1.0000x reference)
import contextlib
import numpy as np
import concourse.bass as bass
import concourse.mybir as mybir

F32 = mybir.dt.float32
BF16 = mybir.dt.bfloat16
AF = mybir.ActivationFunctionType
ALU = mybir.AluOpType
AX = mybir.AxisListType.X
P = 128


class Cfg:
    def __init__(self, D=1024, L=8192, H=4096, STATE=64, TT_FFN=1024):
        self.D, self.L, self.H, self.STATE = D, L, H, STATE
        self.INNER, self.O3 = 2 * D, 6 * D
        self.TAPS = 128
        self.TT_FFN = TT_FFN
        assert L % TT_FFN == 0 and L % P == 0 and L % 512 == 0
        self.NCH = L // P



class RecInst:
    def __init__(self, method, args, kwargs, is_nop=False):
        self.method, self.args, self.kwargs = method, args, kwargs
        self.incs = []
        self.is_nop = is_nop


class RecWait:
    def __init__(self, sem_name, val):
        self.sem_name, self.val = sem_name, val


class Rec:
    def __init__(self, owner):
        self._ops = []
        self._owner = owner

    def wait_ge(self, sem_handle_or_name, val):
        self._ops.append(RecWait(sem_handle_or_name, val))

    def nop(self):
        e = RecInst("nop", (), {}, is_nop=True)
        e.rec = self
        self._ops.append(e)
        return e

    def __getattr__(self, m):
        if m.startswith("_"):
            raise AttributeError(m)

        def call(*a, **k):
            e = RecInst(m, a, k)
            e.rec = self
            self._ops.append(e)
            return e
        return call

    def last_real(self):
        for e in reversed(self._ops):
            if isinstance(e, RecInst):
                return None if e.is_nop else e
        return None

    def replay(self, eng, sems):
        for e in self._ops:
            if isinstance(e, RecWait):
                eng.wait_ge(sems[e.sem_name], e.val)
            else:
                if e.is_nop and not e.incs:
                    continue
                meth = "drain" if (e.is_nop and e.incs) else e.method
                inst = getattr(eng, meth)(*(() if meth == "drain" else e.args),
                                          **({} if meth == "drain" else e.kwargs))
                for (nm, kk) in e.incs:
                    inst.then_inc(sems[nm], kk)


class MK:
    SEMS = ["ePE", "eSC", "eVE", "eGP", "eSY"]

    def __init__(self, cfg: Cfg, stop_after=None):
        self.c = cfg
        self.stop_after = stop_after
        self.nc = bass.Bass("TRN2", target_bir_lowering=False)
        self.counts = {}
        self.sems = {}
        self.dma_streams = []
        self.mk = {}
        self._entry_rec = {}
        self.rT, self.rS, self.rV = Rec(self), Rec(self), Rec(self)
        self.rG, self.rSY = Rec(self), Rec(self)
        self._stack = contextlib.ExitStack()

    def stream(self, name):
        if name not in self.sems:
            self.sems[name] = self._stack.enter_context(self.nc.semaphore(name))
            self.counts[name] = 0
            self.dma_streams.append(name)
        return name

    def dma(self, rec, stname, out, in_, **kw):
        st = self.stream(stname)
        inst = rec.dma_start(out=out, in_=in_, **kw)
        c = self.inc(st, inst, 16)
        return (st, c)

    def wm(self, eng, mark):
        if mark:
            self.w(eng, mark[0], mark[1])

    def wm2(self, eng, mark):
        if mark:
            self.w(eng, mark[0], mark[1])

    def inc(self, name, entry, k=1):
        if entry.is_nop:
            rec = entry.rec
            idx = rec._ops.index(entry)
            prev = rec._ops[idx - 1] if idx > 0 else None
            if isinstance(prev, RecInst) and not prev.is_nop and not prev.incs:
                entry = prev
        if entry.incs and not entry.is_nop:
            nxt = RecInst("drain", (), {})
            nxt.rec = entry.rec
            pos = entry.rec._ops.index(entry) + 1
            entry.rec._ops.insert(pos, nxt)
            entry = nxt
        entry.incs.append((name, k))
        self.counts[name] += k
        return self.counts[name]

    def w(self, eng, name, val):
        if val and val > 0:
            eng.wait_ge(name, val)

    def _run(self, rec, f):
        f(rec)

    def T(self, f): self._run(self.rT, f)
    def S(self, f): self._run(self.rS, f)
    def V(self, f): self._run(self.rV, f)
    def G(self, f): self._run(self.rG, f)
    def SY(self, f): self._run(self.rSY, f)

    def build(self):
        c, nc = self.c, self.nc
        D, L, H, INNER, O3, STATE, TAPS = c.D, c.L, c.H, c.INNER, c.O3, c.STATE, c.TAPS
        NH, ND, NI, NO3 = H // P, D // P, INNER // P, O3 // P
        st = self._stack

        for s in self.SEMS:
            self.sems[s] = st.enter_context(nc.semaphore(s))
            self.counts[s] = 0

        def din(name, shape):
            return nc.dram_tensor(name, shape, F32, kind="ExternalInput")
        self.t_x = din("x", [L, D])
        self.t_gamma, self.t_beta = din("norm_gamma", [D]), din("norm_beta", [D])
        self.t_inw, self.t_inb = din("in_w", [D, O3]), din("in_b", [O3])
        self.t_convw, self.t_convb = din("conv_w", [INNER, 3]), din("conv_b", [INNER])
        self.t_alog, self.t_bp = din("A_log", [INNER, STATE]), din("B_p", [INNER, STATE])
        self.t_cp, self.t_dp = din("C_p", [INNER, STATE]), din("D_p", [INNER])
        self.t_outw, self.t_outb = din("out_w", [INNER, D]), din("out_b", [D])
        self.t_w1, self.t_w1s = din("w1_w", [H, D]), din("w1_s", [H])
        self.t_w2, self.t_w2s = din("w2_w", [H, D]), din("w2_s", [H])
        self.t_h1, self.t_h1s = din("h1_w", [H, H]), din("h1_s", [H])
        self.t_h2, self.t_h2s = din("h2_w", [H, H]), din("h2_s", [H])
        self.t_w3, self.t_w3s = din("w3_w", [D, H]), din("w3_s", [D])
        self.t_out = nc.dram_tensor("out", [L, D], F32, kind="ExternalOutput")

        sa = self.stop_after

        def scratch(name, shape, dtype, dbg=False):
            return nc.dram_tensor(name, shape, dtype,
                                  kind="ExternalOutput" if dbg else "Internal")
        self.t_w1p = scratch("w1p", [NH, P, D], BF16, sa == "p0")
        self.t_w2p = scratch("w2p", [NH, P, D], BF16)
        self.t_h1p = scratch("h1p", [NH, P, H], BF16)
        self.t_h2p = scratch("h2p", [NH, P, H], BF16)
        self.t_w3p = scratch("w3p", [ND, P, H], BF16, sa == "p0")
        self.t_inwp = scratch("inwp", [NO3, P, D], BF16, sa == "p0")
        self.t_outwp = scratch("outwp", [ND, P, INNER], BF16)
        self.t_kpad = scratch("kpad", [INNER, 3 * TAPS], BF16, sa == "p0")
        self.t_wfull = scratch("wfull", [INNER, P, 2 * TAPS], BF16, sa == "p0")
        self.t_vinb = scratch("vinb_dbg", [P, NO3], F32, sa == "p0")
        self.t_x1 = scratch("x1", [L, D], F32, sa == "p1")
        self.t_r2 = scratch("r2", [L, D], F32, sa in ("a1", "a2", "scan"))
        self.t_u = scratch("u_pre", [INNER, L], F32, sa == "a1")
        self.t_sv = scratch("sv", [INNER, L], BF16, sa == "a1")
        self.t_sg = scratch("sg", [INNER, L], BF16)
        self.t_uT = scratch("uT", [INNER // P, P, c.NCH * P], BF16, sa == "a2")
        self.t_z = scratch("z", [INNER, L], BF16, sa == "scan")
        self.t_x2 = scratch("x2", [L, D], F32, sa == "oproj")

        def sb(name, shape, dtype=F32):
            return st.enter_context(nc.sbuf_tensor(name, shape, dtype))
        self.idn = sb("idn", [P, P])
        self.idnb = sb("idnb", [P, P], BF16)
        self.iota = sb("iota", [P, TAPS])
        self.v_w1s, self.v_w2s = sb("v_w1s", [P, NH]), sb("v_w2s", [P, NH])
        self.v_h1s, self.v_h2s = sb("v_h1s", [P, NH]), sb("v_h2s", [P, NH])
        self.v_w3s = sb("v_w3s", [P, ND])
        self.v_gamma, self.v_beta = sb("v_gamma", [P, ND]), sb("v_beta", [P, ND])
        self.v_inb = sb("v_inb", [P, NO3])
        self.v_convw = sb("v_convw", [P, 3 * NI])
        self.v_convb = sb("v_convb", [P, NI])
        self.v_outb = sb("v_outb", [P, ND])
        self.v_dp = sb("v_dp", [P, NI])
        self.v_eps = sb("v_eps", [P, 1])
        self.sb_na = sb("sb_na", [P, NI * STATE])
        self.sb_cb = sb("sb_cb", [P, NI * STATE])
        self.sb_ke = sb("sb_ke", [P, 4 * self.c.TAPS], BF16)
        self.sb_k = sb("sb_k", [P, self.c.TAPS])
        self.sb_k2 = sb("sb_k2", [P, self.c.TAPS])
        self.sb_z3 = sb("sb_z3", [P, self.c.TAPS + 1])

        done = False
        self.phase0()
        self.barrier()
        if sa == "p0":
            done = True
        if not done:
            self.ffn(self.t_x, self.t_x1, side_work=self.kbuild_side_work())
            self.barrier()
            done = sa == "p1"
        if not done:
            self.mixer_a1()
            self.barrier()
            done = sa == "a1"
        if not done:
            self.mixer_a2()
            self.barrier()
            done = sa == "a2"
        if not done:
            self.scan()
            self.barrier()
            done = sa == "scan"
        if not done:
            self.outproj()
            self.barrier()
            done = sa == "oproj"
        if not done:
            self.ffn(self.t_x2, self.t_out)
        self.final_wait()
        self._auto_drain()
        with nc.Block() as blk:
            blk.tensor(lambda e: self.rT.replay(e, self.sems))
            blk.scalar(lambda e: self.rS.replay(e, self.sems))
            blk.vector(lambda e: self.rV.replay(e, self.sems))
            blk.gpsimd(lambda e: self.rG.replay(e, self.sems))
            blk.sync(lambda e: self.rSY.replay(e, self.sems))
        return nc

    def _auto_drain(self):
        for rec in (self.rV, self.rS, self.rG):
            newops = []
            pend = False
            for e in rec._ops:
                if isinstance(e, RecInst) and not e.is_nop and e.method != "drain":
                    if pend:
                        d = RecInst("drain", (), {})
                        d.rec = rec
                        newops.append(d)
                    newops.append(e)
                    pend = True
                else:
                    if isinstance(e, RecInst) and e.method == "drain":
                        pend = False
                    newops.append(e)
            rec._ops[:] = newops

    def barrier(self):
        cn = self.counts
        eng = [("ePE", self.T), ("eSC", self.S), ("eVE", self.V), ("eGP", self.G), ("eSY", self.SY)]
        sync_streams = [st for st in self.dma_streams if st.startswith("s_")]
        g_streams = [st for st in self.dma_streams
                     if st.startswith("g_") and st != "g_wfull"]
        for nm, sec in eng:
            def f(e, nm=nm):
                if nm == "eSY":
                    for st in sync_streams:
                        self.w(e, st, cn[st])
                if nm == "eGP":
                    for st in g_streams:
                        self.w(e, st, cn[st])
                self.inc(nm, e.nop())
            sec(f)
        targ = {nm: cn[nm] for nm, _ in eng}
        for nm, sec in eng:
            def f(e, nm=nm):
                for o, _ in eng:
                    if o != nm:
                        e.wait_ge(o, targ[o])
                self.inc(nm, e.nop())
            sec(f)
        targ2 = {nm: cn[nm] for nm, _ in eng}
        for nm, sec in eng:
            def f(e, nm=nm):
                for o, _ in eng:
                    if o != nm:
                        e.wait_ge(o, targ2[o])
            sec(f)

    def final_wait(self):
        self.barrier()
        tot = {st: self.counts[st] for st in self.dma_streams}

        def f(g):
            for st, v in tot.items():
                self.w(g, st, v)
        self.G(f)

    def phase0(self):
        c, nc = self.c, self.nc
        D, H, INNER, O3, STATE, TAPS = c.D, c.H, c.INNER, c.O3, c.STATE, c.TAPS
        NH, ND, NI, NO3 = H // P, D // P, INNER // P, O3 // P
        mk = self.mk
        WMAX = max(O3, H)

        _es = contextlib.ExitStack()
        ps_a = _es.enter_context(nc.psum_tensor("p0ps_a", [P, 512], F32))
        ps_c = _es.enter_context(nc.psum_tensor("p0ps_c", [P, 512], F32))
        ps_d = _es.enter_context(nc.psum_tensor("p0ps_d", [P, 512], F32))
        p0_vec = _es.enter_context(nc.sbuf_tensor("p0_vec", [P, P], F32))
        p0_ld = _es.enter_context(nc.sbuf_tensor("p0_ld", [P, 2 * WMAX], F32))
        p0_bf = _es.enter_context(nc.sbuf_tensor("p0_bf", [P, 2 * WMAX], BF16))
        p0_ev = _es.enter_context(nc.sbuf_tensor("p0_ev", [P, 4 * P], BF16))
        p0_st = _es.enter_context(nc.sbuf_tensor("p0_st", [P, 2 * 32 * P], BF16))
        p0_as = _es.enter_context(nc.sbuf_tensor("p0_as", [P, NI * STATE], F32))
        p0_e = _es.enter_context(nc.sbuf_tensor("p0_e", [P, STATE * TAPS], F32))
        p0_k = _es.enter_context(nc.sbuf_tensor("p0_k", [P, TAPS], F32))
        p0_k2 = _es.enter_context(nc.sbuf_tensor("p0_k2", [P, TAPS], F32))
        p0_z = _es.enter_context(nc.sbuf_tensor("p0_z", [P, 3 * TAPS], F32))
        p0_bias = _es.enter_context(nc.sbuf_tensor("p0_bias", [P, NO3], F32))
        with _es:
            def f(g):
                self.inc("eGP", g.memset(self.idn[:], 0.0))
                self.inc("eGP", g.memset(self.idnb[:], 0.0))
                g.iota(self.iota[:], pattern=[[1, TAPS]], base=0, channel_multiplier=0,
                        allow_small_or_imprecise_dtypes=True)
                g.memset(p0_z[:], 0.0)
                g.memset(self.v_eps[:], 1e-5)
                self.inc("eGP", g.memset(p0_bias[:], 0.0))
                self.w(g, "eGP", self.counts["eGP"])
                self.inc("eGP", g.affine_select(out=self.idn[:], in_=self.idn[:],
                                compare_op=ALU.not_equal,
                                fill=1.0, base=0, pattern=[[-1, P]], channel_multiplier=1))
                self.inc("eGP", g.affine_select(out=self.idnb[:], in_=self.idnb[:],
                                compare_op=ALU.not_equal,
                                fill=1.0, base=0, pattern=[[-1, P]], channel_multiplier=1))
                self.w(g, "eGP", self.counts["eGP"])
            self.G(f)
            g_consts = self.counts["eGP"]

            vecs = [(self.t_w1s, self.v_w1s, NH, None), (self.t_w2s, self.v_w2s, NH, None),
                    (self.t_h1s, self.v_h1s, NH, None), (self.t_h2s, self.v_h2s, NH, None),
                    (self.t_w3s, self.v_w3s, ND, 0.5),
                    (self.t_gamma, self.v_gamma, ND, None), (self.t_beta, self.v_beta, ND, None),
                    (self.t_convb, self.v_convb, NI, None), (self.t_outb, self.v_outb, ND, None),
                    (self.t_dp, self.v_dp, NI, None)]
            for vi, (src, dst, n_, fold) in enumerate(vecs):
                def f(sy, src=src, n_=n_):
                    self.w(sy, "ePE", mk.get(("p0v_tp", vi - 1), 0))
                    self.dma(sy, "s_vec", p0_vec[0:n_, :],
                             src.ap().rearrange("(a b) -> a b", b=P))
                self.SY(f)
                qwm = ("s_vec", self.counts["s_vec"])

                def f(t, n_=n_, qwm=qwm, vi=vi):
                    if vi == 0:
                        self.w(t, "eGP", g_consts)
                    self.wm(t, qwm)
                    self.w(t, "eSC", mk.get(("p0v_ev", vi - 1), 0))
                    mk[("p0v_tp", vi)] = self.inc(
                        "ePE", t.matmul(ps_a[:, 0:n_], p0_vec[0:n_, :], self.idn[0:n_, 0:n_], start=True, stop=True))
                self.T(f)

                def f(s, dst=dst, n_=n_, fold=fold, vi=vi):
                    self.w(s, "ePE", mk[("p0v_tp", vi)])
                    if fold is not None:
                        s.activation(dst[:, 0:n_], ps_a[:, 0:n_], AF.Identity, scale=fold)
                        mk[("p0v_ev", vi)] = self.inc("eSC", s.nop())
                    else:
                        mk[("p0v_ev", vi)] = self.inc("eSC", s.copy(dst[:, 0:n_], ps_a[:, 0:n_]))
                self.S(f)

            else:
                NI_scan = NI
            def f(sy):
                for ct in range(NI_scan):
                    self.dma(sy, "s_const", self.v_convw[:, 3 * ct:3 * ct + 3],
                             self.t_convw.ap()[P * ct:P * ct + P, :])
                self.dma(sy, "s_const", p0_as[:].rearrange("p (ct s) -> p ct s", s=STATE),
                         self.t_alog.ap().rearrange("(ct p) s -> p ct s", p=P))
                self.dma(sy, "s_const", self.sb_na[:].rearrange("p (ct s) -> p ct s", s=STATE),
                         self.t_bp.ap().rearrange("(ct p) s -> p ct s", p=P))
                self.dma(sy, "s_const", self.sb_cb[:].rearrange("p (ct s) -> p ct s", s=STATE),
                         self.t_cp.ap().rearrange("(ct p) s -> p ct s", p=P))
            self.SY(f)
            qw_scan = ("s_const", self.counts["s_const"])

            def f(v):
                self.wm(v, qw_scan)
                self.inc("eVE", v.tensor_mul(self.sb_cb[:], self.sb_cb[:], self.sb_na[:]))
            self.V(f)
            ve_cb = self.counts["eVE"]

            def f(s):
                self.w(s, "eVE", ve_cb)
                self.inc("eSC", s.activation(self.sb_na[:], p0_as[:], AF.Exp))
            self.S(f)
            sc_na = self.counts["eSC"]

            def f(v):
                self.w(v, "eSC", sc_na)
                self.inc("eVE", v.tensor_scalar(self.sb_na[:], self.sb_na[:], -1.0, None, ALU.mult))
            self.V(f)
            ve_na = self.counts["eVE"]


            rs = {"slot": 0, "pc": 0, "st": 0}

            def restripe_T(t_src, t_dst, O, K, tag):
                NK = K // P
                for ot in range(O // P):
                    gs = rs["slot"]; rs["slot"] += 1
                    slot = gs % 2
                    gst = rs["st"]; rs["st"] += 1
                    ssl = gst % 2

                    def f(sy, ot=ot, slot=slot, gs=gs):
                        for m_ in mk.get(("rs_ld", gs - 2), []):
                            self.wm(sy, m_)
                        self.dma(sy, "s_rs%d" % slot, p0_ld[:, slot * WMAX:slot * WMAX + K],
                                 t_src.ap()[P * ot:P * ot + P, :])
                    self.SY(f)
                    qwm = ("s_rs%d" % slot, self.counts["s_rs%d" % slot])

                    def f(v, slot=slot, qwm=qwm, gs=gs):
                        self.wm(v, qwm)
                        self.wm2(v, mk.get(("rs_bf", gs - 2), 0))
                        mk[("rs_cast", gs)] = self.inc("eVE", v.tensor_copy(
                            p0_bf[:, slot * WMAX:slot * WMAX + K],
                            p0_ld[:, slot * WMAX:slot * WMAX + K]))
                        mk[("rs_ld", gs)] = [("eVE", mk[("rs_cast", gs)])]
                    self.V(f)
                    ve_m = self.counts["eVE"]

                    for kt in range(NK):
                        pc = rs["pc"]; rs["pc"] += 1
                        tb = ps_c if pc % 2 == 0 else ps_d

                        def f(t, kt=kt, slot=slot, pc=pc, tb=tb, ve_m=ve_m, gs=gs, NK=NK):
                            self.w(t, "eVE", ve_m)
                            self.w(t, "eSC", mk.get(("rs_ev", pc - 2), 0))
                            mm = self.inc("ePE", t.matmul(
                                tb[:, 0:P],
                                p0_bf[:, slot * WMAX + P * kt: slot * WMAX + P * (kt + 1)],
                                self.idnb[:], start=True, stop=True))
                            if kt == NK - 1:
                                mk[("rs_bf", gs)] = ("ePE", mm)
                        self.T(f)
                        pem = self.counts["ePE"]

                        def f(s, kt=kt, pc=pc, tb=tb, pem=pem, ssl=ssl, gst=gst, NK=NK):
                            self.w(s, "ePE", pem)
                            if kt == 0:
                                self.wm(s, mk.get(("rst_out", gst - 2), 0))
                            s.copy(p0_st[:, ssl * 32 * P + kt * P: ssl * 32 * P + (kt + 1) * P],
                                   tb[:, 0:P])
                            mk[("rs_ev", pc)] = self.inc("eSC", s.nop())
                        self.S(f)
                    scm = self.counts["eSC"]

                    def f(g, ot=ot, ssl=ssl, gst=gst, scm=scm, NK=NK):
                        self.w(g, "eSC", scm)
                        mk[("rst_out", gst)] = self.dma(
                            g, "g_rso%d" % ssl, t_dst.ap()[ot],
                            p0_st[:, ssl * 32 * P: ssl * 32 * P + K])
                    self.G(f)

            def restripe_N(t_src, t_dst, K, O, tag, gamma=None, bias_beta=False):
                NK = K // P
                for kt in range(NK):
                    gs = rs["slot"]; rs["slot"] += 1
                    slot = gs % 2

                    def f(sy, kt=kt, slot=slot, gs=gs):
                        for m_ in mk.get(("rs_ld", gs - 2), []):
                            self.wm(sy, m_)
                        self.dma(sy, "s_rs%d" % slot, p0_ld[:, slot * WMAX:slot * WMAX + O],
                                 t_src.ap()[P * kt:P * kt + P, :])
                    self.SY(f)
                    qwm = ("s_rs%d" % slot, self.counts["s_rs%d" % slot])

                    def f(v, kt=kt, slot=slot, qwm=qwm, gs=gs):
                        self.wm(v, qwm)
                        self.wm2(v, mk.get(("rs_bf", gs - 2), 0))
                        if gamma is not None:
                            ii = v.tensor_scalar(p0_bf[:, slot * WMAX:slot * WMAX + O],
                                                 p0_ld[:, slot * WMAX:slot * WMAX + O],
                                                 gamma[:, kt:kt + 1], None, ALU.mult)
                        else:
                            ii = v.tensor_copy(p0_bf[:, slot * WMAX:slot * WMAX + O],
                                               p0_ld[:, slot * WMAX:slot * WMAX + O])
                        mk[("rs_cast", gs)] = self.inc("eVE", ii)
                        mk[("rs_ld", gs)] = [("eVE", mk[("rs_cast", gs)])]
                    self.V(f)

                    if bias_beta:
                        def f(t, kt=kt, slot=slot, qwm=qwm):
                            self.wm(t, qwm)
                            self.w(t, "eVE", mk.get((tag, "bacc", kt - 1), 0))
                            for ot in range(O // P):
                                t.matmul(ps_a[:, ot:ot + 1],
                                         p0_ld[:, slot * WMAX + P * ot: slot * WMAX + P * (ot + 1)],
                                         self.v_beta[:, kt:kt + 1], start=True, stop=True)
                            mk[(tag, "bmm", kt)] = self.inc("ePE", t.nop())
                            mk[("rs_ld", gs)] = mk.get(("rs_ld", gs), []) + [("ePE", mk[(tag, "bmm", kt)])]
                        self.T(f)
                        peb = self.counts["ePE"]

                        def f(v, kt=kt, peb=peb):
                            self.w(v, "ePE", peb)
                            mk[(tag, "bacc", kt)] = self.inc("eVE", v.tensor_tensor(
                                p0_bias[:, 0:O // P], p0_bias[:, 0:O // P],
                                ps_a[:, 0:O // P], ALU.add))
                        self.V(f)

                    def f(g, kt=kt, slot=slot, gs=gs):
                        self.w(g, "eVE", mk[("rs_cast", gs)])
                        dst = bass.AP(t_dst, kt * P,
                                      [[K, P], [P * K, O // P], [1, P]])
                        src = p0_bf[:, slot * WMAX:slot * WMAX + O].rearrange("p (ot o) -> p ot o", o=P)
                        m = self.dma(g, "g_rsn%d" % slot, dst, src)
                        mk[("rs_bf", gs)] = m
                    self.G(f)

            restripe_T(self.t_w1, self.t_w1p, H, D, "w1")
            restripe_T(self.t_w2, self.t_w2p, H, D, "w2")
            restripe_T(self.t_h1, self.t_h1p, H, H, "h1")
            restripe_T(self.t_h2, self.t_h2p, H, H, "h2")
            restripe_T(self.t_w3, self.t_w3p, D, H, "w3")
            restripe_N(self.t_inw, self.t_inwp, D, O3, "inw", gamma=self.v_gamma, bias_beta=True)
            restripe_N(self.t_outw, self.t_outwp, INNER, D, "outw")


            ve_bias = mk[("inw", "bacc", ND - 1)]

            def f(sy):
                self.w(sy, "ePE", mk[("p0v_tp", len(vecs) - 1)])
                self.dma(sy, "s_vec", p0_vec[0:NO3, :],
                         self.t_inb.ap().rearrange("(a b) -> a b", b=P))
            self.SY(f)
            qw_inb = ("s_vec", self.counts["s_vec"])

            def f(t):
                self.wm(t, qw_inb)
                self.w(t, "eVE", ve_bias)
                self.inc("ePE", t.matmul(ps_a[:, 256:256 + NO3], p0_vec[0:NO3, :],
                                            self.idn[0:NO3, 0:NO3], start=True, stop=True))
            self.T(f)
            pe_inb = self.counts["ePE"]

            def f(v):
                self.w(v, "ePE", pe_inb)
                self.inc("eVE", v.tensor_tensor(self.v_inb[:], p0_bias[:, 0:NO3],
                                                ps_a[:, 256:256 + NO3], ALU.add))
            self.V(f)
            ve_vinb = self.counts["eVE"]

            if self.stop_after == "p0":
                def f(g):
                    self.w(g, "eVE", ve_vinb)
                    self.dma(g, "g_dbg", self.t_vinb.ap(), self.v_inb[:])
                self.G(f)


    def kbuild_side_work(self):
        c = self.c
        STATE, TAPS, INNER = c.STATE, c.TAPS, c.INNER
        NI = INNER // P
        mk = self.mk
        works = []

        zmark = [0]

        def mkzero():
            def f(g):
                zmark[0] = self.inc("eGP", g.memset(self.sb_z3[:], 0.0))
            self.G(f)
        works.append(mkzero)

        for ct in range(NI):
            for ng in range(STATE // 4):
                def unit(ct=ct, ng=ng):
                    def f(s, ct=ct, ng=ng):
                        if ng == 0:
                            self.wm(s, mk.get(("kout", ct - 1), 0))
                        pv = (ct, ng - 1) if ng > 0 else (ct - 1, STATE // 4 - 1)
                        self.w(s, "eVE", mk.get(("kacc",) + pv, 0))
                        for j in range(4):
                            n_ = ng * 4 + j
                            s.activation(self.sb_ke[:, j * TAPS:(j + 1) * TAPS], self.iota[:],
                                         AF.Exp, scale=self.sb_na[:, ct * STATE + n_:ct * STATE + n_ + 1])
                        mk[("kexp", ct, ng)] = self.inc("eSC", s.nop())
                    self.S(f)

                    def f(v, ct=ct, ng=ng):
                        self.w(v, "eSC", mk[("kexp", ct, ng)])
                        for j in range(4):
                            n_ = ng * 4 + j
                            e_sl = self.sb_ke[:, j * TAPS:(j + 1) * TAPS]
                            cb_sl = self.sb_cb[:, ct * STATE + n_:ct * STATE + n_ + 1]
                            if n_ == 0:
                                v.tensor_scalar(self.sb_k2[:], e_sl, cb_sl, None, ALU.mult)
                            else:
                                src, dst = (self.sb_k2, self.sb_k) if n_ % 2 == 1 else (self.sb_k, self.sb_k2)
                                v.scalar_tensor_tensor(dst[:], e_sl, cb_sl, src[:], ALU.mult, ALU.add)
                        if ng == STATE // 4 - 1:
                            fin = self.sb_k if STATE % 2 == 0 else self.sb_k2
                            v.tensor_tensor(fin[:, 0:1], fin[:, 0:1], self.v_dp[:, ct:ct + 1], ALU.add)
                        mk[("kacc", ct, ng)] = self.inc("eVE", v.nop())
                    self.V(f)

                    if ng == STATE // 4 - 1:
                        def f(g, ct=ct):
                            fin = self.sb_k if STATE % 2 == 0 else self.sb_k2
                            self.w(g, "eVE", mk[("kacc", ct, STATE // 4 - 1)])
                            self.w(g, "eGP", zmark[0])
                            kst = "g_kpad%d" % (ct % 2)
                            self.dma(g, kst, self.t_kpad.ap()[P * ct:P * ct + P, 0:TAPS - 1],
                                     self.sb_z3[:, 0:TAPS - 1])
                            self.dma(g, kst, self.t_kpad.ap()[P * ct:P * ct + P, 2 * TAPS - 1:],
                                     self.sb_z3[:, 0:TAPS + 1])
                            mk[("kout", ct)] = self.dma(
                                g, kst, self.t_kpad.ap()[P * ct:P * ct + P, TAPS - 1:2 * TAPS - 1],
                                fin[:])
                        self.G(f)
                works.append(unit)

        kp_marks = [("g_kpad0", 0), ("g_kpad1", 0)]

        def wfull_one(j):
            def f(g, j=j):
                if j == 0:
                    kp_marks[0] = ("g_kpad0", self.counts.get("g_kpad0", 0))
                    kp_marks[1] = ("g_kpad1", self.counts.get("g_kpad1", 0))
                    self.wm(g, kp_marks[0])
                    self.wm(g, kp_marks[1])
                src = bass.AP(self.t_kpad, (TAPS - 1 - j), [[3 * TAPS, INNER], [1, 2 * TAPS]])
                dst = bass.AP(self.t_wfull, j * (2 * TAPS), [[P * 2 * TAPS, INNER], [1, 2 * TAPS]])
                self.dma(g, "g_wfull", dst, src)
                if j == P - 1:
                    self.qg_wfull = ("g_wfull", self.counts["g_wfull"])
            self.G(f)
        for j in range(P):
            works.append(lambda j=j: wfull_one(j))
        return works

    def ffn(self, t_in, t_outx, side_work=()):
        side_work = list(side_work)
        c, nc = self.c, self.nc
        D, L, H = c.D, c.L, c.H
        NH, ND = H // P, D // P
        TT = c.TT_FFN
        NT, NS = L // TT, TT // P
        HF = min(512, TT)
        NHF = TT // HF
        mk = {}
        tag = "ffnA" if t_outx is self.t_x1 else "ffnB"

        _es = contextlib.ExitStack()
        b_xs = _es.enter_context(nc.sbuf_tensor(tag + "_xs", [P, 2 * D], F32))
        b_xn = _es.enter_context(nc.sbuf_tensor(tag + "_xn", [P, 2 * D], BF16))
        b_xT = _es.enter_context(nc.sbuf_tensor(tag + "_xT", [P, ND * TT], BF16))
        b_h = _es.enter_context(nc.sbuf_tensor(tag + "_h", [P, NH * TT], BF16))
        b_g = _es.enter_context(nc.sbuf_tensor(tag + "_g", [P, NH * TT], BF16))
        b_w = _es.enter_context(nc.sbuf_tensor(tag + "_w", [P, 3 * H], BF16))
        b_t1 = _es.enter_context(nc.sbuf_tensor(tag + "_t1", [P, HF], F32))
        b_o = _es.enter_context(nc.sbuf_tensor(tag + "_o", [P, 2 * HF], F32))
        b_op = _es.enter_context(nc.sbuf_tensor(tag + "_op", [P, 2 * HF], F32))
        b_st = _es.enter_context(nc.sbuf_tensor(tag + "_st", [P, 64], F32))
        b_sq = _es.enter_context(nc.sbuf_tensor(tag + "_sq", [P, D], F32))
        mm0 = _es.enter_context(nc.psum_tensor(tag + "_p0", [P, 512], F32))
        mm1 = _es.enter_context(nc.psum_tensor(tag + "_p1", [P, 512], F32))
        mm2 = _es.enter_context(nc.psum_tensor(tag + "_p2", [P, 512], F32))
        mm3 = _es.enter_context(nc.psum_tensor(tag + "_p3", [P, 512], F32))
        tpa = _es.enter_context(nc.psum_tensor(tag + "_tpa", [P, 512], F32))
        tpb = _es.enter_context(nc.psum_tensor(tag + "_tpb", [P, 512], F32))
        tfa = _es.enter_context(nc.psum_tensor(tag + "_tfa", [P, 512], F32))
        tfb = _es.enter_context(nc.psum_tensor(tag + "_tfb", [P, 512], F32))
        with _es:
            tp2 = [tpa, tpb]
            tf2 = [tfa, tfb]
            mmps = [mm0, mm1, mm2, mm3]

            def f(g):
                for ch in range(8):
                    r0 = (L // 8) * ch
                    mk["precopy"] = self.dma(g, "g_pre", t_outx.ap()[r0:r0 + L // 8, :],
                                             t_in.ap()[r0:r0 + L // 8, :])
            self.G(f)

            for ti in range(NT):
                for s_ in range(NS):
                    sl = s_ % 2

                    def f(sy, ti=ti, s_=s_, sl=sl):
                        self.w(sy, "eVE", mk.get(("lnv", ti, s_ - 2),
                                                 mk.get(("lnv", ti - 1, NS - 2 + s_), 0)))
                        r0 = ti * TT + s_ * P
                        self.dma(sy, "s_x%d" % sl, b_xs[:, sl * D:sl * D + D],
                                 t_in.ap()[r0:r0 + P, :])
                    self.SY(f)
                    qwm = ("s_x%d" % sl, self.counts["s_x%d" % sl])

                    def f(s, sl=sl, qwm=qwm, ti=ti, s_=s_):
                        self.wm(s, qwm)
                        self.w(s, "eVE", mk.get(("lnr", ti, s_ - 1),
                                                 mk.get(("lnr", ti - 1, NS - 1 + s_), 0)))
                        s.activation(b_sq[:], b_xs[:, sl * D:sl * D + D], AF.Square)
                        mk[("lnsq", ti, s_)] = self.inc("eSC", s.nop())
                    self.S(f)

                    def f(v, sl=sl, qwm=qwm, ti=ti, s_=s_):
                        self.wm(v, qwm)
                        self.w(v, "eSC", mk[("lnsq", ti, s_)])
                        v.reduce_sum(b_st[:, s_ * 8 + 0:s_ * 8 + 1], b_xs[:, sl * D:sl * D + D], axis=AX)
                        mk[("lnr", ti, s_)] = self.inc("eVE", v.reduce_sum(b_st[:, s_ * 8 + 1:s_ * 8 + 2], b_sq[:], axis=AX))
                        v.tensor_scalar(b_st[:, s_ * 8 + 0:s_ * 8 + 1], b_st[:, s_ * 8 + 0:s_ * 8 + 1], 1.0 / D, None, ALU.mult)
                        v.tensor_scalar(b_st[:, s_ * 8 + 1:s_ * 8 + 2], b_st[:, s_ * 8 + 1:s_ * 8 + 2], 1.0 / D, None, ALU.mult)
                        v.tensor_scalar(b_st[:, s_ * 8 + 2:s_ * 8 + 3], b_st[:, s_ * 8 + 0:s_ * 8 + 1], b_st[:, s_ * 8 + 0:s_ * 8 + 1], None, ALU.mult)
                        self.inc("eVE", v.tensor_sub(b_st[:, s_ * 8 + 3:s_ * 8 + 4], b_st[:, s_ * 8 + 1:s_ * 8 + 2], b_st[:, s_ * 8 + 2:s_ * 8 + 3]))
                    self.V(f)
                    vevar = self.counts["eVE"]

                    def f(s, vevar=vevar):
                        self.w(s, "eVE", vevar)
                        self.inc("eSC", s.activation(b_st[:, s_ * 8 + 4:s_ * 8 + 5], b_st[:, s_ * 8 + 3:s_ * 8 + 4], AF.Sqrt,
                                                     bias=self.v_eps[:, 0:1]))
                    self.S(f)
                    scsd = self.counts["eSC"]

                    def f(v, sl=sl, scsd=scsd, ti=ti, s_=s_):
                        self.w(v, "eSC", scsd)
                        self.w(v, "ePE", mk.get(("xntp", ti, s_ - 2),
                                                mk.get(("xntp", ti - 1, NS - 2 + s_), 0)))
                        v.reciprocal(b_st[:, s_ * 8 + 5:s_ * 8 + 6], b_st[:, s_ * 8 + 4:s_ * 8 + 5])
                        mk[("lnv", ti, s_)] = self.inc("eVE", v.tensor_scalar(b_xn[:, sl * D:sl * D + D], b_xs[:, sl * D:sl * D + D],
                                        b_st[:, s_ * 8 + 0:s_ * 8 + 1], b_st[:, s_ * 8 + 5:s_ * 8 + 6], ALU.subtract, ALU.mult))
                    self.V(f)

                    for kt in range(ND):
                        pp = (s_ * ND + kt) % 2
                        tb = tp2[pp]

                        def f(t, ti=ti, s_=s_, sl=sl, kt=kt, tb=tb):
                            self.w(t, "eVE", mk[("lnv", ti, s_)])
                            self.w(t, "eSC", mk.get(("xTev", ti, s_ * ND + kt - 2), 0))
                            mm = self.inc("ePE", t.matmul(
                                tb[:, 0:P],
                                b_xn[:, sl * D + kt * P: sl * D + (kt + 1) * P], self.idnb[:], start=True, stop=True))
                            if kt == ND - 1:
                                mk[("xntp", ti, s_)] = mm
                        self.T(f)
                        pem = self.counts["ePE"]

                        def f(s, ti=ti, s_=s_, kt=kt, tb=tb, pem=pem):
                            self.w(s, "ePE", pem)
                            s.copy(b_xT[:, kt * TT + s_ * P: kt * TT + (s_ + 1) * P],
                                   tb[:, 0:P])
                            mk[("xTev", ti, s_ * ND + kt)] = self.inc("eSC", s.nop())
                        self.S(f)
                xT_ready = self.counts["eSC"]

                for ot in range(NH):
                    wsl = ot % 2
                    assert H >= 2 * D

                    def f(sy, ti=ti, ot=ot, wsl=wsl):
                        self.w(sy, "ePE", mk.get(("c1mm", ti, ot - 2), mk.get(("done", ti - 1), 0)))
                        self.dma(sy, "s_w%d" % wsl, b_w[:, wsl * H: wsl * H + D],
                                 self.t_w1p.ap()[ot])
                        self.dma(sy, "s_w%d" % wsl, b_w[:, wsl * H + D: wsl * H + 2 * D],
                                 self.t_w2p.ap()[ot])
                    self.SY(f)
                    qwm = ("s_w%d" % wsl, self.counts["s_w%d" % wsl])

                    for hf in range(NHF):
                        sidx = ot * NHF + hf
                        pa, pb = mmps[(2 * sidx) % 4], mmps[(2 * sidx + 1) % 4]

                        def f(t, ti=ti, ot=ot, hf=hf, wsl=wsl, qwm=qwm, pa=pa, pb=pb, sidx=sidx):
                            self.wm(t, qwm)
                            if ot == 0 and hf == 0:
                                self.w(t, "eSC", xT_ready)
                            self.w(t, "eVE", mk.get(("c1h", ti, sidx - 2),
                                                     mk.get(("c1h", ti - 1, NH * NHF - 2 + sidx), 0)))
                            for kt in range(ND):
                                rhs = b_xT[:, kt * TT + hf * HF: kt * TT + hf * HF + HF]
                                t.matmul(pa[:, 0:HF], b_w[:, wsl * H + kt * P: wsl * H + (kt + 1) * P],
                                         rhs, start=(kt == 0), stop=(kt == ND - 1))
                            for kt in range(ND):
                                rhs = b_xT[:, kt * TT + hf * HF: kt * TT + hf * HF + HF]
                                t.matmul(pb[:, 0:HF], b_w[:, wsl * H + D + kt * P: wsl * H + D + (kt + 1) * P],
                                         rhs, start=(kt == 0), stop=(kt == ND - 1))
                            mm = self.inc("ePE", t.nop())
                            if hf == NHF - 1:
                                mk[("c1mm", ti, ot)] = mm
                        self.T(f)
                        pem = self.counts["ePE"]

                        def f(s, ot=ot, hf=hf, pem=pem, pa=pa, ti=ti, sidx=sidx):
                            self.w(s, "ePE", pem)
                            self.w(s, "eVE", mk.get(("c1h", ti, sidx - 1),
                                                    mk.get(("c1h", ti - 1, NH * NHF - 1 + sidx), 0)))
                            s.activation(b_t1[:, 0:HF], pa[:, 0:HF],
                                         AF.Silu, scale=self.v_w1s[:, ot:ot + 1])
                            mk[("c1silu", ti, sidx)] = self.inc("eSC", s.nop())
                        self.S(f)
                        scm = self.counts["eSC"]

                        def f(v, ot=ot, hf=hf, scm=scm, pb=pb, ti=ti, sidx=sidx):
                            self.w(v, "eSC", scm)
                            mk[("c1h", ti, sidx)] = self.inc("eVE", v.scalar_tensor_tensor(
                                b_h[:, ot * TT + hf * HF: ot * TT + hf * HF + HF],
                                pb[:, 0:HF], self.v_w2s[:, ot:ot + 1],
                                b_t1[:, 0:HF], ALU.mult, ALU.mult))
                        self.V(f)
                h_ready = self.counts["eVE"]

                prev_done = ("c1mm", NH - 1)
                for stg, wp, vs, src, dst in [("g1", self.t_h1p, self.v_h1s, b_h, b_g),
                                              ("g2", self.t_h2p, self.v_h2s, b_g, b_h)]:
                    for ot in range(NH):
                        wsl = ot % 3

                        def f(sy, ti=ti, ot=ot, wsl=wsl, wp=wp, stg=stg, prev_done=prev_done):
                            if ot < 3:
                                self.w(sy, "ePE", mk.get((prev_done[0], ti, prev_done[1]), 0))
                            self.w(sy, "ePE", mk.get((stg + "mm", ti, ot - 3), 0))
                            self.dma(sy, "s_w%d" % wsl, b_w[:, wsl * H: wsl * H + H],
                                     wp.ap()[ot])
                        self.SY(f)
                        qwm = ("s_w%d" % wsl, self.counts["s_w%d" % wsl])

                        for hf in range(NHF):
                            sidx = ot * NHF + hf
                            pa = mmps[sidx % 4]

                            def f(t, ti=ti, ot=ot, hf=hf, wsl=wsl, qwm=qwm, pa=pa, stg=stg,
                                  src=src, sidx=sidx):
                                self.wm(t, qwm)
                                if ot == 0 and hf == 0 and stg == "g1":
                                    self.w(t, "eVE", h_ready)
                                if ot == 0 and hf == 0 and stg == "g2":
                                    self.w(t, "eSC", mk.get(("g1ev", ti, NH * NHF - 1), 0))
                                self.w(t, "eSC", mk.get((stg + "ev", ti, sidx - 4), 0))
                                for kt in range(NH):
                                    rhs = src[:, kt * TT + hf * HF: kt * TT + hf * HF + HF]
                                    t.matmul(pa[:, 0:HF], b_w[:, wsl * H + kt * P: wsl * H + (kt + 1) * P],
                                             rhs, start=(kt == 0), stop=(kt == NH - 1))
                                mm = self.inc("ePE", t.nop())
                                if hf == NHF - 1:
                                    mk[(stg + "mm", ti, ot)] = mm
                            self.T(f)
                            pem = self.counts["ePE"]

                            def f(s, ot=ot, hf=hf, pem=pem, pa=pa, vs=vs, dst=dst, stg=stg,
                                  ti=ti, sidx=sidx):
                                self.w(s, "ePE", pem)
                                s.activation(dst[:, ot * TT + hf * HF: ot * TT + hf * HF + HF],
                                             pa[:, 0:HF], AF.Gelu, scale=vs[:, ot:ot + 1])
                                mk[(stg + "ev", ti, sidx)] = self.inc("eSC", s.nop())
                            self.S(f)
                        if side_work:
                            side_work.pop(0)()
                    prev_done = (stg + "mm", NH - 1)

                for dt_ in range(ND):
                    wsl = dt_ % 3

                    def f(sy, ti=ti, dt_=dt_, wsl=wsl):
                        if dt_ < 3:
                            self.w(sy, "ePE", mk.get(("g2mm", ti, NH - 1), 0))
                        self.w(sy, "ePE", mk.get(("w3mm", ti, dt_ - 3), 0))
                        self.dma(sy, "s_w%d" % wsl, b_w[:, wsl * H: wsl * H + H],
                                 self.t_w3p.ap()[dt_])
                    self.SY(f)
                    qwm = ("s_w%d" % wsl, self.counts["s_w%d" % wsl])

                    for hf in range(NHF):
                        sidx = dt_ * NHF + hf
                        pa = mmps[sidx % 4]

                        def f(t, ti=ti, dt_=dt_, hf=hf, wsl=wsl, qwm=qwm, pa=pa, sidx=sidx):
                            self.wm(t, qwm)
                            if dt_ == 0 and hf == 0:
                                self.w(t, "eSC", mk.get(("g2ev", ti, NH * NHF - 1), 0))
                            self.w(t, "eSC", mk.get(("w3ev", ti, sidx - 4), 0))
                            for kt in range(NH):
                                rhs = b_h[:, kt * TT + hf * HF: kt * TT + hf * HF + HF]
                                t.matmul(pa[:, 0:HF], b_w[:, wsl * H + kt * P: wsl * H + (kt + 1) * P],
                                         rhs, start=(kt == 0), stop=(kt == NH - 1))
                            mm = self.inc("ePE", t.nop())
                            if hf == NHF - 1:
                                mk[("w3mm", ti, dt_)] = mm
                        self.T(f)
                        pem = self.counts["ePE"]

                        def f(s, dt_=dt_, hf=hf, pem=pem, pa=pa, ti=ti, sidx=sidx):
                            self.w(s, "ePE", pem)
                            self.w(s, "ePE", mk.get(("otp", ti, sidx - 2), 0))
                            s.activation(b_o[:, (sidx % 2) * HF:(sidx % 2) * HF + HF], pa[:, 0:HF],
                                         AF.Identity, scale=self.v_w3s[:, dt_:dt_ + 1])
                            mk[("w3ev", ti, sidx)] = self.inc("eSC", s.nop())
                        self.S(f)
                        scm = self.counts["eSC"]

                        for pj in range(HF // P):
                            pidx = sidx * (HF // P) + pj
                            tfb_ = tf2[pidx % 2]

                            def f(t, dt_=dt_, pj=pj, tfb_=tfb_, scm=scm, ti=ti, sidx=sidx, pidx=pidx):
                                self.w(t, "eSC", scm)
                                self.w(t, "eSC", mk.get(("opev", ti, pidx - 2), 0))
                                mm = self.inc("ePE", t.matmul(
                                    tfb_[:, 0:P],
                                    b_o[:, (sidx % 2) * HF + pj * P:(sidx % 2) * HF + (pj + 1) * P],
                                    self.idn[:], start=True, stop=True))
                                if pj == HF // P - 1:
                                    mk[("otp", ti, sidx)] = mm
                            self.T(f)
                            pet = self.counts["ePE"]

                            def f(s, pj=pj, tfb_=tfb_, pet=pet, ti=ti, sidx=sidx, pidx=pidx):
                                self.w(s, "ePE", pet)
                                self.wm(s, mk.get(("oacc", ti, sidx - 2),
                                                  mk.get(("oacc", ti - 1, ND * NHF - 2 + sidx), 0)))
                                s.copy(b_op[:, (sidx % 2) * HF + pj * P:(sidx % 2) * HF + (pj + 1) * P],
                                       tfb_[:, 0:P])
                                mk[("opev", ti, pidx)] = self.inc("eSC", s.nop())
                            self.S(f)
                        scm2 = self.counts["eSC"]

                        def f(g, ti=ti, dt_=dt_, hf=hf, scm2=scm2, sidx=sidx):
                            self.w(g, "eSC", scm2)
                            if ti == 0 and dt_ == 0 and hf == 0:
                                self.wm(g, mk["precopy"])
                            r0 = ti * TT + hf * HF
                            src = b_op[:, (sidx % 2) * HF:(sidx % 2) * HF + HF] \
                                .rearrange("p (pj q) -> p pj q", q=P)
                            dst3 = bass.AP(t_outx, r0 * D + dt_ * P,
                                           [[D, P], [P * D, HF // P], [1, P]])
                            mk[("oacc", ti, sidx)] = self.dma(
                                g, "g_oac%d" % (sidx % 2), dst3, src, accum_op=ALU.add)
                        self.G(f)
                mk[("done", ti)] = self.counts["ePE"]
            while side_work:
                side_work.pop(0)()


    def mixer_a1(self):
        c, nc = self.c, self.nc
        D, L, O3, INNER = c.D, c.L, c.O3, c.INNER
        ND, NO3, NI = D // P, O3 // P, INNER // P
        TT = 512
        NT, NS = L // TT, TT // P
        mk = {}

        _es = contextlib.ExitStack()
        b_xs = _es.enter_context(nc.sbuf_tensor("a1_xs", [P, 2 * D], F32))
        b_x1n = _es.enter_context(nc.sbuf_tensor("a1_x1n", [P, 2 * D], F32))
        b_r2 = _es.enter_context(nc.sbuf_tensor("a1_r2", [P, 2 * D], F32))
        b_xn = _es.enter_context(nc.sbuf_tensor("a1_xn", [P, 2 * D], BF16))
        b_xT = _es.enter_context(nc.sbuf_tensor("a1_xT", [P, ND * TT], BF16))
        b_w = _es.enter_context(nc.sbuf_tensor("a1_w", [P, NO3 * D], BF16))
        b_ev = _es.enter_context(nc.sbuf_tensor("a1_ev", [P, 2 * TT], F32))
        b_evb = _es.enter_context(nc.sbuf_tensor("a1_evb", [P, 2 * TT], BF16))
        b_st = _es.enter_context(nc.sbuf_tensor("a1_st", [P, 64], F32))
        b_sq = _es.enter_context(nc.sbuf_tensor("a1_sq", [P, D], F32))
        mm0 = _es.enter_context(nc.psum_tensor("a1_p0", [P, 512], F32))
        mm1 = _es.enter_context(nc.psum_tensor("a1_p1", [P, 512], F32))
        mm2 = _es.enter_context(nc.psum_tensor("a1_p2", [P, 512], F32))
        mm3 = _es.enter_context(nc.psum_tensor("a1_p3", [P, 512], F32))
        tpa = _es.enter_context(nc.psum_tensor("a1_tpa", [P, 512], F32))
        tpb = _es.enter_context(nc.psum_tensor("a1_tpb", [P, 512], F32))
        with _es:
            tp2 = [tpa, tpb]
            mmps = [mm0, mm1, mm2, mm3]

            inw_marks = []

            def f(sy):
                CH = NO3 // 4
                for g_ in range(4):
                    dst = b_w[:, g_ * CH * D:(g_ + 1) * CH * D].rearrange("p (a k) -> p a k", k=D)
                    src = self.t_inwp.ap()[g_ * CH:(g_ + 1) * CH].transpose([1, 0, 2])
                    inw_marks.append(self.dma(sy, "s_iw%d" % g_, dst, src))
            self.SY(f)

            for ti in range(NT):
                for s_ in range(NS):
                    sl = s_ % 2

                    def f(sy, ti=ti, s_=s_, sl=sl):
                        self.w(sy, "eVE", mk.get(("ln2v", ti, s_ - 2),
                                                 mk.get(("ln2v", ti - 1, NS - 2 + s_), 0)))
                        r0 = ti * TT + s_ * P
                        self.dma(sy, "s_x%d" % sl, b_xs[:, sl * D:sl * D + D],
                                 self.t_x1.ap()[r0:r0 + P, :])
                    self.SY(f)
                    qwm = ("s_x%d" % sl, self.counts["s_x%d" % sl])

                    def f(s, sl=sl, qwm=qwm, ti=ti, s_=s_):
                        self.wm(s, qwm)
                        self.w(s, "eVE", mk.get(("ln2r", ti, s_ - 1),
                                                 mk.get(("ln2r", ti - 1, NS - 1 + s_), 0)))
                        s.activation(b_sq[:], b_xs[:, sl * D:sl * D + D], AF.Square)
                        mk[("sq1", ti, s_)] = self.inc("eSC", s.nop())
                    self.S(f)

                    def f(v, sl=sl, qwm=qwm, ti=ti, s_=s_):
                        self.wm(v, qwm)
                        self.w(v, "eSC", mk[("sq1", ti, s_)])
                        v.reduce_sum(b_st[:, s_ * 8 + 0:s_ * 8 + 1], b_xs[:, sl * D:sl * D + D], axis=AX)
                        mk[("ln1r", ti, s_)] = self.inc("eVE", v.reduce_sum(b_st[:, s_ * 8 + 1:s_ * 8 + 2], b_sq[:], axis=AX))
                        v.tensor_scalar(b_st[:, s_ * 8 + 0:s_ * 8 + 1], b_st[:, s_ * 8 + 0:s_ * 8 + 1], 1.0 / D, None, ALU.mult)
                        v.tensor_scalar(b_st[:, s_ * 8 + 1:s_ * 8 + 2], b_st[:, s_ * 8 + 1:s_ * 8 + 2], 1.0 / D, None, ALU.mult)
                        v.tensor_scalar(b_st[:, s_ * 8 + 2:s_ * 8 + 3], b_st[:, s_ * 8 + 0:s_ * 8 + 1], b_st[:, s_ * 8 + 0:s_ * 8 + 1], None, ALU.mult)
                        self.inc("eVE", v.tensor_sub(b_st[:, s_ * 8 + 3:s_ * 8 + 4], b_st[:, s_ * 8 + 1:s_ * 8 + 2], b_st[:, s_ * 8 + 2:s_ * 8 + 3]))
                    self.V(f)
                    vevar = self.counts["eVE"]

                    def f(s, vevar=vevar):
                        self.w(s, "eVE", vevar)
                        self.inc("eSC", s.activation(b_st[:, s_ * 8 + 4:s_ * 8 + 5], b_st[:, s_ * 8 + 3:s_ * 8 + 4], AF.Sqrt,
                                                     bias=self.v_eps[:, 0:1]))
                    self.S(f)
                    scsd = self.counts["eSC"]

                    def f(v, sl=sl, scsd=scsd, ti=ti, s_=s_):
                        self.w(v, "eSC", scsd)
                        self.wm(v, mk.get(("r2o", ti, s_ - 2),
                                          mk.get(("r2o", ti - 1, NS - 2 + s_), 0)))
                        self.w(v, "eVE", mk.get(("ln2v", ti, s_ - 2),
                                                mk.get(("ln2v", ti - 1, NS - 2 + s_), 0)))
                        v.reciprocal(b_st[:, s_ * 8 + 5:s_ * 8 + 6], b_st[:, s_ * 8 + 4:s_ * 8 + 5])
                        v.tensor_scalar(b_x1n[:, sl * D:sl * D + D], b_xs[:, sl * D:sl * D + D],
                                        b_st[:, s_ * 8 + 0:s_ * 8 + 1], b_st[:, s_ * 8 + 5:s_ * 8 + 6], ALU.subtract, ALU.mult)
                        mk[("ln1v", ti, s_)] = self.inc("eVE", v.tensor_add(b_r2[:, sl * D:sl * D + D], b_xs[:, sl * D:sl * D + D],
                                     b_x1n[:, sl * D:sl * D + D]))
                    self.V(f)
                    ve1 = self.counts["eVE"]

                    def f(g, ti=ti, s_=s_, sl=sl, ve1=ve1):
                        self.w(g, "eVE", ve1)
                        r0 = ti * TT + s_ * P
                        mk[("r2o", ti, s_)] = self.dma(
                            g, "g_r2%d" % (s_ % 2), self.t_r2.ap()[r0:r0 + P, :],
                            b_r2[:, sl * D:sl * D + D])
                    self.G(f)

                    def f(s, sl=sl, ve1=ve1, ti=ti, s_=s_):
                        self.w(s, "eVE", ve1)
                        s.activation(b_sq[:], b_x1n[:, sl * D:sl * D + D], AF.Square)
                        mk[("sq2", ti, s_)] = self.inc("eSC", s.nop())
                    self.S(f)

                    def f(v, sl=sl, ti=ti, s_=s_):
                        self.w(v, "eSC", mk[("sq2", ti, s_)])
                        v.reduce_sum(b_st[:, s_ * 8 + 0:s_ * 8 + 1], b_x1n[:, sl * D:sl * D + D], axis=AX)
                        mk[("ln2r", ti, s_)] = self.inc("eVE", v.reduce_sum(b_st[:, s_ * 8 + 1:s_ * 8 + 2], b_sq[:], axis=AX))
                        v.tensor_scalar(b_st[:, s_ * 8 + 0:s_ * 8 + 1], b_st[:, s_ * 8 + 0:s_ * 8 + 1], 1.0 / D, None, ALU.mult)
                        v.tensor_scalar(b_st[:, s_ * 8 + 1:s_ * 8 + 2], b_st[:, s_ * 8 + 1:s_ * 8 + 2], 1.0 / D, None, ALU.mult)
                        v.tensor_scalar(b_st[:, s_ * 8 + 2:s_ * 8 + 3], b_st[:, s_ * 8 + 0:s_ * 8 + 1], b_st[:, s_ * 8 + 0:s_ * 8 + 1], None, ALU.mult)
                        self.inc("eVE", v.tensor_sub(b_st[:, s_ * 8 + 3:s_ * 8 + 4], b_st[:, s_ * 8 + 1:s_ * 8 + 2], b_st[:, s_ * 8 + 2:s_ * 8 + 3]))
                    self.V(f)
                    vevar2 = self.counts["eVE"]

                    def f(s, vevar2=vevar2):
                        self.w(s, "eVE", vevar2)
                        self.inc("eSC", s.activation(b_st[:, s_ * 8 + 4:s_ * 8 + 5], b_st[:, s_ * 8 + 3:s_ * 8 + 4], AF.Sqrt,
                                                     bias=self.v_eps[:, 0:1]))
                    self.S(f)
                    scsd2 = self.counts["eSC"]

                    def f(v, sl=sl, scsd2=scsd2, ti=ti, s_=s_):
                        self.w(v, "eSC", scsd2)
                        self.w(v, "ePE", mk.get(("xntp", ti, s_ - 2),
                                                mk.get(("xntp", ti - 1, NS - 2 + s_), 0)))
                        v.reciprocal(b_st[:, s_ * 8 + 5:s_ * 8 + 6], b_st[:, s_ * 8 + 4:s_ * 8 + 5])
                        mk[("ln2v", ti, s_)] = self.inc("eVE", v.tensor_scalar(b_xn[:, sl * D:sl * D + D], b_x1n[:, sl * D:sl * D + D],
                                        b_st[:, s_ * 8 + 0:s_ * 8 + 1], b_st[:, s_ * 8 + 5:s_ * 8 + 6], ALU.subtract, ALU.mult))
                    self.V(f)

                    for kt in range(ND):
                        pp = (s_ * ND + kt) % 2
                        tb = tp2[pp]

                        def f(t, ti=ti, s_=s_, sl=sl, kt=kt, tb=tb):
                            self.w(t, "eVE", mk[("ln2v", ti, s_)])
                            self.w(t, "eSC", mk.get(("xTev", ti, s_ * ND + kt - 2), 0))
                            mm = self.inc("ePE", t.matmul(
                                tb[:, 0:P],
                                b_xn[:, sl * D + kt * P: sl * D + (kt + 1) * P], self.idnb[:], start=True, stop=True))
                            if kt == ND - 1:
                                mk[("xntp", ti, s_)] = mm
                        self.T(f)
                        pem = self.counts["ePE"]

                        def f(s, ti=ti, s_=s_, kt=kt, tb=tb, pem=pem):
                            self.w(s, "ePE", pem)
                            s.copy(b_xT[:, kt * TT + s_ * P: kt * TT + (s_ + 1) * P],
                                   tb[:, 0:P])
                            mk[("xTev", ti, s_ * ND + kt)] = self.inc("eSC", s.nop())
                        self.S(f)
                xT_ready = self.counts["eSC"]

                for ot in range(NO3):
                    pa = mmps[ot % 4]

                    def f(t, ti=ti, ot=ot, pa=pa):
                        if ti == 0 and ot % (NO3 // 4) == 0:
                            self.wm(t, inw_marks[ot // (NO3 // 4)])
                        if ot == 0:
                            self.w(t, "eSC", xT_ready)
                        self.w(t, "eSC", mk.get(("ipev", ti, ot - 4), 0))
                        for kt in range(ND):
                            t.matmul(pa[:, 0:TT], b_w[:, ot * D + kt * P: ot * D + (kt + 1) * P],
                                     b_xT[:, kt * TT:(kt + 1) * TT], start=(kt == 0), stop=(kt == ND - 1))
                        mk[("ipmm", ti, ot)] = self.inc("ePE", t.nop())
                    self.T(f)
                    pem = self.counts["ePE"]

                    def f(s, ti=ti, ot=ot, pa=pa, pem=pem):
                        self.w(s, "ePE", pem)
                        self.wm(s, mk.get(("ipo", ti, ot - 2),
                                          mk.get(("ipo", ti - 1, NO3 - 2 + ot), 0)))
                        bias = self.v_inb[:, ot:ot + 1]
                        if ot < NI:
                            s.activation(b_ev[:, (ot % 2) * TT:(ot % 2) * TT + TT], pa[:, 0:TT],
                                         AF.Identity, bias=bias)
                        elif ot < 2 * NI:
                            s.activation(b_evb[:, (ot % 2) * TT:(ot % 2) * TT + TT], pa[:, 0:TT],
                                         AF.Silu, bias=bias)
                        else:
                            s.activation(b_evb[:, (ot % 2) * TT:(ot % 2) * TT + TT], pa[:, 0:TT],
                                         AF.Sigmoid, bias=bias)
                        mk[("ipev", ti, ot)] = self.inc("eSC", s.nop())
                    self.S(f)
                    scm = self.counts["eSC"]

                    def f(g, ti=ti, ot=ot, scm=scm):
                        self.w(g, "eSC", scm)
                        t0 = ti * TT
                        if ot < NI:
                            dst = self.t_u.ap()[P * ot:P * (ot + 1), t0:t0 + TT]
                            srcb = b_ev[:, (ot % 2) * TT:(ot % 2) * TT + TT]
                        elif ot < 2 * NI:
                            r = ot - NI
                            dst = self.t_sv.ap()[P * r:P * (r + 1), t0:t0 + TT]
                            srcb = b_evb[:, (ot % 2) * TT:(ot % 2) * TT + TT]
                        else:
                            r = ot - 2 * NI
                            dst = self.t_sg.ap()[P * r:P * (r + 1), t0:t0 + TT]
                            srcb = b_evb[:, (ot % 2) * TT:(ot % 2) * TT + TT]
                        mk[("ipo", ti, ot)] = self.dma(g, "g_ip%d" % (ot % 2), dst, srcb)
                    self.G(f)

    def mixer_a2(self):
        c, nc = self.c, self.nc
        L, INNER = c.L, c.INNER
        NI = INNER // P
        CT = 512
        NTT = L // CT
        NPJ = CT // P
        mk = {}

        _es = contextlib.ExitStack()
        b_u = _es.enter_context(nc.sbuf_tensor("a2_u", [P, 2 * (CT + 2)], F32))
        b_c = _es.enter_context(nc.sbuf_tensor("a2_c", [P, 2 * CT], F32))
        b_s = _es.enter_context(nc.sbuf_tensor("a2_s", [P, 2 * CT], F32))
        b_ev = _es.enter_context(nc.sbuf_tensor("a2_ev", [P, 4 * P], F32))
        b_uTB = _es.enter_context(nc.sbuf_tensor("a2_uTB", [P, 2 * L], BF16))
        tpa = _es.enter_context(nc.psum_tensor("a2_tpa", [P, 512], F32))
        tpb = _es.enter_context(nc.psum_tensor("a2_tpb", [P, 512], F32))
        with _es:
            tp2 = [tpa, tpb]
            for ct in range(NI):
                for tt in range(NTT):
                    sl = tt % 2
                    UW = CT + 2

                    def f(sy, ct=ct, tt=tt, sl=sl):
                        self.w(sy, "eVE", mk.get(("cv", ct, tt - 2), mk.get(("cv", ct - 1, NTT - 2 + tt), 0)))
                        t0 = tt * CT
                        lo = max(t0 - 1, 0)
                        hi = min(t0 + CT + 1, L)
                        off = 0 if tt > 0 else 1
                        self.dma(sy, "s_u%d" % sl,
                                 b_u[:, sl * UW + off: sl * UW + off + (hi - lo)],
                                 self.t_u.ap()[P * ct:P * (ct + 1), lo:hi])
                    self.SY(f)
                    qwm = ("s_u%d" % sl, self.counts["s_u%d" % sl])

                    if tt == 0 or tt == NTT - 1:
                        def f(g, ct=ct, tt=tt, sl=sl, qwm=qwm):
                            self.wm(g, qwm)
                            if tt == 0:
                                g.memset(b_u[:, sl * UW: sl * UW + 1], 0.0)
                            if tt == NTT - 1:
                                g.memset(b_u[:, sl * UW + CT + 1: sl * UW + CT + 2], 0.0)
                            mk[("hz", ct, tt)] = self.inc("eGP", g.nop())
                        self.G(f)

                    def f(v, ct=ct, tt=tt, sl=sl, qwm=qwm):
                        self.wm(v, qwm)
                        if ("hz", ct, tt) in mk:
                            self.w(v, "eGP", mk[("hz", ct, tt)])
                        self.w(v, "eSC", mk.get(("si", ct, tt - 2), mk.get(("si", ct - 1, NTT - 2 + tt), 0)))
                        k0 = self.v_convw[:, 3 * ct + 0:3 * ct + 1]
                        k1 = self.v_convw[:, 3 * ct + 1:3 * ct + 2]
                        k2 = self.v_convw[:, 3 * ct + 2:3 * ct + 3]
                        v.tensor_scalar(b_c[:, sl * CT: sl * CT + CT], b_u[:, sl * UW + 1: sl * UW + 1 + CT],
                                        k1, None, ALU.mult)
                        v.scalar_tensor_tensor(b_c[:, sl * CT: sl * CT + CT], b_u[:, sl * UW: sl * UW + CT],
                                               k0, b_c[:, sl * CT: sl * CT + CT], ALU.mult, ALU.add)
                        mk[("cv", ct, tt)] = self.inc("eVE", v.scalar_tensor_tensor(b_c[:, sl * CT: sl * CT + CT], b_u[:, sl * UW + 2: sl * UW + 2 + CT],
                                               k2, b_c[:, sl * CT: sl * CT + CT], ALU.mult, ALU.add))
                    self.V(f)
                    vem = self.counts["eVE"]

                    def f(s, ct=ct, tt=tt, sl=sl, vem=vem):
                        self.w(s, "eVE", vem)
                        self.w(s, "ePE", mk.get(("tp", ct, tt - 2), mk.get(("tp", ct - 1, NTT - 2 + tt), 0)))
                        s.activation(b_s[:, sl * CT: sl * CT + CT], b_c[:, sl * CT: sl * CT + CT],
                                     AF.Silu, bias=self.v_convb[:, ct:ct + 1])
                        mk[("si", ct, tt)] = self.inc("eSC", s.nop())
                    self.S(f)
                    scm = self.counts["eSC"]

                    for pj in range(NPJ):
                        gidx = (ct * NTT + tt) * NPJ + pj
                        psl = gidx % 4
                        tb = tp2[gidx % 2]

                        def f(t, ct=ct, tt=tt, sl=sl, pj=pj, tb=tb, scm=scm, gidx=gidx):
                            self.w(t, "eSC", scm)
                            self.w(t, "eSC", mk.get(("tev", gidx - 2), 0))
                            mm = self.inc("ePE", t.matmul(
                                tb[:, 0:P],
                                b_s[:, sl * CT + pj * P: sl * CT + (pj + 1) * P], self.idn[:], start=True, stop=True))
                            if pj == NPJ - 1:
                                mk[("tp", ct, tt)] = mm
                        self.T(f)
                        pem = self.counts["ePE"]

                        def f(s, ct=ct, tt=tt, pj=pj, tb=tb, pem=pem, gidx=gidx):
                            self.w(s, "ePE", pem)
                            if tt == 0 and pj == 0:
                                self.wm(s, mk.get(("uto", ct - 2), 0))
                            i_ = tt * NPJ + pj
                            s.copy(b_uTB[:, (ct % 2) * L + i_ * P:(ct % 2) * L + (i_ + 1) * P],
                                   tb[:, 0:P])
                            mk[("tev", gidx)] = self.inc("eSC", s.nop())
                        self.S(f)

                if True:
                    sct = self.counts["eSC"]

                    def f(g, ct=ct, sct=sct):
                        self.w(g, "eSC", sct)
                        mk[("uto", ct)] = self.dma(
                            g, "g_uT%d" % (ct % 2), self.t_uT.ap()[ct],
                            b_uTB[:, (ct % 2) * L:(ct % 2) * L + L])
                    self.G(f)

    def scan(self):
        c, nc = self.c, self.nc
        L, INNER, TAPS = c.L, c.INNER, c.TAPS
        NI, NCH = INNER // P, c.NCH
        mk = {}
        GB = 8
        NG = P // 8
        NSL = 4
        BW = (NCH + 1) * P

        _es = contextlib.ExitStack()
        b_B = _es.enter_context(nc.sbuf_tensor("sc_B", [P, 2 * BW], BF16))
        b_w = _es.enter_context(nc.sbuf_tensor("sc_w", [P, NSL * GB * 2 * TAPS], BF16))
        b_S = _es.enter_context(nc.sbuf_tensor("sc_S", [P, 2 * NCH * P], BF16))
        b_Y = _es.enter_context(nc.sbuf_tensor("sc_Y", [P, L], BF16))
        b_sv = _es.enter_context(nc.sbuf_tensor("sc_sv", [P, 2 * L], BF16))
        b_sg = _es.enter_context(nc.sbuf_tensor("sc_sg", [P, 2 * L], BF16))
        b_z = _es.enter_context(nc.sbuf_tensor("sc_z", [P, L], BF16))
        ps0 = _es.enter_context(nc.psum_tensor("sc_p0", [P, 512], F32))
        ps1 = _es.enter_context(nc.psum_tensor("sc_p1", [P, 512], F32))
        ps2 = _es.enter_context(nc.psum_tensor("sc_p2", [P, 512], F32))
        ps3 = _es.enter_context(nc.psum_tensor("sc_p3", [P, 512], F32))
        tpa = _es.enter_context(nc.psum_tensor("sc_tpa", [P, 512], F32))
        tpb = _es.enter_context(nc.psum_tensor("sc_tpb", [P, 512], F32))
        with _es:
            tp2 = [tpa, tpb]
            ymm = [ps0, ps1, ps2, ps3]

            def f(g):
                self.inc("eGP", g.memset(b_B[:, 0:P], 0.0))
                self.inc("eGP", g.memset(b_B[:, BW:BW + P], 0.0))
            self.G(f)
            gpz = self.counts["eGP"]

            def emit_wf_load(g_, cb):
                def f(sy, cb=cb, g_=g_):
                    self.w(sy, "ePE", mk.get(("wcons", cb, g_ - NSL),
                                             mk.get(("wcons", cb - 1, NG - NSL + g_), 0)))
                    if cb == 0 and g_ < NSL:
                        self.wm(sy, self.qg_wfull)
                    src = self.t_wfull.ap()[cb * P + g_ * GB: cb * P + (g_ + 1) * GB] \
                        .transpose([1, 0, 2])
                    dst = b_w[:, (g_ % NSL) * GB * 2 * TAPS:((g_ % NSL) + 1) * GB * 2 * TAPS] \
                        .rearrange("p (c f) -> p c f", c=GB)
                    self.dma(sy, "s_wf%d" % (g_ % NSL), dst, src)
                self.SY(f)
                mk[("wld", cb, g_)] = ("s_wf%d" % (g_ % NSL), self.counts["s_wf%d" % (g_ % NSL)])

            def emit_A(cb):
                sl = cb % 2
                for g_ in range(NSL):
                    emit_wf_load(g_, cb)

                def f(sy, cb=cb, sl=sl):
                    self.w(sy, "eGP", gpz)
                    self.w(sy, "ePE", mk.get(("bfree", cb - 2), 0))
                    self.w(sy, "eVE", mk.get(("gate", cb - 2), 0))
                    self.dma(sy, "s_B%d" % sl, b_B[:, sl * BW + P: sl * BW + BW],
                             self.t_uT.ap()[cb])
                    self.dma(sy, "s_B%d" % sl, b_sv[:, sl * L: sl * L + L],
                             self.t_sv.ap()[P * cb:P * (cb + 1), :])
                    self.dma(sy, "s_B%d" % sl, b_sg[:, sl * L: sl * L + L],
                             self.t_sg.ap()[P * cb:P * (cb + 1), :])
                self.SY(f)
                qwB = ("s_B%d" % sl, self.counts["s_B%d" % sl])
                mk[("svld", cb)] = qwB

                for gr in range(NG):
                    pa = ymm[gr % 4]

                    def f(t, cb=cb, gr=gr, sl=sl, pa=pa, qwB=qwB):
                        self.wm(t, qwB)
                        self.wm(t, mk[("wld", cb, gr)])
                        self.w(t, "eSC", mk.get(("yev", cb, gr - 4), mk.get(("yev", cb - 1, NG - 4 + gr), 0)))
                        for cr in range(8):
                            cl = gr * 8 + cr
                            wbase = (gr % NSL) * GB * 2 * TAPS + cr * 2 * TAPS
                            W0 = b_w[:, wbase: wbase + TAPS]
                            W1 = b_w[:, wbase + TAPS: wbase + 2 * TAPS]
                            X0 = b_B[:, sl * BW + P + cl: sl * BW + P + cl + (NCH - 1) * P + 1: P]
                            X1 = b_B[:, sl * BW + cl: sl * BW + cl + (NCH - 1) * P + 1: P]
                            t.matmul(pa[:, cr * NCH:(cr + 1) * NCH], W0, X0, start=True, stop=False)
                            t.matmul(pa[:, cr * NCH:(cr + 1) * NCH], W1, X1, start=False, stop=True)
                        mm = self.inc("ePE", t.nop())
                        mk[("ymm", cb, gr)] = mm
                        if gr == NG - 1:
                            mk[("bfree", cb)] = mm
                        mk[("wcons", cb, gr)] = mm
                    self.T(f)
                    pem = self.counts["ePE"]
                    if gr + NSL < NG:
                        emit_wf_load(gr + NSL, cb)

                    def f(s, cb=cb, gr=gr, sl=sl, pa=pa, pem=pem):
                        self.w(s, "ePE", pem)
                        if gr == 0:
                            self.w(s, "ePE", mk.get(("stp", cb - 2), 0))
                        src = pa[:, 0:8 * NCH].rearrange("p (c i) -> p c i", c=8)
                        dst = b_S[:, sl * NCH * P:(sl + 1) * NCH * P] \
                            .rearrange("p (i c) -> p c i", c=P)[:, gr * 8:(gr + 1) * 8, :]
                        s.copy(dst, src)
                        mk[("yev", cb, gr)] = self.inc("eSC", s.nop())
                    self.S(f)
                mk[("S_ready", cb)] = self.counts["eSC"]

            def emit_B(cb):
                sl = cb % 2
                for m_ in range(NCH):
                    tb = tp2[m_ % 2]

                    def f(t, cb=cb, m_=m_, sl=sl, tb=tb):
                        self.w(t, "eSC", mk[("S_ready", cb)])
                        self.w(t, "eSC", mk.get(("Yev", cb, m_ - 2),
                                                mk.get(("Yev", cb - 1, NCH - 2 + m_), 0)))
                        mm = self.inc("ePE", t.matmul(
                            tb[:, 0:P], b_S[:, sl * NCH * P + m_ * P: sl * NCH * P + (m_ + 1) * P],
                            self.idnb[:], start=True, stop=True))
                        if m_ == NCH - 1:
                            mk[("stp", cb)] = mm
                    self.T(f)
                    pem = self.counts["ePE"]

                    def f(s, cb=cb, m_=m_, tb=tb, pem=pem):
                        self.w(s, "ePE", pem)
                        if m_ == 0:
                            self.w(s, "eVE", mk.get(("gate", cb - 1), 0))
                        s.copy(b_Y[:, m_ * P:(m_ + 1) * P], tb[:, 0:P])
                        mk[("Yev", cb, m_)] = self.inc("eSC", s.nop())
                    self.S(f)
                Y_ready = self.counts["eSC"]

                def f(v, cb=cb, sl=sl, Y_ready=Y_ready):
                    self.w(v, "eSC", Y_ready)
                    self.wm(v, mk[("svld", cb)])
                    self.wm(v, mk.get(("zo", cb - 1), 0))
                    v.tensor_mul(b_Y[:], b_Y[:], b_sg[:, sl * L: sl * L + L])
                    mk[("gate", cb)] = self.inc("eVE", v.tensor_tensor(
                        b_z[:], b_Y[:], b_sv[:, sl * L: sl * L + L], ALU.add))
                self.V(f)
                veg = self.counts["eVE"]

                def f(g, cb=cb, veg=veg):
                    self.w(g, "eVE", veg)
                    mk[("zo", cb)] = self.dma(
                        g, "g_z", self.t_z.ap()[P * cb:P * (cb + 1), :], b_z[:])
                self.G(f)

            emit_A(0)
            for cb in range(NI):
                if cb + 1 < NI:
                    emit_A(cb + 1)
                emit_B(cb)

    def outproj(self):
        c, nc = self.c, self.nc
        D, L, INNER = c.D, c.L, c.INNER
        ND, NI = D // P, INNER // P
        TT = 512
        NT, NS = L // TT, TT // P
        mk = {}

        _es = contextlib.ExitStack()
        b_w = _es.enter_context(nc.sbuf_tensor("op_w", [P, ND * NI * P], BF16))
        b_z = _es.enter_context(nc.sbuf_tensor("op_z", [P, 2 * NI * TT], BF16))
        b_o = _es.enter_context(nc.sbuf_tensor("op_o", [P, 2 * TT], F32))
        b_op = _es.enter_context(nc.sbuf_tensor("op_op", [P, 2 * TT], F32))
        ps0 = _es.enter_context(nc.psum_tensor("op_p0", [P, 512], F32))
        ps1 = _es.enter_context(nc.psum_tensor("op_p1", [P, 512], F32))
        tfa = _es.enter_context(nc.psum_tensor("op_tfa", [P, 512], F32))
        tfb = _es.enter_context(nc.psum_tensor("op_tfb", [P, 512], F32))
        with _es:
            tf2 = [tfa, tfb]
            mmps = [ps0, ps1]

            def f(g):
                for ch in range(8):
                    r0 = (L // 8) * ch
                    mk["precopy"] = self.dma(g, "g_pre", self.t_x2.ap()[r0:r0 + L // 8, :],
                                             self.t_r2.ap()[r0:r0 + L // 8, :])
            self.G(f)

            def f(sy):
                for dt_ in range(ND):
                    self.dma(sy, "s_opw", b_w[:, dt_ * NI * P:(dt_ + 1) * NI * P],
                             self.t_outwp.ap()[dt_])
            self.SY(f)
            qww = ("s_opw", self.counts["s_opw"])

            for ti in range(NT):
                zsl = ti % 2

                def f(sy, ti=ti, zsl=zsl):
                    self.w(sy, "ePE", mk.get(("opmm", ti - 2, ND - 1), 0))
                    t0 = ti * TT
                    src = bass.AP(self.t_z, t0, [[L, P], [P * L, NI], [1, TT]])
                    dst = b_z[:, zsl * NI * TT:(zsl + 1) * NI * TT].rearrange("p (k t) -> p k t", t=TT)
                    self.dma(sy, "s_z%d" % zsl, dst, src)
                self.SY(f)
                qwz = ("s_z%d" % zsl, self.counts["s_z%d" % zsl])

                for dt_ in range(ND):
                    pa = mmps[dt_ % 2]

                    def f(t, ti=ti, dt_=dt_, zsl=zsl, pa=pa, qwz=qwz):
                        self.wm(t, qwz)
                        self.wm(t, qww)
                        self.w(t, "eSC", mk.get(("opev", ti, dt_ - 2), mk.get(("opev", ti - 1, ND - 2 + dt_), 0)))
                        for kt in range(NI):
                            t.matmul(pa[:, 0:TT], b_w[:, (dt_ * NI + kt) * P:(dt_ * NI + kt + 1) * P],
                                     b_z[:, zsl * NI * TT + kt * TT: zsl * NI * TT + (kt + 1) * TT],
                                     start=(kt == 0), stop=(kt == NI - 1))
                        mk[("opmm", ti, dt_)] = self.inc("ePE", t.nop())
                    self.T(f)
                    pem = self.counts["ePE"]

                    def f(s, ti=ti, dt_=dt_, pa=pa, pem=pem):
                        self.w(s, "ePE", pem)
                        self.w(s, "ePE", mk.get(("optp", ti, dt_ - 2), 0))
                        s.activation(b_o[:, (dt_ % 2) * TT:(dt_ % 2) * TT + TT], pa[:, 0:TT],
                                     AF.Identity, bias=self.v_outb[:, dt_:dt_ + 1])
                        mk[("opev", ti, dt_)] = self.inc("eSC", s.nop())
                    self.S(f)
                    scm = self.counts["eSC"]

                    for pj in range(NS):
                        pidx = (ti * ND + dt_) * NS + pj
                        tfb_ = tf2[pidx % 2]

                        def f(t, ti=ti, dt_=dt_, pj=pj, tfb_=tfb_, scm=scm, pidx=pidx):
                            self.w(t, "eSC", scm)
                            self.w(t, "eSC", mk.get(("oev", pidx - 2), 0))
                            mm = self.inc("ePE", t.matmul(
                                tfb_[:, 0:P],
                                b_o[:, (dt_ % 2) * TT + pj * P:(dt_ % 2) * TT + (pj + 1) * P],
                                self.idn[:], start=True, stop=True))
                            if pj == NS - 1:
                                mk[("optp", ti, dt_)] = mm
                        self.T(f)
                        pet = self.counts["ePE"]

                        def f(s, ti=ti, dt_=dt_, pj=pj, tfb_=tfb_, pet=pet, pidx=pidx):
                            self.w(s, "ePE", pet)
                            self.wm(s, mk.get(("oacc", ti, dt_ - 2),
                                              mk.get(("oacc", ti - 1, ND - 2 + dt_), 0)))
                            s.copy(b_op[:, (dt_ % 2) * TT + pj * P:(dt_ % 2) * TT + (pj + 1) * P],
                                   tfb_[:, 0:P])
                            mk[("oev", pidx)] = self.inc("eSC", s.nop())
                        self.S(f)
                    scm2 = self.counts["eSC"]

                    def f(g, ti=ti, dt_=dt_, scm2=scm2):
                        self.w(g, "eSC", scm2)
                        if ti == 0 and dt_ == 0:
                            self.wm(g, mk["precopy"])
                        r0 = ti * TT
                        dst3 = bass.AP(self.t_x2, r0 * D + dt_ * P, [[D, P], [P * D, NS], [1, P]])
                        src = b_op[:, (dt_ % 2) * TT:(dt_ % 2) * TT + TT].rearrange("p (pj q) -> p pj q", q=P)
                        mk[("oacc", ti, dt_)] = self.dma(g, "g_oac%d" % (dt_ % 2), dst3, src,
                                                         accum_op=ALU.add)
                    self.G(f)


def build_program(cfg: Cfg, stop_after=None):
    m = MK(cfg, stop_after)
    nc = m.build()
    m._stack.close()
    return nc, m


_CFG = Cfg(D=1024, L=8192, H=4096, STATE=64, TT_FFN=1024)
_NCORES = 8


def kernel(**inputs):
    from concourse.bass_utils import run_bass_kernel_spmd
    nc, m = build_program(_CFG, None)
    INNER = _CFG.INNER
    x = inputs["L0_attn_x"]
    base = {
        "norm_gamma": inputs["L0_attn_norm_gamma"],
        "norm_beta": inputs["L0_attn_norm_beta"],
        "in_w": inputs["L0_attn_in_w"],
        "in_b": inputs["L0_attn_in_b"],
        "conv_w": np.reshape(inputs["L0_attn_conv_w"], (INNER, 3)),
        "conv_b": inputs["L0_attn_conv_b"],
        "A_log": inputs["L0_attn_A_log"],
        "B_p": inputs["L0_attn_B_p"],
        "C_p": inputs["L0_attn_C_p"],
        "D_p": inputs["L0_attn_D_p"],
        "out_w": inputs["L0_attn_out_w"],
        "out_b": inputs["L0_attn_out_b"],
        "w1_w": inputs["L0_ffn_w1_w"], "w1_s": inputs["L0_ffn_w1_s"],
        "w2_w": inputs["L0_ffn_w2_w"], "w2_s": inputs["L0_ffn_w2_s"],
        "h1_w": inputs["L0_ffn_h1_w"], "h1_s": inputs["L0_ffn_h1_s"],
        "h2_w": inputs["L0_ffn_h2_w"], "h2_s": inputs["L0_ffn_h2_s"],
        "w3_w": inputs["L0_ffn_w3_w"], "w3_s": inputs["L0_ffn_w3_s"],
    }
    in_maps = [dict(base, x=np.ascontiguousarray(x[b])) for b in range(_NCORES)]
    res = run_bass_kernel_spmd(nc, in_maps, core_ids=list(range(_NCORES)))
    return np.stack([np.asarray(res.results[b]["out"]) for b in range(_NCORES)])


# version 11
# speedup vs baseline: 1.0295x; 1.0295x over previous
import contextlib
import numpy as np
import concourse.bass as bass
import concourse.mybir as mybir

F32 = mybir.dt.float32
BF16 = mybir.dt.bfloat16
AF = mybir.ActivationFunctionType
ALU = mybir.AluOpType
AX = mybir.AxisListType.X
P = 128


class Cfg:
    def __init__(self, D=1024, L=8192, H=4096, STATE=64, TT_FFN=1024):
        self.D, self.L, self.H, self.STATE = D, L, H, STATE
        self.INNER, self.O3 = 2 * D, 6 * D
        self.TAPS = 128
        self.TT_FFN = TT_FFN
        assert L % TT_FFN == 0 and L % P == 0 and L % 512 == 0
        self.NCH = L // P



class RecInst:
    def __init__(self, method, args, kwargs, is_nop=False):
        self.method, self.args, self.kwargs = method, args, kwargs
        self.incs = []
        self.is_nop = is_nop


class RecWait:
    def __init__(self, sem_name, val):
        self.sem_name, self.val = sem_name, val


class Rec:
    def __init__(self, owner):
        self._ops = []
        self._owner = owner

    def wait_ge(self, sem_handle_or_name, val):
        self._ops.append(RecWait(sem_handle_or_name, val))

    def nop(self):
        e = RecInst("nop", (), {}, is_nop=True)
        e.rec = self
        self._ops.append(e)
        return e

    def __getattr__(self, m):
        if m.startswith("_"):
            raise AttributeError(m)

        def call(*a, **k):
            e = RecInst(m, a, k)
            e.rec = self
            self._ops.append(e)
            return e
        return call

    def last_real(self):
        for e in reversed(self._ops):
            if isinstance(e, RecInst):
                return None if e.is_nop else e
        return None

    def replay(self, eng, sems):
        for e in self._ops:
            if isinstance(e, RecWait):
                eng.wait_ge(sems[e.sem_name], e.val)
            else:
                if e.is_nop and not e.incs:
                    continue
                meth = "drain" if (e.is_nop and e.incs) else e.method
                inst = getattr(eng, meth)(*(() if meth == "drain" else e.args),
                                          **({} if meth == "drain" else e.kwargs))
                for (nm, kk) in e.incs:
                    inst.then_inc(sems[nm], kk)


class MK:
    SEMS = ["ePE", "eSC", "eVE", "eGP", "eSY"]

    def __init__(self, cfg: Cfg, stop_after=None):
        self.c = cfg
        self.stop_after = stop_after
        self.nc = bass.Bass("TRN2", target_bir_lowering=False)
        self.counts = {}
        self.sems = {}
        self.dma_streams = []
        self.mk = {}
        self._entry_rec = {}
        self.rT, self.rS, self.rV = Rec(self), Rec(self), Rec(self)
        self.rG, self.rSY = Rec(self), Rec(self)
        self._stack = contextlib.ExitStack()

    def stream(self, name):
        if name not in self.sems:
            self.sems[name] = self._stack.enter_context(self.nc.semaphore(name))
            self.counts[name] = 0
            self.dma_streams.append(name)
        return name

    def dma(self, rec, stname, out, in_, **kw):
        st = self.stream(stname)
        inst = rec.dma_start(out=out, in_=in_, **kw)
        c = self.inc(st, inst, 16)
        return (st, c)

    def wm(self, eng, mark):
        if mark:
            self.w(eng, mark[0], mark[1])

    def wm2(self, eng, mark):
        if mark:
            self.w(eng, mark[0], mark[1])

    def inc(self, name, entry, k=1):
        if entry.is_nop:
            rec = entry.rec
            idx = rec._ops.index(entry)
            prev = rec._ops[idx - 1] if idx > 0 else None
            if isinstance(prev, RecInst) and not prev.is_nop and not prev.incs:
                entry = prev
        if entry.incs and not entry.is_nop:
            nxt = RecInst("drain", (), {})
            nxt.rec = entry.rec
            pos = entry.rec._ops.index(entry) + 1
            entry.rec._ops.insert(pos, nxt)
            entry = nxt
        entry.incs.append((name, k))
        self.counts[name] += k
        return self.counts[name]

    def w(self, eng, name, val):
        if val and val > 0:
            eng.wait_ge(name, val)

    def _run(self, rec, f):
        f(rec)

    def T(self, f): self._run(self.rT, f)
    def S(self, f): self._run(self.rS, f)
    def V(self, f): self._run(self.rV, f)
    def G(self, f): self._run(self.rG, f)
    def SY(self, f): self._run(self.rSY, f)

    def build(self):
        c, nc = self.c, self.nc
        D, L, H, INNER, O3, STATE, TAPS = c.D, c.L, c.H, c.INNER, c.O3, c.STATE, c.TAPS
        NH, ND, NI, NO3 = H // P, D // P, INNER // P, O3 // P
        st = self._stack

        for s in self.SEMS:
            self.sems[s] = st.enter_context(nc.semaphore(s))
            self.counts[s] = 0

        def din(name, shape):
            return nc.dram_tensor(name, shape, F32, kind="ExternalInput")
        self.t_x = din("x", [L, D])
        self.t_gamma, self.t_beta = din("norm_gamma", [D]), din("norm_beta", [D])
        self.t_inw, self.t_inb = din("in_w", [D, O3]), din("in_b", [O3])
        self.t_convw, self.t_convb = din("conv_w", [INNER, 3]), din("conv_b", [INNER])
        self.t_alog, self.t_bp = din("A_log", [INNER, STATE]), din("B_p", [INNER, STATE])
        self.t_cp, self.t_dp = din("C_p", [INNER, STATE]), din("D_p", [INNER])
        self.t_outw, self.t_outb = din("out_w", [INNER, D]), din("out_b", [D])
        self.t_w1, self.t_w1s = din("w1_w", [H, D]), din("w1_s", [H])
        self.t_w2, self.t_w2s = din("w2_w", [H, D]), din("w2_s", [H])
        self.t_h1, self.t_h1s = din("h1_w", [H, H]), din("h1_s", [H])
        self.t_h2, self.t_h2s = din("h2_w", [H, H]), din("h2_s", [H])
        self.t_w3, self.t_w3s = din("w3_w", [D, H]), din("w3_s", [D])
        self.t_out = nc.dram_tensor("out", [L, D], F32, kind="ExternalOutput")

        sa = self.stop_after

        def scratch(name, shape, dtype, dbg=False):
            return nc.dram_tensor(name, shape, dtype,
                                  kind="ExternalOutput" if dbg else "Internal")
        self.t_w1p = scratch("w1p", [NH, P, D], BF16, sa == "p0")
        self.t_w2p = scratch("w2p", [NH, P, D], BF16)
        self.t_h1p = scratch("h1p", [NH, P, H], BF16)
        self.t_h2p = scratch("h2p", [NH, P, H], BF16)
        self.t_w3p = scratch("w3p", [ND, P, H], BF16, sa == "p0")
        self.t_inwp = scratch("inwp", [NO3, P, D], BF16, sa == "p0")
        self.t_outwp = scratch("outwp", [ND, P, INNER], BF16)
        self.t_kpad = scratch("kpad", [INNER, 3 * TAPS], BF16, sa == "p0")
        self.t_wfull = scratch("wfull", [INNER, P, 2 * TAPS], BF16, sa == "p0")
        self.t_vinb = scratch("vinb_dbg", [P, NO3], F32, sa == "p0")
        self.t_x1 = scratch("x1", [L, D], F32, sa == "p1")
        self.t_r2 = scratch("r2", [L, D], F32, sa in ("a1", "a2", "scan"))
        self.t_u = scratch("u_pre", [INNER, L], F32, sa == "a1")
        self.t_sv = scratch("sv", [INNER, L], BF16, sa == "a1")
        self.t_sg = scratch("sg", [INNER, L], BF16)
        self.t_uT = scratch("uT", [INNER // P, P, c.NCH * P], BF16, sa == "a2")
        self.t_z = scratch("z", [INNER, L], BF16, sa == "scan")
        self.t_x2 = scratch("x2", [L, D], F32, sa == "oproj")

        def sb(name, shape, dtype=F32):
            return st.enter_context(nc.sbuf_tensor(name, shape, dtype))
        self.idn = sb("idn", [P, P])
        self.idnb = sb("idnb", [P, P], BF16)
        self.iota = sb("iota", [P, TAPS])
        self.v_w1s, self.v_w2s = sb("v_w1s", [P, NH]), sb("v_w2s", [P, NH])
        self.v_h1s, self.v_h2s = sb("v_h1s", [P, NH]), sb("v_h2s", [P, NH])
        self.v_w3s = sb("v_w3s", [P, ND])
        self.v_gamma, self.v_beta = sb("v_gamma", [P, ND]), sb("v_beta", [P, ND])
        self.v_inb = sb("v_inb", [P, NO3])
        self.v_convw = sb("v_convw", [P, 3 * NI])
        self.v_convb = sb("v_convb", [P, NI])
        self.v_outb = sb("v_outb", [P, ND])
        self.v_dp = sb("v_dp", [P, NI])
        self.v_eps = sb("v_eps", [P, 1])
        self.sb_na = sb("sb_na", [P, NI * STATE])
        self.sb_cb = sb("sb_cb", [P, NI * STATE])
        self.sb_ke = sb("sb_ke", [P, 4 * self.c.TAPS], BF16)
        self.sb_k = sb("sb_k", [P, self.c.TAPS])
        self.sb_k2 = sb("sb_k2", [P, self.c.TAPS])
        self.sb_z3 = sb("sb_z3", [P, self.c.TAPS + 1])

        done = False
        self.phase0()
        self.barrier()
        if sa == "p0":
            done = True
        if not done:
            self.ffn(self.t_x, self.t_x1, side_work=self.kbuild_side_work())
            self.barrier()
            done = sa == "p1"
        if not done:
            self.mixer_a1()
            self.barrier()
            done = sa == "a1"
        if not done:
            self.mixer_a2()
            self.barrier()
            done = sa == "a2"
        if not done:
            self.scan()
            self.barrier()
            done = sa == "scan"
        if not done:
            self.outproj()
            self.barrier()
            done = sa == "oproj"
        if not done:
            self.ffn(self.t_x2, self.t_out)
        self.final_wait()
        self._auto_drain()
        with nc.Block() as blk:
            blk.tensor(lambda e: self.rT.replay(e, self.sems))
            blk.scalar(lambda e: self.rS.replay(e, self.sems))
            blk.vector(lambda e: self.rV.replay(e, self.sems))
            blk.gpsimd(lambda e: self.rG.replay(e, self.sems))
            blk.sync(lambda e: self.rSY.replay(e, self.sems))
        return nc

    def _auto_drain(self):
        for rec in (self.rV, self.rS, self.rG):
            newops = []
            pend = False
            for e in rec._ops:
                if isinstance(e, RecInst) and not e.is_nop and e.method != "drain":
                    if pend:
                        d = RecInst("drain", (), {})
                        d.rec = rec
                        newops.append(d)
                    newops.append(e)
                    pend = True
                else:
                    if isinstance(e, RecInst) and e.method == "drain":
                        pend = False
                    newops.append(e)
            rec._ops[:] = newops

    def barrier(self):
        cn = self.counts
        eng = [("ePE", self.T), ("eSC", self.S), ("eVE", self.V), ("eGP", self.G), ("eSY", self.SY)]
        sync_streams = [st for st in self.dma_streams if st.startswith("s_")]
        g_streams = [st for st in self.dma_streams
                     if st.startswith("g_") and st != "g_wfull"]
        for nm, sec in eng:
            def f(e, nm=nm):
                if nm == "eSY":
                    for st in sync_streams:
                        self.w(e, st, cn[st])
                if nm == "eGP":
                    for st in g_streams:
                        self.w(e, st, cn[st])
                self.inc(nm, e.nop())
            sec(f)
        targ = {nm: cn[nm] for nm, _ in eng}
        for nm, sec in eng:
            def f(e, nm=nm):
                for o, _ in eng:
                    if o != nm:
                        e.wait_ge(o, targ[o])
                self.inc(nm, e.nop())
            sec(f)
        targ2 = {nm: cn[nm] for nm, _ in eng}
        for nm, sec in eng:
            def f(e, nm=nm):
                for o, _ in eng:
                    if o != nm:
                        e.wait_ge(o, targ2[o])
            sec(f)

    def final_wait(self):
        self.barrier()
        tot = {st: self.counts[st] for st in self.dma_streams}

        def f(g):
            for st, v in tot.items():
                self.w(g, st, v)
        self.G(f)

    def phase0(self):
        c, nc = self.c, self.nc
        D, H, INNER, O3, STATE, TAPS = c.D, c.H, c.INNER, c.O3, c.STATE, c.TAPS
        NH, ND, NI, NO3 = H // P, D // P, INNER // P, O3 // P
        mk = self.mk
        WMAX = max(O3, H)

        _es = contextlib.ExitStack()
        ps_a = _es.enter_context(nc.psum_tensor("p0ps_a", [P, 512], F32))
        ps_c = _es.enter_context(nc.psum_tensor("p0ps_c", [P, 512], F32))
        ps_d = _es.enter_context(nc.psum_tensor("p0ps_d", [P, 512], F32))
        p0_vec = _es.enter_context(nc.sbuf_tensor("p0_vec", [P, P], F32))
        p0_ld = _es.enter_context(nc.sbuf_tensor("p0_ld", [P, 2 * WMAX], F32))
        p0_bf = _es.enter_context(nc.sbuf_tensor("p0_bf", [P, 2 * WMAX], BF16))
        p0_ev = _es.enter_context(nc.sbuf_tensor("p0_ev", [P, 4 * P], BF16))
        p0_st = _es.enter_context(nc.sbuf_tensor("p0_st", [P, 2 * 32 * P], BF16))
        p0_as = _es.enter_context(nc.sbuf_tensor("p0_as", [P, NI * STATE], F32))
        p0_e = _es.enter_context(nc.sbuf_tensor("p0_e", [P, STATE * TAPS], F32))
        p0_k = _es.enter_context(nc.sbuf_tensor("p0_k", [P, TAPS], F32))
        p0_k2 = _es.enter_context(nc.sbuf_tensor("p0_k2", [P, TAPS], F32))
        p0_z = _es.enter_context(nc.sbuf_tensor("p0_z", [P, 3 * TAPS], F32))
        p0_bias = _es.enter_context(nc.sbuf_tensor("p0_bias", [P, NO3], F32))
        with _es:
            def f(g):
                self.inc("eGP", g.memset(self.idn[:], 0.0))
                self.inc("eGP", g.memset(self.idnb[:], 0.0))
                g.iota(self.iota[:], pattern=[[1, TAPS]], base=0, channel_multiplier=0,
                        allow_small_or_imprecise_dtypes=True)
                g.memset(p0_z[:], 0.0)
                g.memset(self.v_eps[:], 1e-5)
                self.inc("eGP", g.memset(p0_bias[:], 0.0))
                self.w(g, "eGP", self.counts["eGP"])
                self.inc("eGP", g.affine_select(out=self.idn[:], in_=self.idn[:],
                                compare_op=ALU.not_equal,
                                fill=1.0, base=0, pattern=[[-1, P]], channel_multiplier=1))
                self.inc("eGP", g.affine_select(out=self.idnb[:], in_=self.idnb[:],
                                compare_op=ALU.not_equal,
                                fill=1.0, base=0, pattern=[[-1, P]], channel_multiplier=1))
                self.w(g, "eGP", self.counts["eGP"])
            self.G(f)
            g_consts = self.counts["eGP"]

            vecs = [(self.t_w1s, self.v_w1s, NH, None), (self.t_w2s, self.v_w2s, NH, None),
                    (self.t_h1s, self.v_h1s, NH, None), (self.t_h2s, self.v_h2s, NH, None),
                    (self.t_w3s, self.v_w3s, ND, 0.5),
                    (self.t_gamma, self.v_gamma, ND, None), (self.t_beta, self.v_beta, ND, None),
                    (self.t_convb, self.v_convb, NI, None), (self.t_outb, self.v_outb, ND, None),
                    (self.t_dp, self.v_dp, NI, None)]
            for vi, (src, dst, n_, fold) in enumerate(vecs):
                def f(sy, src=src, n_=n_):
                    self.w(sy, "ePE", mk.get(("p0v_tp", vi - 1), 0))
                    self.dma(sy, "s_vec", p0_vec[0:n_, :],
                             src.ap().rearrange("(a b) -> a b", b=P))
                self.SY(f)
                qwm = ("s_vec", self.counts["s_vec"])

                def f(t, n_=n_, qwm=qwm, vi=vi):
                    if vi == 0:
                        self.w(t, "eGP", g_consts)
                    self.wm(t, qwm)
                    self.w(t, "eSC", mk.get(("p0v_ev", vi - 1), 0))
                    mk[("p0v_tp", vi)] = self.inc(
                        "ePE", t.matmul(ps_a[:, 0:n_], p0_vec[0:n_, :], self.idn[0:n_, 0:n_], start=True, stop=True))
                self.T(f)

                def f(s, dst=dst, n_=n_, fold=fold, vi=vi):
                    self.w(s, "ePE", mk[("p0v_tp", vi)])
                    if fold is not None:
                        s.activation(dst[:, 0:n_], ps_a[:, 0:n_], AF.Identity, scale=fold)
                        mk[("p0v_ev", vi)] = self.inc("eSC", s.nop())
                    else:
                        mk[("p0v_ev", vi)] = self.inc("eSC", s.copy(dst[:, 0:n_], ps_a[:, 0:n_]))
                self.S(f)

            else:
                NI_scan = NI
            def f(sy):
                for ct in range(NI_scan):
                    self.dma(sy, "s_const", self.v_convw[:, 3 * ct:3 * ct + 3],
                             self.t_convw.ap()[P * ct:P * ct + P, :])
                self.dma(sy, "s_const", p0_as[:].rearrange("p (ct s) -> p ct s", s=STATE),
                         self.t_alog.ap().rearrange("(ct p) s -> p ct s", p=P))
                self.dma(sy, "s_const", self.sb_na[:].rearrange("p (ct s) -> p ct s", s=STATE),
                         self.t_bp.ap().rearrange("(ct p) s -> p ct s", p=P))
                self.dma(sy, "s_const", self.sb_cb[:].rearrange("p (ct s) -> p ct s", s=STATE),
                         self.t_cp.ap().rearrange("(ct p) s -> p ct s", p=P))
            self.SY(f)
            qw_scan = ("s_const", self.counts["s_const"])

            def f(v):
                self.wm(v, qw_scan)
                self.inc("eVE", v.tensor_mul(self.sb_cb[:], self.sb_cb[:], self.sb_na[:]))
            self.V(f)
            ve_cb = self.counts["eVE"]

            def f(s):
                self.w(s, "eVE", ve_cb)
                self.inc("eSC", s.activation(self.sb_na[:], p0_as[:], AF.Exp))
            self.S(f)
            sc_na = self.counts["eSC"]

            def f(v):
                self.w(v, "eSC", sc_na)
                self.inc("eVE", v.tensor_scalar(self.sb_na[:], self.sb_na[:], -1.0, None, ALU.mult))
            self.V(f)
            ve_na = self.counts["eVE"]


            rs = {"slot": 0, "pc": 0, "st": 0}

            def restripe_T(t_src, t_dst, O, K, tag):
                NK = K // P
                for ot in range(O // P):
                    gs = rs["slot"]; rs["slot"] += 1
                    slot = gs % 2
                    gst = rs["st"]; rs["st"] += 1
                    ssl = gst % 2

                    def f(sy, ot=ot, slot=slot, gs=gs):
                        for m_ in mk.get(("rs_ld", gs - 2), []):
                            self.wm(sy, m_)
                        self.dma(sy, "s_rs%d" % slot, p0_ld[:, slot * WMAX:slot * WMAX + K],
                                 t_src.ap()[P * ot:P * ot + P, :])
                    self.SY(f)
                    qwm = ("s_rs%d" % slot, self.counts["s_rs%d" % slot])

                    def f(v, slot=slot, qwm=qwm, gs=gs):
                        self.wm(v, qwm)
                        self.wm2(v, mk.get(("rs_bf", gs - 2), 0))
                        mk[("rs_cast", gs)] = self.inc("eVE", v.tensor_copy(
                            p0_bf[:, slot * WMAX:slot * WMAX + K],
                            p0_ld[:, slot * WMAX:slot * WMAX + K]))
                        mk[("rs_ld", gs)] = [("eVE", mk[("rs_cast", gs)])]
                    self.V(f)
                    ve_m = self.counts["eVE"]

                    for kt in range(NK):
                        pc = rs["pc"]; rs["pc"] += 1
                        tb = ps_c if pc % 2 == 0 else ps_d

                        def f(t, kt=kt, slot=slot, pc=pc, tb=tb, ve_m=ve_m, gs=gs, NK=NK):
                            self.w(t, "eVE", ve_m)
                            self.w(t, "eSC", mk.get(("rs_ev", pc - 2), 0))
                            mm = self.inc("ePE", t.matmul(
                                tb[:, 0:P],
                                p0_bf[:, slot * WMAX + P * kt: slot * WMAX + P * (kt + 1)],
                                self.idnb[:], start=True, stop=True))
                            if kt == NK - 1:
                                mk[("rs_bf", gs)] = ("ePE", mm)
                        self.T(f)
                        pem = self.counts["ePE"]

                        def f(s, kt=kt, pc=pc, tb=tb, pem=pem, ssl=ssl, gst=gst, NK=NK):
                            self.w(s, "ePE", pem)
                            if kt == 0:
                                self.wm(s, mk.get(("rst_out", gst - 2), 0))
                            s.copy(p0_st[:, ssl * 32 * P + kt * P: ssl * 32 * P + (kt + 1) * P],
                                   tb[:, 0:P])
                            mk[("rs_ev", pc)] = self.inc("eSC", s.nop())
                        self.S(f)
                    scm = self.counts["eSC"]

                    def f(g, ot=ot, ssl=ssl, gst=gst, scm=scm, NK=NK):
                        self.w(g, "eSC", scm)
                        mk[("rst_out", gst)] = self.dma(
                            g, "g_rso%d" % ssl, t_dst.ap()[ot],
                            p0_st[:, ssl * 32 * P: ssl * 32 * P + K])
                    self.G(f)

            def restripe_N(t_src, t_dst, K, O, tag, gamma=None, bias_beta=False):
                NK = K // P
                for kt in range(NK):
                    gs = rs["slot"]; rs["slot"] += 1
                    slot = gs % 2

                    def f(sy, kt=kt, slot=slot, gs=gs):
                        for m_ in mk.get(("rs_ld", gs - 2), []):
                            self.wm(sy, m_)
                        self.dma(sy, "s_rs%d" % slot, p0_ld[:, slot * WMAX:slot * WMAX + O],
                                 t_src.ap()[P * kt:P * kt + P, :])
                    self.SY(f)
                    qwm = ("s_rs%d" % slot, self.counts["s_rs%d" % slot])

                    def f(v, kt=kt, slot=slot, qwm=qwm, gs=gs):
                        self.wm(v, qwm)
                        self.wm2(v, mk.get(("rs_bf", gs - 2), 0))
                        if gamma is not None:
                            ii = v.tensor_scalar(p0_bf[:, slot * WMAX:slot * WMAX + O],
                                                 p0_ld[:, slot * WMAX:slot * WMAX + O],
                                                 gamma[:, kt:kt + 1], None, ALU.mult)
                        else:
                            ii = v.tensor_copy(p0_bf[:, slot * WMAX:slot * WMAX + O],
                                               p0_ld[:, slot * WMAX:slot * WMAX + O])
                        mk[("rs_cast", gs)] = self.inc("eVE", ii)
                        mk[("rs_ld", gs)] = [("eVE", mk[("rs_cast", gs)])]
                    self.V(f)

                    if bias_beta:
                        def f(t, kt=kt, slot=slot, qwm=qwm):
                            self.wm(t, qwm)
                            self.w(t, "eVE", mk.get((tag, "bacc", kt - 1), 0))
                            for ot in range(O // P):
                                t.matmul(ps_a[:, ot:ot + 1],
                                         p0_ld[:, slot * WMAX + P * ot: slot * WMAX + P * (ot + 1)],
                                         self.v_beta[:, kt:kt + 1], start=True, stop=True)
                            mk[(tag, "bmm", kt)] = self.inc("ePE", t.nop())
                            mk[("rs_ld", gs)] = mk.get(("rs_ld", gs), []) + [("ePE", mk[(tag, "bmm", kt)])]
                        self.T(f)
                        peb = self.counts["ePE"]

                        def f(v, kt=kt, peb=peb):
                            self.w(v, "ePE", peb)
                            mk[(tag, "bacc", kt)] = self.inc("eVE", v.tensor_tensor(
                                p0_bias[:, 0:O // P], p0_bias[:, 0:O // P],
                                ps_a[:, 0:O // P], ALU.add))
                        self.V(f)

                    def f(g, kt=kt, slot=slot, gs=gs):
                        self.w(g, "eVE", mk[("rs_cast", gs)])
                        dst = bass.AP(t_dst, kt * P,
                                      [[K, P], [P * K, O // P], [1, P]])
                        src = p0_bf[:, slot * WMAX:slot * WMAX + O].rearrange("p (ot o) -> p ot o", o=P)
                        m = self.dma(g, "g_rsn%d" % slot, dst, src)
                        mk[("rs_bf", gs)] = m
                    self.G(f)

            restripe_T(self.t_w1, self.t_w1p, H, D, "w1")
            restripe_T(self.t_w2, self.t_w2p, H, D, "w2")
            restripe_T(self.t_h1, self.t_h1p, H, H, "h1")
            restripe_T(self.t_h2, self.t_h2p, H, H, "h2")
            restripe_T(self.t_w3, self.t_w3p, D, H, "w3")
            restripe_N(self.t_inw, self.t_inwp, D, O3, "inw", gamma=self.v_gamma, bias_beta=True)
            restripe_N(self.t_outw, self.t_outwp, INNER, D, "outw")


            ve_bias = mk[("inw", "bacc", ND - 1)]

            def f(sy):
                self.w(sy, "ePE", mk[("p0v_tp", len(vecs) - 1)])
                self.dma(sy, "s_vec", p0_vec[0:NO3, :],
                         self.t_inb.ap().rearrange("(a b) -> a b", b=P))
            self.SY(f)
            qw_inb = ("s_vec", self.counts["s_vec"])

            def f(t):
                self.wm(t, qw_inb)
                self.w(t, "eVE", ve_bias)
                self.inc("ePE", t.matmul(ps_a[:, 256:256 + NO3], p0_vec[0:NO3, :],
                                            self.idn[0:NO3, 0:NO3], start=True, stop=True))
            self.T(f)
            pe_inb = self.counts["ePE"]

            def f(v):
                self.w(v, "ePE", pe_inb)
                self.inc("eVE", v.tensor_tensor(self.v_inb[:], p0_bias[:, 0:NO3],
                                                ps_a[:, 256:256 + NO3], ALU.add))
            self.V(f)
            ve_vinb = self.counts["eVE"]

            if self.stop_after == "p0":
                def f(g):
                    self.w(g, "eVE", ve_vinb)
                    self.dma(g, "g_dbg", self.t_vinb.ap(), self.v_inb[:])
                self.G(f)


    def kbuild_side_work(self):
        c = self.c
        STATE, TAPS, INNER = c.STATE, c.TAPS, c.INNER
        NI = INNER // P
        mk = self.mk
        works = []

        zmark = [0]

        def mkzero():
            def f(g):
                zmark[0] = self.inc("eGP", g.memset(self.sb_z3[:], 0.0))
            self.G(f)
        works.append(mkzero)

        for ct in range(NI):
            for ng in range(STATE // 4):
                def unit(ct=ct, ng=ng):
                    def f(s, ct=ct, ng=ng):
                        if ng == 0:
                            self.wm(s, mk.get(("kout", ct - 1), 0))
                        pv = (ct, ng - 1) if ng > 0 else (ct - 1, STATE // 4 - 1)
                        self.w(s, "eVE", mk.get(("kacc",) + pv, 0))
                        for j in range(4):
                            n_ = ng * 4 + j
                            s.activation(self.sb_ke[:, j * TAPS:(j + 1) * TAPS], self.iota[:],
                                         AF.Exp, scale=self.sb_na[:, ct * STATE + n_:ct * STATE + n_ + 1])
                        mk[("kexp", ct, ng)] = self.inc("eSC", s.nop())
                    self.S(f)

                    def f(v, ct=ct, ng=ng):
                        self.w(v, "eSC", mk[("kexp", ct, ng)])
                        for j in range(4):
                            n_ = ng * 4 + j
                            e_sl = self.sb_ke[:, j * TAPS:(j + 1) * TAPS]
                            cb_sl = self.sb_cb[:, ct * STATE + n_:ct * STATE + n_ + 1]
                            if n_ == 0:
                                v.tensor_scalar(self.sb_k2[:], e_sl, cb_sl, None, ALU.mult)
                            else:
                                src, dst = (self.sb_k2, self.sb_k) if n_ % 2 == 1 else (self.sb_k, self.sb_k2)
                                v.scalar_tensor_tensor(dst[:], e_sl, cb_sl, src[:], ALU.mult, ALU.add)
                        if ng == STATE // 4 - 1:
                            fin = self.sb_k if STATE % 2 == 0 else self.sb_k2
                            v.tensor_tensor(fin[:, 0:1], fin[:, 0:1], self.v_dp[:, ct:ct + 1], ALU.add)
                        mk[("kacc", ct, ng)] = self.inc("eVE", v.nop())
                    self.V(f)

                    if ng == STATE // 4 - 1:
                        def f(g, ct=ct):
                            fin = self.sb_k if STATE % 2 == 0 else self.sb_k2
                            self.w(g, "eVE", mk[("kacc", ct, STATE // 4 - 1)])
                            self.w(g, "eGP", zmark[0])
                            kst = "g_kpad%d" % (ct % 2)
                            self.dma(g, kst, self.t_kpad.ap()[P * ct:P * ct + P, 0:TAPS - 1],
                                     self.sb_z3[:, 0:TAPS - 1])
                            self.dma(g, kst, self.t_kpad.ap()[P * ct:P * ct + P, 2 * TAPS - 1:],
                                     self.sb_z3[:, 0:TAPS + 1])
                            mk[("kout", ct)] = self.dma(
                                g, kst, self.t_kpad.ap()[P * ct:P * ct + P, TAPS - 1:2 * TAPS - 1],
                                fin[:])
                        self.G(f)
                works.append(unit)

        kp_marks = [("g_kpad0", 0), ("g_kpad1", 0)]

        def wfull_one(j):
            def f(g, j=j):
                if j == 0:
                    kp_marks[0] = ("g_kpad0", self.counts.get("g_kpad0", 0))
                    kp_marks[1] = ("g_kpad1", self.counts.get("g_kpad1", 0))
                    self.wm(g, kp_marks[0])
                    self.wm(g, kp_marks[1])
                src = bass.AP(self.t_kpad, (TAPS - 1 - j), [[3 * TAPS, INNER], [1, 2 * TAPS]])
                dst = bass.AP(self.t_wfull, j * (2 * TAPS), [[P * 2 * TAPS, INNER], [1, 2 * TAPS]])
                self.dma(g, "g_wfull", dst, src)
                if j == P - 1:
                    self.qg_wfull = ("g_wfull", self.counts["g_wfull"])
            self.G(f)
        for j in range(P):
            works.append(lambda j=j: wfull_one(j))
        return works

    def ffn(self, t_in, t_outx, side_work=()):
        side_work = list(side_work)
        c, nc = self.c, self.nc
        D, L, H = c.D, c.L, c.H
        NH, ND = H // P, D // P
        TT = c.TT_FFN
        NT, NS = L // TT, TT // P
        HF = min(512, TT)
        NHF = TT // HF
        mk = {}
        tag = "ffnA" if t_outx is self.t_x1 else "ffnB"

        _es = contextlib.ExitStack()
        b_xs = _es.enter_context(nc.sbuf_tensor(tag + "_xs", [P, 2 * D], F32))
        b_xn = _es.enter_context(nc.sbuf_tensor(tag + "_xn", [P, 2 * D], BF16))
        b_xT = _es.enter_context(nc.sbuf_tensor(tag + "_xT", [P, ND * TT], BF16))
        b_h = _es.enter_context(nc.sbuf_tensor(tag + "_h", [P, NH * TT], BF16))
        b_g = _es.enter_context(nc.sbuf_tensor(tag + "_g", [P, NH * TT], BF16))
        b_w = _es.enter_context(nc.sbuf_tensor(tag + "_w", [P, 3 * H], BF16))
        b_t1 = _es.enter_context(nc.sbuf_tensor(tag + "_t1", [P, HF], F32))
        b_o = _es.enter_context(nc.sbuf_tensor(tag + "_o", [P, 2 * HF], F32))
        b_op = _es.enter_context(nc.sbuf_tensor(tag + "_op", [P, 2 * HF], F32))
        b_st = _es.enter_context(nc.sbuf_tensor(tag + "_st", [P, 64], F32))
        b_sq = _es.enter_context(nc.sbuf_tensor(tag + "_sq", [P, D], F32))
        mm0 = _es.enter_context(nc.psum_tensor(tag + "_p0", [P, 512], F32))
        mm1 = _es.enter_context(nc.psum_tensor(tag + "_p1", [P, 512], F32))
        mm2 = _es.enter_context(nc.psum_tensor(tag + "_p2", [P, 512], F32))
        mm3 = _es.enter_context(nc.psum_tensor(tag + "_p3", [P, 512], F32))
        tpa = _es.enter_context(nc.psum_tensor(tag + "_tpa", [P, 512], F32))
        tpb = _es.enter_context(nc.psum_tensor(tag + "_tpb", [P, 512], F32))
        tfa = _es.enter_context(nc.psum_tensor(tag + "_tfa", [P, 512], F32))
        tfb = _es.enter_context(nc.psum_tensor(tag + "_tfb", [P, 512], F32))
        with _es:
            tp2 = [tpa, tpb]
            tf2 = [tfa, tfb]
            mmps = [mm0, mm1, mm2, mm3]

            def f(g):
                for ch in range(8):
                    r0 = (L // 8) * ch
                    mk["precopy"] = self.dma(g, "g_pre", t_outx.ap()[r0:r0 + L // 8, :],
                                             t_in.ap()[r0:r0 + L // 8, :])
            self.G(f)

            for ti in range(NT):
                for s_ in range(NS):
                    sl = s_ % 2

                    def f(sy, ti=ti, s_=s_, sl=sl):
                        self.w(sy, "eVE", mk.get(("lnv", ti, s_ - 2),
                                                 mk.get(("lnv", ti - 1, NS - 2 + s_), 0)))
                        r0 = ti * TT + s_ * P
                        self.dma(sy, "s_x%d" % sl, b_xs[:, sl * D:sl * D + D],
                                 t_in.ap()[r0:r0 + P, :])
                    self.SY(f)
                    qwm = ("s_x%d" % sl, self.counts["s_x%d" % sl])

                    def f(s, sl=sl, qwm=qwm, ti=ti, s_=s_):
                        self.wm(s, qwm)
                        self.w(s, "eVE", mk.get(("lnr", ti, s_ - 1),
                                                 mk.get(("lnr", ti - 1, NS - 1 + s_), 0)))
                        s.activation(b_sq[:], b_xs[:, sl * D:sl * D + D], AF.Square)
                        mk[("lnsq", ti, s_)] = self.inc("eSC", s.nop())
                    self.S(f)

                    def f(v, sl=sl, qwm=qwm, ti=ti, s_=s_):
                        self.wm(v, qwm)
                        self.w(v, "eSC", mk[("lnsq", ti, s_)])
                        v.reduce_sum(b_st[:, s_ * 8 + 0:s_ * 8 + 1], b_xs[:, sl * D:sl * D + D], axis=AX)
                        mk[("lnr", ti, s_)] = self.inc("eVE", v.reduce_sum(b_st[:, s_ * 8 + 1:s_ * 8 + 2], b_sq[:], axis=AX))
                        v.tensor_scalar(b_st[:, s_ * 8 + 0:s_ * 8 + 1], b_st[:, s_ * 8 + 0:s_ * 8 + 1], 1.0 / D, None, ALU.mult)
                        v.tensor_scalar(b_st[:, s_ * 8 + 1:s_ * 8 + 2], b_st[:, s_ * 8 + 1:s_ * 8 + 2], 1.0 / D, None, ALU.mult)
                        v.tensor_scalar(b_st[:, s_ * 8 + 2:s_ * 8 + 3], b_st[:, s_ * 8 + 0:s_ * 8 + 1], b_st[:, s_ * 8 + 0:s_ * 8 + 1], None, ALU.mult)
                        self.inc("eVE", v.tensor_sub(b_st[:, s_ * 8 + 3:s_ * 8 + 4], b_st[:, s_ * 8 + 1:s_ * 8 + 2], b_st[:, s_ * 8 + 2:s_ * 8 + 3]))
                    self.V(f)
                    vevar = self.counts["eVE"]

                    def f(s, vevar=vevar):
                        self.w(s, "eVE", vevar)
                        self.inc("eSC", s.activation(b_st[:, s_ * 8 + 4:s_ * 8 + 5], b_st[:, s_ * 8 + 3:s_ * 8 + 4], AF.Sqrt,
                                                     bias=self.v_eps[:, 0:1]))
                    self.S(f)
                    scsd = self.counts["eSC"]

                    def f(v, sl=sl, scsd=scsd, ti=ti, s_=s_):
                        self.w(v, "eSC", scsd)
                        self.w(v, "ePE", mk.get(("xntp", ti, s_ - 2),
                                                mk.get(("xntp", ti - 1, NS - 2 + s_), 0)))
                        v.reciprocal(b_st[:, s_ * 8 + 5:s_ * 8 + 6], b_st[:, s_ * 8 + 4:s_ * 8 + 5])
                        mk[("lnv", ti, s_)] = self.inc("eVE", v.tensor_scalar(b_xn[:, sl * D:sl * D + D], b_xs[:, sl * D:sl * D + D],
                                        b_st[:, s_ * 8 + 0:s_ * 8 + 1], b_st[:, s_ * 8 + 5:s_ * 8 + 6], ALU.subtract, ALU.mult))
                    self.V(f)

                    for kt in range(ND):
                        pp = (s_ * ND + kt) % 2
                        tb = tp2[pp]

                        def f(t, ti=ti, s_=s_, sl=sl, kt=kt, tb=tb):
                            self.w(t, "eVE", mk[("lnv", ti, s_)])
                            self.w(t, "eSC", mk.get(("xTev", ti, s_ * ND + kt - 2), 0))
                            mm = self.inc("ePE", t.matmul(
                                tb[:, 0:P],
                                b_xn[:, sl * D + kt * P: sl * D + (kt + 1) * P], self.idnb[:], start=True, stop=True))
                            if kt == ND - 1:
                                mk[("xntp", ti, s_)] = mm
                        self.T(f)
                        pem = self.counts["ePE"]

                        def f(s, ti=ti, s_=s_, kt=kt, tb=tb, pem=pem):
                            self.w(s, "ePE", pem)
                            s.copy(b_xT[:, kt * TT + s_ * P: kt * TT + (s_ + 1) * P],
                                   tb[:, 0:P])
                            mk[("xTev", ti, s_ * ND + kt)] = self.inc("eSC", s.nop())
                        self.S(f)
                xT_ready = self.counts["eSC"]

                for ot in range(NH):
                    wsl = ot % 2
                    assert H >= 2 * D

                    def f(sy, ti=ti, ot=ot, wsl=wsl):
                        self.w(sy, "ePE", mk.get(("c1mm", ti, ot - 2), mk.get(("done", ti - 1), 0)))
                        self.dma(sy, "s_w%d" % wsl, b_w[:, wsl * H: wsl * H + D],
                                 self.t_w1p.ap()[ot])
                        self.dma(sy, "s_w%d" % wsl, b_w[:, wsl * H + D: wsl * H + 2 * D],
                                 self.t_w2p.ap()[ot])
                    self.SY(f)
                    qwm = ("s_w%d" % wsl, self.counts["s_w%d" % wsl])

                    for hf in range(NHF):
                        sidx = ot * NHF + hf
                        pa, pb = mmps[(2 * sidx) % 4], mmps[(2 * sidx + 1) % 4]

                        def f(t, ti=ti, ot=ot, hf=hf, wsl=wsl, qwm=qwm, pa=pa, pb=pb, sidx=sidx):
                            self.wm(t, qwm)
                            if ot == 0 and hf == 0:
                                self.w(t, "eSC", xT_ready)
                            self.w(t, "eVE", mk.get(("c1h", ti, sidx - 2),
                                                     mk.get(("c1h", ti - 1, NH * NHF - 2 + sidx), 0)))
                            for kt in range(ND):
                                rhs = b_xT[:, kt * TT + hf * HF: kt * TT + hf * HF + HF]
                                t.matmul(pa[:, 0:HF], b_w[:, wsl * H + kt * P: wsl * H + (kt + 1) * P],
                                         rhs, start=(kt == 0), stop=(kt == ND - 1))
                            for kt in range(ND):
                                rhs = b_xT[:, kt * TT + hf * HF: kt * TT + hf * HF + HF]
                                t.matmul(pb[:, 0:HF], b_w[:, wsl * H + D + kt * P: wsl * H + D + (kt + 1) * P],
                                         rhs, start=(kt == 0), stop=(kt == ND - 1))
                            mm = self.inc("ePE", t.nop())
                            if hf == NHF - 1:
                                mk[("c1mm", ti, ot)] = mm
                        self.T(f)
                        pem = self.counts["ePE"]

                        def f(s, ot=ot, hf=hf, pem=pem, pa=pa, ti=ti, sidx=sidx):
                            self.w(s, "ePE", pem)
                            self.w(s, "eVE", mk.get(("c1h", ti, sidx - 1),
                                                    mk.get(("c1h", ti - 1, NH * NHF - 1 + sidx), 0)))
                            s.activation(b_t1[:, 0:HF], pa[:, 0:HF],
                                         AF.Silu, scale=self.v_w1s[:, ot:ot + 1])
                            mk[("c1silu", ti, sidx)] = self.inc("eSC", s.nop())
                        self.S(f)
                        scm = self.counts["eSC"]

                        def f(v, ot=ot, hf=hf, scm=scm, pb=pb, ti=ti, sidx=sidx):
                            self.w(v, "eSC", scm)
                            mk[("c1h", ti, sidx)] = self.inc("eVE", v.scalar_tensor_tensor(
                                b_h[:, ot * TT + hf * HF: ot * TT + hf * HF + HF],
                                pb[:, 0:HF], self.v_w2s[:, ot:ot + 1],
                                b_t1[:, 0:HF], ALU.mult, ALU.mult))
                        self.V(f)
                h_ready = self.counts["eVE"]

                prev_done = ("c1mm", NH - 1)
                for stg, wp, vs, src, dst in [("g1", self.t_h1p, self.v_h1s, b_h, b_g),
                                              ("g2", self.t_h2p, self.v_h2s, b_g, b_h)]:
                    for ot in range(NH):
                        wsl = ot % 3

                        def f(sy, ti=ti, ot=ot, wsl=wsl, wp=wp, stg=stg, prev_done=prev_done):
                            if ot < 3:
                                self.w(sy, "ePE", mk.get((prev_done[0], ti, prev_done[1]), 0))
                            self.w(sy, "ePE", mk.get((stg + "mm", ti, ot - 3), 0))
                            self.dma(sy, "s_w%d" % wsl, b_w[:, wsl * H: wsl * H + H],
                                     wp.ap()[ot])
                        self.SY(f)
                        qwm = ("s_w%d" % wsl, self.counts["s_w%d" % wsl])

                        for hf in range(NHF):
                            sidx = ot * NHF + hf
                            pa = mmps[sidx % 4]

                            def f(t, ti=ti, ot=ot, hf=hf, wsl=wsl, qwm=qwm, pa=pa, stg=stg,
                                  src=src, sidx=sidx):
                                self.wm(t, qwm)
                                if ot == 0 and hf == 0 and stg == "g1":
                                    self.w(t, "eVE", h_ready)
                                if ot == 0 and hf == 0 and stg == "g2":
                                    self.w(t, "eSC", mk.get(("g1ev", ti, NH * NHF - 1), 0))
                                self.w(t, "eSC", mk.get((stg + "ev", ti, sidx - 4), 0))
                                for kt in range(NH):
                                    rhs = src[:, kt * TT + hf * HF: kt * TT + hf * HF + HF]
                                    t.matmul(pa[:, 0:HF], b_w[:, wsl * H + kt * P: wsl * H + (kt + 1) * P],
                                             rhs, start=(kt == 0), stop=(kt == NH - 1))
                                mm = self.inc("ePE", t.nop())
                                if hf == NHF - 1:
                                    mk[(stg + "mm", ti, ot)] = mm
                            self.T(f)
                            pem = self.counts["ePE"]

                            def f(s, ot=ot, hf=hf, pem=pem, pa=pa, vs=vs, dst=dst, stg=stg,
                                  ti=ti, sidx=sidx):
                                self.w(s, "ePE", pem)
                                s.activation(dst[:, ot * TT + hf * HF: ot * TT + hf * HF + HF],
                                             pa[:, 0:HF], AF.Gelu, scale=vs[:, ot:ot + 1])
                                mk[(stg + "ev", ti, sidx)] = self.inc("eSC", s.nop())
                            self.S(f)
                        if side_work:
                            side_work.pop(0)()
                    prev_done = (stg + "mm", NH - 1)

                for dt_ in range(ND):
                    wsl = dt_ % 3

                    def f(sy, ti=ti, dt_=dt_, wsl=wsl):
                        if dt_ < 3:
                            self.w(sy, "ePE", mk.get(("g2mm", ti, NH - 1), 0))
                        self.w(sy, "ePE", mk.get(("w3mm", ti, dt_ - 3), 0))
                        self.dma(sy, "s_w%d" % wsl, b_w[:, wsl * H: wsl * H + H],
                                 self.t_w3p.ap()[dt_])
                    self.SY(f)
                    qwm = ("s_w%d" % wsl, self.counts["s_w%d" % wsl])

                    for hf in range(NHF):
                        sidx = dt_ * NHF + hf
                        pa = mmps[sidx % 4]

                        def f(t, ti=ti, dt_=dt_, hf=hf, wsl=wsl, qwm=qwm, pa=pa, sidx=sidx):
                            self.wm(t, qwm)
                            if dt_ == 0 and hf == 0:
                                self.w(t, "eSC", mk.get(("g2ev", ti, NH * NHF - 1), 0))
                            self.w(t, "eSC", mk.get(("w3ev", ti, sidx - 4), 0))
                            for kt in range(NH):
                                rhs = b_h[:, kt * TT + hf * HF: kt * TT + hf * HF + HF]
                                t.matmul(pa[:, 0:HF], b_w[:, wsl * H + kt * P: wsl * H + (kt + 1) * P],
                                         rhs, start=(kt == 0), stop=(kt == NH - 1))
                            mm = self.inc("ePE", t.nop())
                            if hf == NHF - 1:
                                mk[("w3mm", ti, dt_)] = mm
                        self.T(f)
                        pem = self.counts["ePE"]

                        def f(s, dt_=dt_, hf=hf, pem=pem, pa=pa, ti=ti, sidx=sidx):
                            self.w(s, "ePE", pem)
                            self.w(s, "ePE", mk.get(("otp", ti, sidx - 2), 0))
                            s.activation(b_o[:, (sidx % 2) * HF:(sidx % 2) * HF + HF], pa[:, 0:HF],
                                         AF.Identity, scale=self.v_w3s[:, dt_:dt_ + 1])
                            mk[("w3ev", ti, sidx)] = self.inc("eSC", s.nop())
                        self.S(f)
                        scm = self.counts["eSC"]

                        for pj in range(HF // P):
                            pidx = sidx * (HF // P) + pj
                            tfb_ = tf2[pidx % 2]

                            def f(t, dt_=dt_, pj=pj, tfb_=tfb_, scm=scm, ti=ti, sidx=sidx, pidx=pidx):
                                self.w(t, "eSC", scm)
                                self.w(t, "eSC", mk.get(("opev", ti, pidx - 2), 0))
                                mm = self.inc("ePE", t.matmul(
                                    tfb_[:, 0:P],
                                    b_o[:, (sidx % 2) * HF + pj * P:(sidx % 2) * HF + (pj + 1) * P],
                                    self.idn[:], start=True, stop=True))
                                if pj == HF // P - 1:
                                    mk[("otp", ti, sidx)] = mm
                            self.T(f)
                            pet = self.counts["ePE"]

                            def f(s, pj=pj, tfb_=tfb_, pet=pet, ti=ti, sidx=sidx, pidx=pidx):
                                self.w(s, "ePE", pet)
                                self.wm(s, mk.get(("oacc", ti, sidx - 2),
                                                  mk.get(("oacc", ti - 1, ND * NHF - 2 + sidx), 0)))
                                s.copy(b_op[:, (sidx % 2) * HF + pj * P:(sidx % 2) * HF + (pj + 1) * P],
                                       tfb_[:, 0:P])
                                mk[("opev", ti, pidx)] = self.inc("eSC", s.nop())
                            self.S(f)
                        scm2 = self.counts["eSC"]

                        def f(g, ti=ti, dt_=dt_, hf=hf, scm2=scm2, sidx=sidx):
                            self.w(g, "eSC", scm2)
                            if ti == 0 and dt_ == 0 and hf == 0:
                                self.wm(g, mk["precopy"])
                            r0 = ti * TT + hf * HF
                            src = b_op[:, (sidx % 2) * HF:(sidx % 2) * HF + HF] \
                                .rearrange("p (pj q) -> p pj q", q=P)
                            dst3 = bass.AP(t_outx, r0 * D + dt_ * P,
                                           [[D, P], [P * D, HF // P], [1, P]])
                            mk[("oacc", ti, sidx)] = self.dma(
                                g, "g_oac%d" % (sidx % 2), dst3, src, accum_op=ALU.add)
                        self.G(f)
                mk[("done", ti)] = self.counts["ePE"]
            while side_work:
                side_work.pop(0)()


    def mixer_a1(self):
        c, nc = self.c, self.nc
        D, L, O3, INNER = c.D, c.L, c.O3, c.INNER
        ND, NO3, NI = D // P, O3 // P, INNER // P
        TT = 512
        NT, NS = L // TT, TT // P
        mk = {}

        _es = contextlib.ExitStack()
        b_xs = _es.enter_context(nc.sbuf_tensor("a1_xs", [P, 2 * D], F32))
        b_x1n = _es.enter_context(nc.sbuf_tensor("a1_x1n", [P, 2 * D], F32))
        b_r2 = _es.enter_context(nc.sbuf_tensor("a1_r2", [P, 2 * D], F32))
        b_xn = _es.enter_context(nc.sbuf_tensor("a1_xn", [P, 2 * D], BF16))
        b_xT = _es.enter_context(nc.sbuf_tensor("a1_xT", [P, ND * TT], BF16))
        b_w = _es.enter_context(nc.sbuf_tensor("a1_w", [P, NO3 * D], BF16))
        b_ev = _es.enter_context(nc.sbuf_tensor("a1_ev", [P, 2 * TT], F32))
        b_evb = _es.enter_context(nc.sbuf_tensor("a1_evb", [P, 2 * TT], BF16))
        b_st = _es.enter_context(nc.sbuf_tensor("a1_st", [P, 64], F32))
        b_sq = _es.enter_context(nc.sbuf_tensor("a1_sq", [P, D], F32))
        mm0 = _es.enter_context(nc.psum_tensor("a1_p0", [P, 512], F32))
        mm1 = _es.enter_context(nc.psum_tensor("a1_p1", [P, 512], F32))
        mm2 = _es.enter_context(nc.psum_tensor("a1_p2", [P, 512], F32))
        mm3 = _es.enter_context(nc.psum_tensor("a1_p3", [P, 512], F32))
        tpa = _es.enter_context(nc.psum_tensor("a1_tpa", [P, 512], F32))
        tpb = _es.enter_context(nc.psum_tensor("a1_tpb", [P, 512], F32))
        with _es:
            tp2 = [tpa, tpb]
            mmps = [mm0, mm1, mm2, mm3]

            inw_marks = []

            def f(sy):
                CH = NO3 // 4
                for g_ in range(4):
                    dst = b_w[:, g_ * CH * D:(g_ + 1) * CH * D].rearrange("p (a k) -> p a k", k=D)
                    src = self.t_inwp.ap()[g_ * CH:(g_ + 1) * CH].transpose([1, 0, 2])
                    inw_marks.append(self.dma(sy, "s_iw%d" % g_, dst, src))
            self.SY(f)

            for ti in range(NT):
                for s_ in range(NS):
                    sl = s_ % 2

                    def f(sy, ti=ti, s_=s_, sl=sl):
                        self.w(sy, "eVE", mk.get(("ln2v", ti, s_ - 2),
                                                 mk.get(("ln2v", ti - 1, NS - 2 + s_), 0)))
                        r0 = ti * TT + s_ * P
                        self.dma(sy, "s_x%d" % sl, b_xs[:, sl * D:sl * D + D],
                                 self.t_x1.ap()[r0:r0 + P, :])
                    self.SY(f)
                    qwm = ("s_x%d" % sl, self.counts["s_x%d" % sl])

                    def f(s, sl=sl, qwm=qwm, ti=ti, s_=s_):
                        self.wm(s, qwm)
                        self.w(s, "eVE", mk.get(("ln2r", ti, s_ - 1),
                                                 mk.get(("ln2r", ti - 1, NS - 1 + s_), 0)))
                        s.activation(b_sq[:], b_xs[:, sl * D:sl * D + D], AF.Square)
                        mk[("sq1", ti, s_)] = self.inc("eSC", s.nop())
                    self.S(f)

                    def f(v, sl=sl, qwm=qwm, ti=ti, s_=s_):
                        self.wm(v, qwm)
                        self.w(v, "eSC", mk[("sq1", ti, s_)])
                        v.reduce_sum(b_st[:, s_ * 8 + 0:s_ * 8 + 1], b_xs[:, sl * D:sl * D + D], axis=AX)
                        mk[("ln1r", ti, s_)] = self.inc("eVE", v.reduce_sum(b_st[:, s_ * 8 + 1:s_ * 8 + 2], b_sq[:], axis=AX))
                        v.tensor_scalar(b_st[:, s_ * 8 + 0:s_ * 8 + 1], b_st[:, s_ * 8 + 0:s_ * 8 + 1], 1.0 / D, None, ALU.mult)
                        v.tensor_scalar(b_st[:, s_ * 8 + 1:s_ * 8 + 2], b_st[:, s_ * 8 + 1:s_ * 8 + 2], 1.0 / D, None, ALU.mult)
                        v.tensor_scalar(b_st[:, s_ * 8 + 2:s_ * 8 + 3], b_st[:, s_ * 8 + 0:s_ * 8 + 1], b_st[:, s_ * 8 + 0:s_ * 8 + 1], None, ALU.mult)
                        self.inc("eVE", v.tensor_sub(b_st[:, s_ * 8 + 3:s_ * 8 + 4], b_st[:, s_ * 8 + 1:s_ * 8 + 2], b_st[:, s_ * 8 + 2:s_ * 8 + 3]))
                    self.V(f)
                    vevar = self.counts["eVE"]

                    def f(s, vevar=vevar):
                        self.w(s, "eVE", vevar)
                        self.inc("eSC", s.activation(b_st[:, s_ * 8 + 4:s_ * 8 + 5], b_st[:, s_ * 8 + 3:s_ * 8 + 4], AF.Sqrt,
                                                     bias=self.v_eps[:, 0:1]))
                    self.S(f)
                    scsd = self.counts["eSC"]

                    def f(v, sl=sl, scsd=scsd, ti=ti, s_=s_):
                        self.w(v, "eSC", scsd)
                        self.wm(v, mk.get(("r2o", ti, s_ - 2),
                                          mk.get(("r2o", ti - 1, NS - 2 + s_), 0)))
                        self.w(v, "eVE", mk.get(("ln2v", ti, s_ - 2),
                                                mk.get(("ln2v", ti - 1, NS - 2 + s_), 0)))
                        v.reciprocal(b_st[:, s_ * 8 + 5:s_ * 8 + 6], b_st[:, s_ * 8 + 4:s_ * 8 + 5])
                        v.tensor_scalar(b_x1n[:, sl * D:sl * D + D], b_xs[:, sl * D:sl * D + D],
                                        b_st[:, s_ * 8 + 0:s_ * 8 + 1], b_st[:, s_ * 8 + 5:s_ * 8 + 6], ALU.subtract, ALU.mult)
                        mk[("ln1v", ti, s_)] = self.inc("eVE", v.tensor_add(b_r2[:, sl * D:sl * D + D], b_xs[:, sl * D:sl * D + D],
                                     b_x1n[:, sl * D:sl * D + D]))
                    self.V(f)
                    ve1 = self.counts["eVE"]

                    def f(g, ti=ti, s_=s_, sl=sl, ve1=ve1):
                        self.w(g, "eVE", ve1)
                        r0 = ti * TT + s_ * P
                        mk[("r2o", ti, s_)] = self.dma(
                            g, "g_r2%d" % (s_ % 2), self.t_r2.ap()[r0:r0 + P, :],
                            b_r2[:, sl * D:sl * D + D])
                    self.G(f)

                    def f(s, sl=sl, ve1=ve1, ti=ti, s_=s_):
                        self.w(s, "eVE", ve1)
                        s.activation(b_sq[:], b_x1n[:, sl * D:sl * D + D], AF.Square)
                        mk[("sq2", ti, s_)] = self.inc("eSC", s.nop())
                    self.S(f)

                    def f(v, sl=sl, ti=ti, s_=s_):
                        self.w(v, "eSC", mk[("sq2", ti, s_)])
                        v.reduce_sum(b_st[:, s_ * 8 + 0:s_ * 8 + 1], b_x1n[:, sl * D:sl * D + D], axis=AX)
                        mk[("ln2r", ti, s_)] = self.inc("eVE", v.reduce_sum(b_st[:, s_ * 8 + 1:s_ * 8 + 2], b_sq[:], axis=AX))
                        v.tensor_scalar(b_st[:, s_ * 8 + 0:s_ * 8 + 1], b_st[:, s_ * 8 + 0:s_ * 8 + 1], 1.0 / D, None, ALU.mult)
                        v.tensor_scalar(b_st[:, s_ * 8 + 1:s_ * 8 + 2], b_st[:, s_ * 8 + 1:s_ * 8 + 2], 1.0 / D, None, ALU.mult)
                        v.tensor_scalar(b_st[:, s_ * 8 + 2:s_ * 8 + 3], b_st[:, s_ * 8 + 0:s_ * 8 + 1], b_st[:, s_ * 8 + 0:s_ * 8 + 1], None, ALU.mult)
                        self.inc("eVE", v.tensor_sub(b_st[:, s_ * 8 + 3:s_ * 8 + 4], b_st[:, s_ * 8 + 1:s_ * 8 + 2], b_st[:, s_ * 8 + 2:s_ * 8 + 3]))
                    self.V(f)
                    vevar2 = self.counts["eVE"]

                    def f(s, vevar2=vevar2):
                        self.w(s, "eVE", vevar2)
                        self.inc("eSC", s.activation(b_st[:, s_ * 8 + 4:s_ * 8 + 5], b_st[:, s_ * 8 + 3:s_ * 8 + 4], AF.Sqrt,
                                                     bias=self.v_eps[:, 0:1]))
                    self.S(f)
                    scsd2 = self.counts["eSC"]

                    def f(v, sl=sl, scsd2=scsd2, ti=ti, s_=s_):
                        self.w(v, "eSC", scsd2)
                        self.w(v, "ePE", mk.get(("xntp", ti, s_ - 2),
                                                mk.get(("xntp", ti - 1, NS - 2 + s_), 0)))
                        v.reciprocal(b_st[:, s_ * 8 + 5:s_ * 8 + 6], b_st[:, s_ * 8 + 4:s_ * 8 + 5])
                        mk[("ln2v", ti, s_)] = self.inc("eVE", v.tensor_scalar(b_xn[:, sl * D:sl * D + D], b_x1n[:, sl * D:sl * D + D],
                                        b_st[:, s_ * 8 + 0:s_ * 8 + 1], b_st[:, s_ * 8 + 5:s_ * 8 + 6], ALU.subtract, ALU.mult))
                    self.V(f)

                    for kt in range(ND):
                        pp = (s_ * ND + kt) % 2
                        tb = tp2[pp]

                        def f(t, ti=ti, s_=s_, sl=sl, kt=kt, tb=tb):
                            self.w(t, "eVE", mk[("ln2v", ti, s_)])
                            self.w(t, "eSC", mk.get(("xTev", ti, s_ * ND + kt - 2), 0))
                            mm = self.inc("ePE", t.matmul(
                                tb[:, 0:P],
                                b_xn[:, sl * D + kt * P: sl * D + (kt + 1) * P], self.idnb[:], start=True, stop=True))
                            if kt == ND - 1:
                                mk[("xntp", ti, s_)] = mm
                        self.T(f)
                        pem = self.counts["ePE"]

                        def f(s, ti=ti, s_=s_, kt=kt, tb=tb, pem=pem):
                            self.w(s, "ePE", pem)
                            s.copy(b_xT[:, kt * TT + s_ * P: kt * TT + (s_ + 1) * P],
                                   tb[:, 0:P])
                            mk[("xTev", ti, s_ * ND + kt)] = self.inc("eSC", s.nop())
                        self.S(f)
                xT_ready = self.counts["eSC"]

                for ot in range(NO3):
                    pa = mmps[ot % 4]

                    def f(t, ti=ti, ot=ot, pa=pa):
                        if ti == 0 and ot % (NO3 // 4) == 0:
                            self.wm(t, inw_marks[ot // (NO3 // 4)])
                        if ot == 0:
                            self.w(t, "eSC", xT_ready)
                        self.w(t, "eSC", mk.get(("ipev", ti, ot - 4), 0))
                        for kt in range(ND):
                            t.matmul(pa[:, 0:TT], b_w[:, ot * D + kt * P: ot * D + (kt + 1) * P],
                                     b_xT[:, kt * TT:(kt + 1) * TT], start=(kt == 0), stop=(kt == ND - 1))
                        mk[("ipmm", ti, ot)] = self.inc("ePE", t.nop())
                    self.T(f)
                    pem = self.counts["ePE"]

                    def f(s, ti=ti, ot=ot, pa=pa, pem=pem):
                        self.w(s, "ePE", pem)
                        self.wm(s, mk.get(("ipo", ti, ot - 2),
                                          mk.get(("ipo", ti - 1, NO3 - 2 + ot), 0)))
                        bias = self.v_inb[:, ot:ot + 1]
                        if ot < NI:
                            s.activation(b_ev[:, (ot % 2) * TT:(ot % 2) * TT + TT], pa[:, 0:TT],
                                         AF.Identity, bias=bias)
                        elif ot < 2 * NI:
                            s.activation(b_evb[:, (ot % 2) * TT:(ot % 2) * TT + TT], pa[:, 0:TT],
                                         AF.Silu, bias=bias)
                        else:
                            s.activation(b_evb[:, (ot % 2) * TT:(ot % 2) * TT + TT], pa[:, 0:TT],
                                         AF.Sigmoid, bias=bias)
                        mk[("ipev", ti, ot)] = self.inc("eSC", s.nop())
                    self.S(f)
                    scm = self.counts["eSC"]

                    def f(g, ti=ti, ot=ot, scm=scm):
                        self.w(g, "eSC", scm)
                        t0 = ti * TT
                        if ot < NI:
                            dst = self.t_u.ap()[P * ot:P * (ot + 1), t0:t0 + TT]
                            srcb = b_ev[:, (ot % 2) * TT:(ot % 2) * TT + TT]
                        elif ot < 2 * NI:
                            r = ot - NI
                            dst = self.t_sv.ap()[P * r:P * (r + 1), t0:t0 + TT]
                            srcb = b_evb[:, (ot % 2) * TT:(ot % 2) * TT + TT]
                        else:
                            r = ot - 2 * NI
                            dst = self.t_sg.ap()[P * r:P * (r + 1), t0:t0 + TT]
                            srcb = b_evb[:, (ot % 2) * TT:(ot % 2) * TT + TT]
                        mk[("ipo", ti, ot)] = self.dma(g, "g_ip%d" % (ot % 2), dst, srcb)
                    self.G(f)

    def mixer_a2(self):
        c, nc = self.c, self.nc
        L, INNER = c.L, c.INNER
        NI = INNER // P
        CT = 512
        NTT = L // CT
        NPJ = CT // P
        mk = {}

        _es = contextlib.ExitStack()
        b_u = _es.enter_context(nc.sbuf_tensor("a2_u", [P, 2 * (CT + 2)], F32))
        b_c = _es.enter_context(nc.sbuf_tensor("a2_c", [P, 2 * CT], F32))
        b_s = _es.enter_context(nc.sbuf_tensor("a2_s", [P, 2 * CT], F32))
        b_ev = _es.enter_context(nc.sbuf_tensor("a2_ev", [P, 4 * P], F32))
        b_uTB = _es.enter_context(nc.sbuf_tensor("a2_uTB", [P, 2 * L], BF16))
        tpa = _es.enter_context(nc.psum_tensor("a2_tpa", [P, 512], F32))
        tpb = _es.enter_context(nc.psum_tensor("a2_tpb", [P, 512], F32))
        with _es:
            tp2 = [tpa, tpb]
            for ct in range(NI):
                for tt in range(NTT):
                    sl = tt % 2
                    UW = CT + 2

                    def f(sy, ct=ct, tt=tt, sl=sl):
                        self.w(sy, "eVE", mk.get(("cv", ct, tt - 2), mk.get(("cv", ct - 1, NTT - 2 + tt), 0)))
                        t0 = tt * CT
                        lo = max(t0 - 1, 0)
                        hi = min(t0 + CT + 1, L)
                        off = 0 if tt > 0 else 1
                        self.dma(sy, "s_u%d" % sl,
                                 b_u[:, sl * UW + off: sl * UW + off + (hi - lo)],
                                 self.t_u.ap()[P * ct:P * (ct + 1), lo:hi])
                    self.SY(f)
                    qwm = ("s_u%d" % sl, self.counts["s_u%d" % sl])

                    if tt == 0 or tt == NTT - 1:
                        def f(g, ct=ct, tt=tt, sl=sl, qwm=qwm):
                            self.wm(g, qwm)
                            if tt == 0:
                                g.memset(b_u[:, sl * UW: sl * UW + 1], 0.0)
                            if tt == NTT - 1:
                                g.memset(b_u[:, sl * UW + CT + 1: sl * UW + CT + 2], 0.0)
                            mk[("hz", ct, tt)] = self.inc("eGP", g.nop())
                        self.G(f)

                    def f(v, ct=ct, tt=tt, sl=sl, qwm=qwm):
                        self.wm(v, qwm)
                        if ("hz", ct, tt) in mk:
                            self.w(v, "eGP", mk[("hz", ct, tt)])
                        self.w(v, "eSC", mk.get(("si", ct, tt - 2), mk.get(("si", ct - 1, NTT - 2 + tt), 0)))
                        k0 = self.v_convw[:, 3 * ct + 0:3 * ct + 1]
                        k1 = self.v_convw[:, 3 * ct + 1:3 * ct + 2]
                        k2 = self.v_convw[:, 3 * ct + 2:3 * ct + 3]
                        v.tensor_scalar(b_c[:, sl * CT: sl * CT + CT], b_u[:, sl * UW + 1: sl * UW + 1 + CT],
                                        k1, None, ALU.mult)
                        v.scalar_tensor_tensor(b_c[:, sl * CT: sl * CT + CT], b_u[:, sl * UW: sl * UW + CT],
                                               k0, b_c[:, sl * CT: sl * CT + CT], ALU.mult, ALU.add)
                        mk[("cv", ct, tt)] = self.inc("eVE", v.scalar_tensor_tensor(b_c[:, sl * CT: sl * CT + CT], b_u[:, sl * UW + 2: sl * UW + 2 + CT],
                                               k2, b_c[:, sl * CT: sl * CT + CT], ALU.mult, ALU.add))
                    self.V(f)
                    vem = self.counts["eVE"]

                    def f(s, ct=ct, tt=tt, sl=sl, vem=vem):
                        self.w(s, "eVE", vem)
                        self.w(s, "ePE", mk.get(("tp", ct, tt - 2), mk.get(("tp", ct - 1, NTT - 2 + tt), 0)))
                        s.activation(b_s[:, sl * CT: sl * CT + CT], b_c[:, sl * CT: sl * CT + CT],
                                     AF.Silu, bias=self.v_convb[:, ct:ct + 1])
                        mk[("si", ct, tt)] = self.inc("eSC", s.nop())
                    self.S(f)
                    scm = self.counts["eSC"]

                    for pj in range(NPJ):
                        gidx = (ct * NTT + tt) * NPJ + pj
                        psl = gidx % 4
                        tb = tp2[gidx % 2]

                        def f(t, ct=ct, tt=tt, sl=sl, pj=pj, tb=tb, scm=scm, gidx=gidx):
                            self.w(t, "eSC", scm)
                            self.w(t, "eSC", mk.get(("tev", gidx - 2), 0))
                            mm = self.inc("ePE", t.matmul(
                                tb[:, 0:P],
                                b_s[:, sl * CT + pj * P: sl * CT + (pj + 1) * P], self.idn[:], start=True, stop=True))
                            if pj == NPJ - 1:
                                mk[("tp", ct, tt)] = mm
                        self.T(f)
                        pem = self.counts["ePE"]

                        def f(s, ct=ct, tt=tt, pj=pj, tb=tb, pem=pem, gidx=gidx):
                            self.w(s, "ePE", pem)
                            if tt == 0 and pj == 0:
                                self.wm(s, mk.get(("uto", ct - 2), 0))
                            i_ = tt * NPJ + pj
                            s.copy(b_uTB[:, (ct % 2) * L + i_ * P:(ct % 2) * L + (i_ + 1) * P],
                                   tb[:, 0:P])
                            mk[("tev", gidx)] = self.inc("eSC", s.nop())
                        self.S(f)

                if True:
                    sct = self.counts["eSC"]

                    def f(g, ct=ct, sct=sct):
                        self.w(g, "eSC", sct)
                        mk[("uto", ct)] = self.dma(
                            g, "g_uT%d" % (ct % 2), self.t_uT.ap()[ct],
                            b_uTB[:, (ct % 2) * L:(ct % 2) * L + L])
                    self.G(f)

    def scan(self):
        c, nc = self.c, self.nc
        L, INNER, TAPS = c.L, c.INNER, c.TAPS
        NI, NCH = INNER // P, c.NCH
        mk = {}
        GB = 8
        NG = P // 8
        NSL = 4
        BW = (NCH + 1) * P

        _es = contextlib.ExitStack()
        b_B = _es.enter_context(nc.sbuf_tensor("sc_B", [P, 2 * BW], BF16))
        b_w = _es.enter_context(nc.sbuf_tensor("sc_w", [P, NSL * GB * 2 * TAPS], BF16))
        b_S = _es.enter_context(nc.sbuf_tensor("sc_S", [P, 2 * NCH * P], BF16))
        b_yt = _es.enter_context(nc.sbuf_tensor("sc_yt", [P, 2 * P], BF16))
        b_sv = _es.enter_context(nc.sbuf_tensor("sc_sv", [P, 2 * L], BF16))
        b_sg = _es.enter_context(nc.sbuf_tensor("sc_sg", [P, 2 * L], BF16))
        b_z = _es.enter_context(nc.sbuf_tensor("sc_z", [P, L], BF16))
        ps0 = _es.enter_context(nc.psum_tensor("sc_p0", [P, 512], F32))
        ps1 = _es.enter_context(nc.psum_tensor("sc_p1", [P, 512], F32))
        ps2 = _es.enter_context(nc.psum_tensor("sc_p2", [P, 512], F32))
        ps3 = _es.enter_context(nc.psum_tensor("sc_p3", [P, 512], F32))
        tpa = _es.enter_context(nc.psum_tensor("sc_tpa", [P, 512], F32))
        tpb = _es.enter_context(nc.psum_tensor("sc_tpb", [P, 512], F32))
        with _es:
            tp2 = [tpa, tpb]
            ymm = [ps0, ps1, ps2, ps3]

            def f(g):
                self.inc("eGP", g.memset(b_B[:, 0:P], 0.0))
                self.inc("eGP", g.memset(b_B[:, BW:BW + P], 0.0))
            self.G(f)
            gpz = self.counts["eGP"]

            def emit_wf_load(g_, cb):
                def f(sy, cb=cb, g_=g_):
                    self.w(sy, "ePE", mk.get(("wcons", cb, g_ - NSL),
                                             mk.get(("wcons", cb - 1, NG - NSL + g_), 0)))
                    if cb == 0 and g_ < NSL:
                        self.wm(sy, self.qg_wfull)
                    src = self.t_wfull.ap()[cb * P + g_ * GB: cb * P + (g_ + 1) * GB] \
                        .transpose([1, 0, 2])
                    dst = b_w[:, (g_ % NSL) * GB * 2 * TAPS:((g_ % NSL) + 1) * GB * 2 * TAPS] \
                        .rearrange("p (c f) -> p c f", c=GB)
                    self.dma(sy, "s_wf%d" % (g_ % NSL), dst, src)
                self.SY(f)
                mk[("wld", cb, g_)] = ("s_wf%d" % (g_ % NSL), self.counts["s_wf%d" % (g_ % NSL)])

            def emit_A(cb):
                sl = cb % 2
                for g_ in range(NSL):
                    emit_wf_load(g_, cb)

                def f(sy, cb=cb, sl=sl):
                    self.w(sy, "eGP", gpz)
                    self.w(sy, "ePE", mk.get(("bfree", cb - 2), 0))
                    self.dma(sy, "s_B%d" % sl, b_B[:, sl * BW + P: sl * BW + BW],
                             self.t_uT.ap()[cb])
                self.SY(f)
                qwB = ("s_B%d" % sl, self.counts["s_B%d" % sl])

                def f(sy, cb=cb, sl=sl):
                    self.w(sy, "eVE", mk.get(("gate", cb - 2), 0))
                    self.dma(sy, "s_sv%d" % sl, b_sv[:, sl * L: sl * L + L],
                             self.t_sv.ap()[P * cb:P * (cb + 1), :])
                    self.dma(sy, "s_sv%d" % sl, b_sg[:, sl * L: sl * L + L],
                             self.t_sg.ap()[P * cb:P * (cb + 1), :])
                self.SY(f)
                mk[("svld", cb)] = ("s_sv%d" % sl, self.counts["s_sv%d" % sl])

                for gr in range(NG):
                    pa = ymm[gr % 4]

                    def f(t, cb=cb, gr=gr, sl=sl, pa=pa, qwB=qwB):
                        self.wm(t, qwB)
                        self.wm(t, mk[("wld", cb, gr)])
                        self.w(t, "eSC", mk.get(("yev", cb, gr - 4), mk.get(("yev", cb - 1, NG - 4 + gr), 0)))
                        for cr in range(8):
                            cl = gr * 8 + cr
                            wbase = (gr % NSL) * GB * 2 * TAPS + cr * 2 * TAPS
                            W0 = b_w[:, wbase: wbase + TAPS]
                            W1 = b_w[:, wbase + TAPS: wbase + 2 * TAPS]
                            X0 = b_B[:, sl * BW + P + cl: sl * BW + P + cl + (NCH - 1) * P + 1: P]
                            X1 = b_B[:, sl * BW + cl: sl * BW + cl + (NCH - 1) * P + 1: P]
                            t.matmul(pa[:, cr * NCH:(cr + 1) * NCH], W0, X0, start=True, stop=False)
                            t.matmul(pa[:, cr * NCH:(cr + 1) * NCH], W1, X1, start=False, stop=True)
                        mm = self.inc("ePE", t.nop())
                        mk[("ymm", cb, gr)] = mm
                        if gr == NG - 1:
                            mk[("bfree", cb)] = mm
                        mk[("wcons", cb, gr)] = mm
                    self.T(f)
                    pem = self.counts["ePE"]
                    if gr + NSL < NG:
                        emit_wf_load(gr + NSL, cb)

                    def f(s, cb=cb, gr=gr, sl=sl, pa=pa, pem=pem):
                        self.w(s, "ePE", pem)
                        if gr == 0:
                            self.w(s, "ePE", mk.get(("stp", cb - 2), 0))
                        src = pa[:, 0:8 * NCH].rearrange("p (c i) -> p c i", c=8)
                        dst = b_S[:, sl * NCH * P:(sl + 1) * NCH * P] \
                            .rearrange("p (i c) -> p c i", c=P)[:, gr * 8:(gr + 1) * 8, :]
                        s.copy(dst, src)
                        mk[("yev", cb, gr)] = self.inc("eSC", s.nop())
                    self.S(f)
                mk[("S_ready", cb)] = self.counts["eSC"]

            def emit_B(cb):
                sl = cb % 2
                for m_ in range(NCH):
                    tb = tp2[m_ % 2]

                    def f(t, cb=cb, m_=m_, sl=sl, tb=tb):
                        self.w(t, "eSC", mk[("S_ready", cb)])
                        self.w(t, "eVE", mk.get(("gvm", cb, m_ - 2),
                                                mk.get(("gvm", cb - 1, NCH - 2 + m_), 0)))
                        mm = self.inc("ePE", t.matmul(
                            tb[:, 0:P], b_S[:, sl * NCH * P + m_ * P: sl * NCH * P + (m_ + 1) * P],
                            self.idnb[:], start=True, stop=True))
                        if m_ == NCH - 1:
                            mk[("stp", cb)] = mm
                    self.T(f)
                    pem = self.counts["ePE"]

                    def f(v, cb=cb, m_=m_, sl=sl, tb=tb, pem=pem):
                        self.w(v, "ePE", pem)
                        if m_ == 0:
                            self.wm(v, mk[("svld", cb)])
                            self.wm(v, mk.get(("zo", cb - 1), 0))
                        mk[("gvm", cb, m_)] = self.inc("eVE", v.tensor_mul(
                            b_yt[:, (m_ % 2) * P:(m_ % 2 + 1) * P], tb[:, 0:P],
                            b_sg[:, sl * L + m_ * P: sl * L + (m_ + 1) * P]))
                        gv = self.inc("eVE", v.tensor_tensor(
                            b_z[:, m_ * P:(m_ + 1) * P],
                            b_yt[:, (m_ % 2) * P:(m_ % 2 + 1) * P],
                            b_sv[:, sl * L + m_ * P: sl * L + (m_ + 1) * P], ALU.add))
                        if m_ == NCH - 1:
                            mk[("gate", cb)] = gv
                    self.V(f)

                def f(g, cb=cb):
                    self.w(g, "eVE", mk[("gate", cb)])
                    mk[("zo", cb)] = self.dma(
                        g, "g_z", self.t_z.ap()[P * cb:P * (cb + 1), :], b_z[:])
                self.G(f)

            emit_A(0)
            for cb in range(NI):
                if cb + 1 < NI:
                    emit_A(cb + 1)
                emit_B(cb)

    def outproj(self):
        c, nc = self.c, self.nc
        D, L, INNER = c.D, c.L, c.INNER
        ND, NI = D // P, INNER // P
        TT = 512
        NT, NS = L // TT, TT // P
        mk = {}

        _es = contextlib.ExitStack()
        b_w = _es.enter_context(nc.sbuf_tensor("op_w", [P, ND * NI * P], BF16))
        b_z = _es.enter_context(nc.sbuf_tensor("op_z", [P, 2 * NI * TT], BF16))
        b_o = _es.enter_context(nc.sbuf_tensor("op_o", [P, 2 * TT], F32))
        b_op = _es.enter_context(nc.sbuf_tensor("op_op", [P, 2 * TT], F32))
        ps0 = _es.enter_context(nc.psum_tensor("op_p0", [P, 512], F32))
        ps1 = _es.enter_context(nc.psum_tensor("op_p1", [P, 512], F32))
        tfa = _es.enter_context(nc.psum_tensor("op_tfa", [P, 512], F32))
        tfb = _es.enter_context(nc.psum_tensor("op_tfb", [P, 512], F32))
        with _es:
            tf2 = [tfa, tfb]
            mmps = [ps0, ps1]

            def f(g):
                for ch in range(8):
                    r0 = (L // 8) * ch
                    mk["precopy"] = self.dma(g, "g_pre", self.t_x2.ap()[r0:r0 + L // 8, :],
                                             self.t_r2.ap()[r0:r0 + L // 8, :])
            self.G(f)

            def f(sy):
                for dt_ in range(ND):
                    self.dma(sy, "s_opw", b_w[:, dt_ * NI * P:(dt_ + 1) * NI * P],
                             self.t_outwp.ap()[dt_])
            self.SY(f)
            qww = ("s_opw", self.counts["s_opw"])

            for ti in range(NT):
                zsl = ti % 2

                def f(sy, ti=ti, zsl=zsl):
                    self.w(sy, "ePE", mk.get(("opmm", ti - 2, ND - 1), 0))
                    t0 = ti * TT
                    src = bass.AP(self.t_z, t0, [[L, P], [P * L, NI], [1, TT]])
                    dst = b_z[:, zsl * NI * TT:(zsl + 1) * NI * TT].rearrange("p (k t) -> p k t", t=TT)
                    self.dma(sy, "s_z%d" % zsl, dst, src)
                self.SY(f)
                qwz = ("s_z%d" % zsl, self.counts["s_z%d" % zsl])

                for dt_ in range(ND):
                    pa = mmps[dt_ % 2]

                    def f(t, ti=ti, dt_=dt_, zsl=zsl, pa=pa, qwz=qwz):
                        self.wm(t, qwz)
                        self.wm(t, qww)
                        self.w(t, "eSC", mk.get(("opev", ti, dt_ - 2), mk.get(("opev", ti - 1, ND - 2 + dt_), 0)))
                        for kt in range(NI):
                            t.matmul(pa[:, 0:TT], b_w[:, (dt_ * NI + kt) * P:(dt_ * NI + kt + 1) * P],
                                     b_z[:, zsl * NI * TT + kt * TT: zsl * NI * TT + (kt + 1) * TT],
                                     start=(kt == 0), stop=(kt == NI - 1))
                        mk[("opmm", ti, dt_)] = self.inc("ePE", t.nop())
                    self.T(f)
                    pem = self.counts["ePE"]

                    def f(s, ti=ti, dt_=dt_, pa=pa, pem=pem):
                        self.w(s, "ePE", pem)
                        self.w(s, "ePE", mk.get(("optp", ti, dt_ - 2), 0))
                        s.activation(b_o[:, (dt_ % 2) * TT:(dt_ % 2) * TT + TT], pa[:, 0:TT],
                                     AF.Identity, bias=self.v_outb[:, dt_:dt_ + 1])
                        mk[("opev", ti, dt_)] = self.inc("eSC", s.nop())
                    self.S(f)
                    scm = self.counts["eSC"]

                    for pj in range(NS):
                        pidx = (ti * ND + dt_) * NS + pj
                        tfb_ = tf2[pidx % 2]

                        def f(t, ti=ti, dt_=dt_, pj=pj, tfb_=tfb_, scm=scm, pidx=pidx):
                            self.w(t, "eSC", scm)
                            self.w(t, "eSC", mk.get(("oev", pidx - 2), 0))
                            mm = self.inc("ePE", t.matmul(
                                tfb_[:, 0:P],
                                b_o[:, (dt_ % 2) * TT + pj * P:(dt_ % 2) * TT + (pj + 1) * P],
                                self.idn[:], start=True, stop=True))
                            if pj == NS - 1:
                                mk[("optp", ti, dt_)] = mm
                        self.T(f)
                        pet = self.counts["ePE"]

                        def f(s, ti=ti, dt_=dt_, pj=pj, tfb_=tfb_, pet=pet, pidx=pidx):
                            self.w(s, "ePE", pet)
                            self.wm(s, mk.get(("oacc", ti, dt_ - 2),
                                              mk.get(("oacc", ti - 1, ND - 2 + dt_), 0)))
                            s.copy(b_op[:, (dt_ % 2) * TT + pj * P:(dt_ % 2) * TT + (pj + 1) * P],
                                   tfb_[:, 0:P])
                            mk[("oev", pidx)] = self.inc("eSC", s.nop())
                        self.S(f)
                    scm2 = self.counts["eSC"]

                    def f(g, ti=ti, dt_=dt_, scm2=scm2):
                        self.w(g, "eSC", scm2)
                        if ti == 0 and dt_ == 0:
                            self.wm(g, mk["precopy"])
                        r0 = ti * TT
                        dst3 = bass.AP(self.t_x2, r0 * D + dt_ * P, [[D, P], [P * D, NS], [1, P]])
                        src = b_op[:, (dt_ % 2) * TT:(dt_ % 2) * TT + TT].rearrange("p (pj q) -> p pj q", q=P)
                        mk[("oacc", ti, dt_)] = self.dma(g, "g_oac%d" % (dt_ % 2), dst3, src,
                                                         accum_op=ALU.add)
                    self.G(f)


def build_program(cfg: Cfg, stop_after=None):
    m = MK(cfg, stop_after)
    nc = m.build()
    m._stack.close()
    return nc, m


_CFG = Cfg(D=1024, L=8192, H=4096, STATE=64, TT_FFN=1024)
_NCORES = 8


def kernel(**inputs):
    from concourse.bass_utils import run_bass_kernel_spmd
    nc, m = build_program(_CFG, None)
    INNER = _CFG.INNER
    x = inputs["L0_attn_x"]
    base = {
        "norm_gamma": inputs["L0_attn_norm_gamma"],
        "norm_beta": inputs["L0_attn_norm_beta"],
        "in_w": inputs["L0_attn_in_w"],
        "in_b": inputs["L0_attn_in_b"],
        "conv_w": np.reshape(inputs["L0_attn_conv_w"], (INNER, 3)),
        "conv_b": inputs["L0_attn_conv_b"],
        "A_log": inputs["L0_attn_A_log"],
        "B_p": inputs["L0_attn_B_p"],
        "C_p": inputs["L0_attn_C_p"],
        "D_p": inputs["L0_attn_D_p"],
        "out_w": inputs["L0_attn_out_w"],
        "out_b": inputs["L0_attn_out_b"],
        "w1_w": inputs["L0_ffn_w1_w"], "w1_s": inputs["L0_ffn_w1_s"],
        "w2_w": inputs["L0_ffn_w2_w"], "w2_s": inputs["L0_ffn_w2_s"],
        "h1_w": inputs["L0_ffn_h1_w"], "h1_s": inputs["L0_ffn_h1_s"],
        "h2_w": inputs["L0_ffn_h2_w"], "h2_s": inputs["L0_ffn_h2_s"],
        "w3_w": inputs["L0_ffn_w3_w"], "w3_s": inputs["L0_ffn_w3_s"],
    }
    in_maps = [dict(base, x=np.ascontiguousarray(x[b])) for b in range(_NCORES)]
    res = run_bass_kernel_spmd(nc, in_maps, core_ids=list(range(_NCORES)))
    return np.stack([np.asarray(res.results[b]["out"]) for b in range(_NCORES)])


# version 12
# speedup vs baseline: 1.0938x; 1.0625x over previous
import contextlib
import numpy as np
import concourse.bass as bass
import concourse.mybir as mybir

F32 = mybir.dt.float32
BF16 = mybir.dt.bfloat16
AF = mybir.ActivationFunctionType
ALU = mybir.AluOpType
AX = mybir.AxisListType.X
P = 128


class Cfg:
    def __init__(self, D=1024, L=8192, H=4096, STATE=64, TT_FFN=1024):
        self.D, self.L, self.H, self.STATE = D, L, H, STATE
        self.INNER, self.O3 = 2 * D, 6 * D
        self.TAPS = 128
        self.TT_FFN = TT_FFN
        assert L % TT_FFN == 0 and L % P == 0 and L % 512 == 0
        self.NCH = L // P



class RecInst:
    def __init__(self, method, args, kwargs, is_nop=False):
        self.method, self.args, self.kwargs = method, args, kwargs
        self.incs = []
        self.is_nop = is_nop


class RecWait:
    def __init__(self, sem_name, val):
        self.sem_name, self.val = sem_name, val


class Rec:
    def __init__(self, owner):
        self._ops = []
        self._owner = owner

    def wait_ge(self, sem_handle_or_name, val):
        self._ops.append(RecWait(sem_handle_or_name, val))

    def nop(self):
        e = RecInst("nop", (), {}, is_nop=True)
        e.rec = self
        self._ops.append(e)
        return e

    def __getattr__(self, m):
        if m.startswith("_"):
            raise AttributeError(m)

        def call(*a, **k):
            e = RecInst(m, a, k)
            e.rec = self
            self._ops.append(e)
            return e
        return call

    def last_real(self):
        for e in reversed(self._ops):
            if isinstance(e, RecInst):
                return None if e.is_nop else e
        return None

    def replay(self, eng, sems):
        for e in self._ops:
            if isinstance(e, RecWait):
                eng.wait_ge(sems[e.sem_name], e.val)
            else:
                if e.is_nop and not e.incs:
                    continue
                meth = "drain" if (e.is_nop and e.incs) else e.method
                inst = getattr(eng, meth)(*(() if meth == "drain" else e.args),
                                          **({} if meth == "drain" else e.kwargs))
                for (nm, kk) in e.incs:
                    inst.then_inc(sems[nm], kk)


class MK:
    SEMS = ["ePE", "eSC", "eVE", "eGP", "eSY"]

    def __init__(self, cfg: Cfg, stop_after=None):
        self.c = cfg
        self.stop_after = stop_after
        self.nc = bass.Bass("TRN2", target_bir_lowering=False)
        self.counts = {}
        self.sems = {}
        self.dma_streams = []
        self.mk = {}
        self._entry_rec = {}
        self.rT, self.rS, self.rV = Rec(self), Rec(self), Rec(self)
        self.rG, self.rSY = Rec(self), Rec(self)
        self._stack = contextlib.ExitStack()

    def stream(self, name):
        if name not in self.sems:
            self.sems[name] = self._stack.enter_context(self.nc.semaphore(name))
            self.counts[name] = 0
            self.dma_streams.append(name)
        return name

    def dma(self, rec, stname, out, in_, **kw):
        st = self.stream(stname)
        inst = rec.dma_start(out=out, in_=in_, **kw)
        c = self.inc(st, inst, 16)
        return (st, c)

    def wm(self, eng, mark):
        if mark:
            self.w(eng, mark[0], mark[1])

    def wm2(self, eng, mark):
        if mark:
            self.w(eng, mark[0], mark[1])

    def inc(self, name, entry, k=1):
        if entry.is_nop:
            rec = entry.rec
            idx = rec._ops.index(entry)
            prev = rec._ops[idx - 1] if idx > 0 else None
            if isinstance(prev, RecInst) and not prev.is_nop and not prev.incs:
                entry = prev
        if entry.incs and not entry.is_nop:
            nxt = RecInst("drain", (), {})
            nxt.rec = entry.rec
            pos = entry.rec._ops.index(entry) + 1
            entry.rec._ops.insert(pos, nxt)
            entry = nxt
        entry.incs.append((name, k))
        self.counts[name] += k
        return self.counts[name]

    def w(self, eng, name, val):
        if val and val > 0:
            eng.wait_ge(name, val)

    def _run(self, rec, f):
        f(rec)

    def T(self, f): self._run(self.rT, f)
    def S(self, f): self._run(self.rS, f)
    def V(self, f): self._run(self.rV, f)
    def G(self, f): self._run(self.rG, f)
    def SY(self, f): self._run(self.rSY, f)

    def build(self):
        c, nc = self.c, self.nc
        D, L, H, INNER, O3, STATE, TAPS = c.D, c.L, c.H, c.INNER, c.O3, c.STATE, c.TAPS
        NH, ND, NI, NO3 = H // P, D // P, INNER // P, O3 // P
        st = self._stack

        for s in self.SEMS:
            self.sems[s] = st.enter_context(nc.semaphore(s))
            self.counts[s] = 0

        def din(name, shape):
            return nc.dram_tensor(name, shape, F32, kind="ExternalInput")
        self.t_x = din("x", [L, D])
        self.t_gamma, self.t_beta = din("norm_gamma", [D]), din("norm_beta", [D])
        self.t_inw, self.t_inb = din("in_w", [D, O3]), din("in_b", [O3])
        self.t_convw, self.t_convb = din("conv_w", [INNER, 3]), din("conv_b", [INNER])
        self.t_alog, self.t_bp = din("A_log", [INNER, STATE]), din("B_p", [INNER, STATE])
        self.t_cp, self.t_dp = din("C_p", [INNER, STATE]), din("D_p", [INNER])
        self.t_outw, self.t_outb = din("out_w", [INNER, D]), din("out_b", [D])
        self.t_w1, self.t_w1s = din("w1_w", [H, D]), din("w1_s", [H])
        self.t_w2, self.t_w2s = din("w2_w", [H, D]), din("w2_s", [H])
        self.t_h1, self.t_h1s = din("h1_w", [H, H]), din("h1_s", [H])
        self.t_h2, self.t_h2s = din("h2_w", [H, H]), din("h2_s", [H])
        self.t_w3, self.t_w3s = din("w3_w", [D, H]), din("w3_s", [D])
        self.t_out = nc.dram_tensor("out", [L, D], F32, kind="ExternalOutput")

        sa = self.stop_after

        def scratch(name, shape, dtype, dbg=False):
            return nc.dram_tensor(name, shape, dtype,
                                  kind="ExternalOutput" if dbg else "Internal")
        self.t_w1p = scratch("w1p", [NH, P, D], BF16, sa == "p0")
        self.t_w2p = scratch("w2p", [NH, P, D], BF16)
        self.t_h1p = scratch("h1p", [NH, P, H], BF16)
        self.t_h2p = scratch("h2p", [NH, P, H], BF16)
        self.t_w3p = scratch("w3p", [ND, P, H], BF16, sa == "p0")
        self.t_inwp = scratch("inwp", [NO3, P, D], BF16, sa == "p0")
        self.t_outwp = scratch("outwp", [ND, P, INNER], BF16)
        self.t_kpad = scratch("kpad", [INNER, 3 * TAPS], BF16, sa == "p0")
        self.t_wfull = scratch("wfull", [INNER, P, 2 * TAPS], BF16, sa == "p0")
        self.t_vinb = scratch("vinb_dbg", [P, NO3], F32, sa == "p0")
        self.t_x1 = scratch("x1", [L, D], F32, sa == "p1")
        self.t_r2 = scratch("r2", [L, D], F32, sa in ("a1", "a2", "scan"))
        self.t_u = scratch("u_pre", [INNER, L], F32, sa == "a1")
        self.t_sv = scratch("sv", [INNER, L], BF16, sa == "a1")
        self.t_sg = scratch("sg", [INNER, L], BF16)
        self.t_uT = scratch("uT", [INNER // P, P, c.NCH * P], BF16, sa == "a2")
        self.t_z = scratch("z", [INNER, L], BF16, sa == "scan")
        self.t_x2 = scratch("x2", [L, D], F32, sa == "oproj")

        def sb(name, shape, dtype=F32):
            return st.enter_context(nc.sbuf_tensor(name, shape, dtype))
        self.idn = sb("idn", [P, P])
        self.idnb = sb("idnb", [P, P], BF16)
        self.iota = sb("iota", [P, TAPS])
        self.v_w1s, self.v_w2s = sb("v_w1s", [P, NH]), sb("v_w2s", [P, NH])
        self.v_h1s, self.v_h2s = sb("v_h1s", [P, NH]), sb("v_h2s", [P, NH])
        self.v_w3s = sb("v_w3s", [P, ND])
        self.v_gamma, self.v_beta = sb("v_gamma", [P, ND]), sb("v_beta", [P, ND])
        self.v_inb = sb("v_inb", [P, NO3])
        self.v_convw = sb("v_convw", [P, 3 * NI])
        self.v_convb = sb("v_convb", [P, NI])
        self.v_outb = sb("v_outb", [P, ND])
        self.v_dp = sb("v_dp", [P, NI])
        self.v_eps = sb("v_eps", [P, 1])
        self.sb_na = sb("sb_na", [P, NI * STATE])
        self.sb_cb = sb("sb_cb", [P, NI * STATE])
        self.sb_ke = sb("sb_ke", [P, 4 * self.c.TAPS], BF16)
        self.sb_k = sb("sb_k", [P, self.c.TAPS])
        self.sb_k2 = sb("sb_k2", [P, self.c.TAPS])
        self.sb_z3 = sb("sb_z3", [P, self.c.TAPS + 1])

        done = False
        self.phase0()
        self.barrier()
        if sa == "p0":
            done = True
        if not done:
            self.ffn(self.t_x, self.t_x1, side_work=self.kbuild_side_work())
            self.barrier()
            done = sa == "p1"
        if not done:
            self.mixer_a1()
            self.barrier()
            done = sa == "a1"
        if not done:
            self.mixer_a2()
            self.barrier()
            done = sa == "a2"
        if not done:
            self.scan()
            self.barrier()
            done = sa == "scan"
        if not done:
            self.outproj()
            self.barrier()
            done = sa == "oproj"
        if not done:
            self.ffn(self.t_x2, self.t_out)
        self.final_wait()
        self._auto_drain()
        with nc.Block() as blk:
            blk.tensor(lambda e: self.rT.replay(e, self.sems))
            blk.scalar(lambda e: self.rS.replay(e, self.sems))
            blk.vector(lambda e: self.rV.replay(e, self.sems))
            blk.gpsimd(lambda e: self.rG.replay(e, self.sems))
            blk.sync(lambda e: self.rSY.replay(e, self.sems))
        return nc

    def _auto_drain(self):
        for rec in (self.rV, self.rS, self.rG):
            newops = []
            pend = False
            for e in rec._ops:
                if isinstance(e, RecInst) and not e.is_nop and e.method != "drain":
                    if pend:
                        d = RecInst("drain", (), {})
                        d.rec = rec
                        newops.append(d)
                    newops.append(e)
                    pend = True
                else:
                    if isinstance(e, RecInst) and e.method == "drain":
                        pend = False
                    newops.append(e)
            rec._ops[:] = newops

    def barrier(self):
        cn = self.counts
        eng = [("ePE", self.T), ("eSC", self.S), ("eVE", self.V), ("eGP", self.G), ("eSY", self.SY)]
        sync_streams = [st for st in self.dma_streams if st.startswith("s_")]
        g_streams = [st for st in self.dma_streams
                     if st.startswith("g_") and st != "g_wfull"]
        for nm, sec in eng:
            def f(e, nm=nm):
                if nm == "eSY":
                    for st in sync_streams:
                        self.w(e, st, cn[st])
                if nm == "eGP":
                    for st in g_streams:
                        self.w(e, st, cn[st])
                self.inc(nm, e.nop())
            sec(f)
        targ = {nm: cn[nm] for nm, _ in eng}
        for nm, sec in eng:
            def f(e, nm=nm):
                for o, _ in eng:
                    if o != nm:
                        e.wait_ge(o, targ[o])
                self.inc(nm, e.nop())
            sec(f)
        targ2 = {nm: cn[nm] for nm, _ in eng}
        for nm, sec in eng:
            def f(e, nm=nm):
                for o, _ in eng:
                    if o != nm:
                        e.wait_ge(o, targ2[o])
            sec(f)

    def final_wait(self):
        self.barrier()
        tot = {st: self.counts[st] for st in self.dma_streams}

        def f(g):
            for st, v in tot.items():
                self.w(g, st, v)
        self.G(f)

    def phase0(self):
        c, nc = self.c, self.nc
        D, H, INNER, O3, STATE, TAPS = c.D, c.H, c.INNER, c.O3, c.STATE, c.TAPS
        NH, ND, NI, NO3 = H // P, D // P, INNER // P, O3 // P
        mk = self.mk
        WMAX = max(O3, H)

        _es = contextlib.ExitStack()
        ps_a = _es.enter_context(nc.psum_tensor("p0ps_a", [P, 512], F32))
        ps_c = _es.enter_context(nc.psum_tensor("p0ps_c", [P, 512], F32))
        ps_d = _es.enter_context(nc.psum_tensor("p0ps_d", [P, 512], F32))
        p0_vec = _es.enter_context(nc.sbuf_tensor("p0_vec", [P, P], F32))
        p0_ld = _es.enter_context(nc.sbuf_tensor("p0_ld", [P, 2 * WMAX], F32))
        p0_bf = _es.enter_context(nc.sbuf_tensor("p0_bf", [P, 2 * WMAX], BF16))
        p0_ev = _es.enter_context(nc.sbuf_tensor("p0_ev", [P, 4 * P], BF16))
        p0_st = _es.enter_context(nc.sbuf_tensor("p0_st", [P, 2 * 32 * P], BF16))
        p0_as = _es.enter_context(nc.sbuf_tensor("p0_as", [P, NI * STATE], F32))
        p0_e = _es.enter_context(nc.sbuf_tensor("p0_e", [P, STATE * TAPS], F32))
        p0_k = _es.enter_context(nc.sbuf_tensor("p0_k", [P, TAPS], F32))
        p0_k2 = _es.enter_context(nc.sbuf_tensor("p0_k2", [P, TAPS], F32))
        p0_z = _es.enter_context(nc.sbuf_tensor("p0_z", [P, 3 * TAPS], F32))
        p0_bias = _es.enter_context(nc.sbuf_tensor("p0_bias", [P, NO3], F32))
        with _es:
            def f(g):
                self.inc("eGP", g.memset(self.idn[:], 0.0))
                self.inc("eGP", g.memset(self.idnb[:], 0.0))
                g.iota(self.iota[:], pattern=[[1, TAPS]], base=0, channel_multiplier=0,
                        allow_small_or_imprecise_dtypes=True)
                g.memset(p0_z[:], 0.0)
                g.memset(self.v_eps[:], 1e-5)
                self.inc("eGP", g.memset(p0_bias[:], 0.0))
                self.w(g, "eGP", self.counts["eGP"])
                self.inc("eGP", g.affine_select(out=self.idn[:], in_=self.idn[:],
                                compare_op=ALU.not_equal,
                                fill=1.0, base=0, pattern=[[-1, P]], channel_multiplier=1))
                self.inc("eGP", g.affine_select(out=self.idnb[:], in_=self.idnb[:],
                                compare_op=ALU.not_equal,
                                fill=1.0, base=0, pattern=[[-1, P]], channel_multiplier=1))
                self.w(g, "eGP", self.counts["eGP"])
            self.G(f)
            g_consts = self.counts["eGP"]

            vecs = [(self.t_w1s, self.v_w1s, NH, None), (self.t_w2s, self.v_w2s, NH, None),
                    (self.t_h1s, self.v_h1s, NH, None), (self.t_h2s, self.v_h2s, NH, None),
                    (self.t_w3s, self.v_w3s, ND, 0.5),
                    (self.t_gamma, self.v_gamma, ND, None), (self.t_beta, self.v_beta, ND, None),
                    (self.t_convb, self.v_convb, NI, None), (self.t_outb, self.v_outb, ND, None),
                    (self.t_dp, self.v_dp, NI, None)]
            for vi, (src, dst, n_, fold) in enumerate(vecs):
                def f(sy, src=src, n_=n_):
                    self.w(sy, "ePE", mk.get(("p0v_tp", vi - 1), 0))
                    self.dma(sy, "s_vec", p0_vec[0:n_, :],
                             src.ap().rearrange("(a b) -> a b", b=P))
                self.SY(f)
                qwm = ("s_vec", self.counts["s_vec"])

                def f(t, n_=n_, qwm=qwm, vi=vi):
                    if vi == 0:
                        self.w(t, "eGP", g_consts)
                    self.wm(t, qwm)
                    self.w(t, "eSC", mk.get(("p0v_ev", vi - 1), 0))
                    mk[("p0v_tp", vi)] = self.inc(
                        "ePE", t.matmul(ps_a[:, 0:n_], p0_vec[0:n_, :], self.idn[0:n_, 0:n_], start=True, stop=True))
                self.T(f)

                def f(s, dst=dst, n_=n_, fold=fold, vi=vi):
                    self.w(s, "ePE", mk[("p0v_tp", vi)])
                    if fold is not None:
                        s.activation(dst[:, 0:n_], ps_a[:, 0:n_], AF.Identity, scale=fold)
                        mk[("p0v_ev", vi)] = self.inc("eSC", s.nop())
                    else:
                        mk[("p0v_ev", vi)] = self.inc("eSC", s.copy(dst[:, 0:n_], ps_a[:, 0:n_]))
                self.S(f)

            else:
                NI_scan = NI
            def f(sy):
                for ct in range(NI_scan):
                    self.dma(sy, "s_const", self.v_convw[:, 3 * ct:3 * ct + 3],
                             self.t_convw.ap()[P * ct:P * ct + P, :])
                self.dma(sy, "s_const", p0_as[:].rearrange("p (ct s) -> p ct s", s=STATE),
                         self.t_alog.ap().rearrange("(ct p) s -> p ct s", p=P))
                self.dma(sy, "s_const", self.sb_na[:].rearrange("p (ct s) -> p ct s", s=STATE),
                         self.t_bp.ap().rearrange("(ct p) s -> p ct s", p=P))
                self.dma(sy, "s_const", self.sb_cb[:].rearrange("p (ct s) -> p ct s", s=STATE),
                         self.t_cp.ap().rearrange("(ct p) s -> p ct s", p=P))
            self.SY(f)
            qw_scan = ("s_const", self.counts["s_const"])

            def f(v):
                self.wm(v, qw_scan)
                self.inc("eVE", v.tensor_mul(self.sb_cb[:], self.sb_cb[:], self.sb_na[:]))
            self.V(f)
            ve_cb = self.counts["eVE"]

            def f(s):
                self.w(s, "eVE", ve_cb)
                self.inc("eSC", s.activation(self.sb_na[:], p0_as[:], AF.Exp))
            self.S(f)
            sc_na = self.counts["eSC"]

            def f(v):
                self.w(v, "eSC", sc_na)
                self.inc("eVE", v.tensor_scalar(self.sb_na[:], self.sb_na[:], -1.0, None, ALU.mult))
            self.V(f)
            ve_na = self.counts["eVE"]


            rs = {"slot": 0, "pc": 0, "st": 0}

            def restripe_T(t_src, t_dst, O, K, tag):
                NK = K // P
                for ot in range(O // P):
                    gs = rs["slot"]; rs["slot"] += 1
                    slot = gs % 2
                    gst = rs["st"]; rs["st"] += 1
                    ssl = gst % 2

                    def f(sy, ot=ot, slot=slot, gs=gs):
                        for m_ in mk.get(("rs_ld", gs - 2), []):
                            self.wm(sy, m_)
                        self.dma(sy, "s_rs%d" % slot, p0_ld[:, slot * WMAX:slot * WMAX + K],
                                 t_src.ap()[P * ot:P * ot + P, :])
                    self.SY(f)
                    qwm = ("s_rs%d" % slot, self.counts["s_rs%d" % slot])

                    def f(v, slot=slot, qwm=qwm, gs=gs):
                        self.wm(v, qwm)
                        self.wm2(v, mk.get(("rs_bf", gs - 2), 0))
                        mk[("rs_cast", gs)] = self.inc("eVE", v.tensor_copy(
                            p0_bf[:, slot * WMAX:slot * WMAX + K],
                            p0_ld[:, slot * WMAX:slot * WMAX + K]))
                        mk[("rs_ld", gs)] = [("eVE", mk[("rs_cast", gs)])]
                    self.V(f)
                    ve_m = self.counts["eVE"]

                    for kt in range(NK):
                        pc = rs["pc"]; rs["pc"] += 1
                        tb = ps_c if pc % 2 == 0 else ps_d

                        def f(t, kt=kt, slot=slot, pc=pc, tb=tb, ve_m=ve_m, gs=gs, NK=NK):
                            self.w(t, "eVE", ve_m)
                            self.w(t, "eSC", mk.get(("rs_ev", pc - 2), 0))
                            mm = self.inc("ePE", t.matmul(
                                tb[:, 0:P],
                                p0_bf[:, slot * WMAX + P * kt: slot * WMAX + P * (kt + 1)],
                                self.idnb[:], start=True, stop=True))
                            if kt == NK - 1:
                                mk[("rs_bf", gs)] = ("ePE", mm)
                        self.T(f)
                        pem = self.counts["ePE"]

                        def f(s, kt=kt, pc=pc, tb=tb, pem=pem, ssl=ssl, gst=gst, NK=NK):
                            self.w(s, "ePE", pem)
                            if kt == 0:
                                self.wm(s, mk.get(("rst_out", gst - 2), 0))
                            s.copy(p0_st[:, ssl * 32 * P + kt * P: ssl * 32 * P + (kt + 1) * P],
                                   tb[:, 0:P])
                            mk[("rs_ev", pc)] = self.inc("eSC", s.nop())
                        self.S(f)
                    scm = self.counts["eSC"]

                    def f(g, ot=ot, ssl=ssl, gst=gst, scm=scm, NK=NK):
                        self.w(g, "eSC", scm)
                        mk[("rst_out", gst)] = self.dma(
                            g, "g_rso%d" % ssl, t_dst.ap()[ot],
                            p0_st[:, ssl * 32 * P: ssl * 32 * P + K])
                    self.G(f)

            def restripe_N(t_src, t_dst, K, O, tag, gamma=None, bias_beta=False):
                NK = K // P
                for kt in range(NK):
                    gs = rs["slot"]; rs["slot"] += 1
                    slot = gs % 2

                    def f(sy, kt=kt, slot=slot, gs=gs):
                        for m_ in mk.get(("rs_ld", gs - 2), []):
                            self.wm(sy, m_)
                        self.dma(sy, "s_rs%d" % slot, p0_ld[:, slot * WMAX:slot * WMAX + O],
                                 t_src.ap()[P * kt:P * kt + P, :])
                    self.SY(f)
                    qwm = ("s_rs%d" % slot, self.counts["s_rs%d" % slot])

                    def f(v, kt=kt, slot=slot, qwm=qwm, gs=gs):
                        self.wm(v, qwm)
                        self.wm2(v, mk.get(("rs_bf", gs - 2), 0))
                        if gamma is not None:
                            ii = v.tensor_scalar(p0_bf[:, slot * WMAX:slot * WMAX + O],
                                                 p0_ld[:, slot * WMAX:slot * WMAX + O],
                                                 gamma[:, kt:kt + 1], None, ALU.mult)
                        else:
                            ii = v.tensor_copy(p0_bf[:, slot * WMAX:slot * WMAX + O],
                                               p0_ld[:, slot * WMAX:slot * WMAX + O])
                        mk[("rs_cast", gs)] = self.inc("eVE", ii)
                        mk[("rs_ld", gs)] = [("eVE", mk[("rs_cast", gs)])]
                    self.V(f)

                    if bias_beta:
                        def f(t, kt=kt, slot=slot, qwm=qwm):
                            self.wm(t, qwm)
                            self.w(t, "eVE", mk.get((tag, "bacc", kt - 1), 0))
                            for ot in range(O // P):
                                t.matmul(ps_a[:, ot:ot + 1],
                                         p0_ld[:, slot * WMAX + P * ot: slot * WMAX + P * (ot + 1)],
                                         self.v_beta[:, kt:kt + 1], start=True, stop=True)
                            mk[(tag, "bmm", kt)] = self.inc("ePE", t.nop())
                            mk[("rs_ld", gs)] = mk.get(("rs_ld", gs), []) + [("ePE", mk[(tag, "bmm", kt)])]
                        self.T(f)
                        peb = self.counts["ePE"]

                        def f(v, kt=kt, peb=peb):
                            self.w(v, "ePE", peb)
                            mk[(tag, "bacc", kt)] = self.inc("eVE", v.tensor_tensor(
                                p0_bias[:, 0:O // P], p0_bias[:, 0:O // P],
                                ps_a[:, 0:O // P], ALU.add))
                        self.V(f)

                    def f(g, kt=kt, slot=slot, gs=gs):
                        self.w(g, "eVE", mk[("rs_cast", gs)])
                        dst = bass.AP(t_dst, kt * P,
                                      [[K, P], [P * K, O // P], [1, P]])
                        src = p0_bf[:, slot * WMAX:slot * WMAX + O].rearrange("p (ot o) -> p ot o", o=P)
                        m = self.dma(g, "g_rsn%d" % slot, dst, src)
                        mk[("rs_bf", gs)] = m
                    self.G(f)

            restripe_T(self.t_w1, self.t_w1p, H, D, "w1")
            restripe_T(self.t_w2, self.t_w2p, H, D, "w2")
            restripe_T(self.t_h1, self.t_h1p, H, H, "h1")
            restripe_T(self.t_h2, self.t_h2p, H, H, "h2")
            restripe_T(self.t_w3, self.t_w3p, D, H, "w3")
            restripe_N(self.t_inw, self.t_inwp, D, O3, "inw", gamma=self.v_gamma, bias_beta=True)
            restripe_N(self.t_outw, self.t_outwp, INNER, D, "outw")


            ve_bias = mk[("inw", "bacc", ND - 1)]

            def f(sy):
                self.w(sy, "ePE", mk[("p0v_tp", len(vecs) - 1)])
                self.dma(sy, "s_vec", p0_vec[0:NO3, :],
                         self.t_inb.ap().rearrange("(a b) -> a b", b=P))
            self.SY(f)
            qw_inb = ("s_vec", self.counts["s_vec"])

            def f(t):
                self.wm(t, qw_inb)
                self.w(t, "eVE", ve_bias)
                self.inc("ePE", t.matmul(ps_a[:, 256:256 + NO3], p0_vec[0:NO3, :],
                                            self.idn[0:NO3, 0:NO3], start=True, stop=True))
            self.T(f)
            pe_inb = self.counts["ePE"]

            def f(v):
                self.w(v, "ePE", pe_inb)
                self.inc("eVE", v.tensor_tensor(self.v_inb[:], p0_bias[:, 0:NO3],
                                                ps_a[:, 256:256 + NO3], ALU.add))
            self.V(f)
            ve_vinb = self.counts["eVE"]

            if self.stop_after == "p0":
                def f(g):
                    self.w(g, "eVE", ve_vinb)
                    self.dma(g, "g_dbg", self.t_vinb.ap(), self.v_inb[:])
                self.G(f)


    def kbuild_side_work(self):
        c = self.c
        STATE, TAPS, INNER = c.STATE, c.TAPS, c.INNER
        NI = INNER // P
        mk = self.mk
        works = []

        zmark = [0]

        def mkzero():
            def f(g):
                zmark[0] = self.inc("eGP", g.memset(self.sb_z3[:], 0.0))
            self.G(f)
        works.append(mkzero)

        for ct in range(NI):
            for ng in range(STATE // 4):
                def unit(ct=ct, ng=ng):
                    def f(s, ct=ct, ng=ng):
                        if ng == 0:
                            self.wm(s, mk.get(("kout", ct - 1), 0))
                        pv = (ct, ng - 1) if ng > 0 else (ct - 1, STATE // 4 - 1)
                        self.w(s, "eVE", mk.get(("kacc",) + pv, 0))
                        for j in range(4):
                            n_ = ng * 4 + j
                            s.activation(self.sb_ke[:, j * TAPS:(j + 1) * TAPS], self.iota[:],
                                         AF.Exp, scale=self.sb_na[:, ct * STATE + n_:ct * STATE + n_ + 1])
                        mk[("kexp", ct, ng)] = self.inc("eSC", s.nop())
                    self.S(f)

                    def f(v, ct=ct, ng=ng):
                        self.w(v, "eSC", mk[("kexp", ct, ng)])
                        for j in range(4):
                            n_ = ng * 4 + j
                            e_sl = self.sb_ke[:, j * TAPS:(j + 1) * TAPS]
                            cb_sl = self.sb_cb[:, ct * STATE + n_:ct * STATE + n_ + 1]
                            if n_ == 0:
                                v.tensor_scalar(self.sb_k2[:], e_sl, cb_sl, None, ALU.mult)
                            else:
                                src, dst = (self.sb_k2, self.sb_k) if n_ % 2 == 1 else (self.sb_k, self.sb_k2)
                                v.scalar_tensor_tensor(dst[:], e_sl, cb_sl, src[:], ALU.mult, ALU.add)
                        if ng == STATE // 4 - 1:
                            fin = self.sb_k if STATE % 2 == 0 else self.sb_k2
                            v.tensor_tensor(fin[:, 0:1], fin[:, 0:1], self.v_dp[:, ct:ct + 1], ALU.add)
                        mk[("kacc", ct, ng)] = self.inc("eVE", v.nop())
                    self.V(f)

                    if ng == STATE // 4 - 1:
                        def f(g, ct=ct):
                            fin = self.sb_k if STATE % 2 == 0 else self.sb_k2
                            self.w(g, "eVE", mk[("kacc", ct, STATE // 4 - 1)])
                            self.w(g, "eGP", zmark[0])
                            kst = "g_kpad%d" % (ct % 2)
                            self.dma(g, kst, self.t_kpad.ap()[P * ct:P * ct + P, 0:TAPS - 1],
                                     self.sb_z3[:, 0:TAPS - 1])
                            self.dma(g, kst, self.t_kpad.ap()[P * ct:P * ct + P, 2 * TAPS - 1:],
                                     self.sb_z3[:, 0:TAPS + 1])
                            mk[("kout", ct)] = self.dma(
                                g, kst, self.t_kpad.ap()[P * ct:P * ct + P, TAPS - 1:2 * TAPS - 1],
                                fin[:])
                        self.G(f)
                works.append(unit)

        kp_marks = [("g_kpad0", 0), ("g_kpad1", 0)]

        def wfull_one(j):
            def f(g, j=j):
                if j == 0:
                    kp_marks[0] = ("g_kpad0", self.counts.get("g_kpad0", 0))
                    kp_marks[1] = ("g_kpad1", self.counts.get("g_kpad1", 0))
                    self.wm(g, kp_marks[0])
                    self.wm(g, kp_marks[1])
                src = bass.AP(self.t_kpad, (TAPS - 1 - j), [[3 * TAPS, INNER], [1, 2 * TAPS]])
                dst = bass.AP(self.t_wfull, j * (2 * TAPS), [[P * 2 * TAPS, INNER], [1, 2 * TAPS]])
                self.dma(g, "g_wfull", dst, src)
                if j == P - 1:
                    self.qg_wfull = ("g_wfull", self.counts["g_wfull"])
            self.G(f)
        for j in range(P):
            works.append(lambda j=j: wfull_one(j))
        return works

    def ffn(self, t_in, t_outx, side_work=()):
        side_work = list(side_work)
        c, nc = self.c, self.nc
        D, L, H = c.D, c.L, c.H
        NH, ND = H // P, D // P
        TT = c.TT_FFN
        NT, NS = L // TT, TT // P
        HF = min(512, TT)
        NHF = TT // HF
        mk = {}
        tag = "ffnA" if t_outx is self.t_x1 else "ffnB"

        _es = contextlib.ExitStack()
        b_xs = _es.enter_context(nc.sbuf_tensor(tag + "_xs", [P, 2 * D], F32))
        b_xn = _es.enter_context(nc.sbuf_tensor(tag + "_xn", [P, 2 * D], BF16))
        b_xT = _es.enter_context(nc.sbuf_tensor(tag + "_xT", [P, ND * TT], BF16))
        b_h = _es.enter_context(nc.sbuf_tensor(tag + "_h", [P, NH * TT], BF16))
        b_g = _es.enter_context(nc.sbuf_tensor(tag + "_g", [P, NH * TT], BF16))
        b_w = _es.enter_context(nc.sbuf_tensor(tag + "_w", [P, 3 * H], BF16))
        b_t1 = _es.enter_context(nc.sbuf_tensor(tag + "_t1", [P, HF], F32))
        b_o = _es.enter_context(nc.sbuf_tensor(tag + "_o", [P, 2 * HF], F32))
        b_op = _es.enter_context(nc.sbuf_tensor(tag + "_op", [P, 2 * HF], F32))
        b_st = _es.enter_context(nc.sbuf_tensor(tag + "_st", [P, 64], F32))
        b_sq = _es.enter_context(nc.sbuf_tensor(tag + "_sq", [P, D], F32))
        mm0 = _es.enter_context(nc.psum_tensor(tag + "_p0", [P, 512], F32))
        mm1 = _es.enter_context(nc.psum_tensor(tag + "_p1", [P, 512], F32))
        mm2 = _es.enter_context(nc.psum_tensor(tag + "_p2", [P, 512], F32))
        mm3 = _es.enter_context(nc.psum_tensor(tag + "_p3", [P, 512], F32))
        tpa = _es.enter_context(nc.psum_tensor(tag + "_tpa", [P, 512], F32))
        tpb = _es.enter_context(nc.psum_tensor(tag + "_tpb", [P, 512], F32))
        tfa = _es.enter_context(nc.psum_tensor(tag + "_tfa", [P, 512], F32))
        tfb = _es.enter_context(nc.psum_tensor(tag + "_tfb", [P, 512], F32))
        with _es:
            tp2 = [tpa, tpb]
            tf2 = [tfa, tfb]
            mmps = [mm0, mm1, mm2, mm3]

            def f(g):
                for ch in range(8):
                    r0 = (L // 8) * ch
                    mk["precopy"] = self.dma(g, "g_pre", t_outx.ap()[r0:r0 + L // 8, :],
                                             t_in.ap()[r0:r0 + L // 8, :])
            self.G(f)

            for ti in range(NT):
                for s_ in range(NS):
                    sl = s_ % 2

                    def f(sy, ti=ti, s_=s_, sl=sl):
                        self.w(sy, "eVE", mk.get(("lnv", ti, s_ - 2),
                                                 mk.get(("lnv", ti - 1, NS - 2 + s_), 0)))
                        r0 = ti * TT + s_ * P
                        self.dma(sy, "s_x%d" % sl, b_xs[:, sl * D:sl * D + D],
                                 t_in.ap()[r0:r0 + P, :])
                    self.SY(f)
                    qwm = ("s_x%d" % sl, self.counts["s_x%d" % sl])

                    def f(s, sl=sl, qwm=qwm, ti=ti, s_=s_):
                        self.wm(s, qwm)
                        self.w(s, "eVE", mk.get(("lnr", ti, s_ - 1),
                                                 mk.get(("lnr", ti - 1, NS - 1 + s_), 0)))
                        s.activation(b_sq[:], b_xs[:, sl * D:sl * D + D], AF.Square)
                        mk[("lnsq", ti, s_)] = self.inc("eSC", s.nop())
                    self.S(f)

                    def f(v, sl=sl, qwm=qwm, ti=ti, s_=s_):
                        self.wm(v, qwm)
                        self.w(v, "eSC", mk[("lnsq", ti, s_)])
                        v.reduce_sum(b_st[:, s_ * 8 + 0:s_ * 8 + 1], b_xs[:, sl * D:sl * D + D], axis=AX)
                        mk[("lnr", ti, s_)] = self.inc("eVE", v.reduce_sum(b_st[:, s_ * 8 + 1:s_ * 8 + 2], b_sq[:], axis=AX))
                        v.tensor_scalar(b_st[:, s_ * 8 + 0:s_ * 8 + 1], b_st[:, s_ * 8 + 0:s_ * 8 + 1], 1.0 / D, None, ALU.mult)
                        v.tensor_scalar(b_st[:, s_ * 8 + 1:s_ * 8 + 2], b_st[:, s_ * 8 + 1:s_ * 8 + 2], 1.0 / D, None, ALU.mult)
                        v.tensor_scalar(b_st[:, s_ * 8 + 2:s_ * 8 + 3], b_st[:, s_ * 8 + 0:s_ * 8 + 1], b_st[:, s_ * 8 + 0:s_ * 8 + 1], None, ALU.mult)
                        self.inc("eVE", v.tensor_sub(b_st[:, s_ * 8 + 3:s_ * 8 + 4], b_st[:, s_ * 8 + 1:s_ * 8 + 2], b_st[:, s_ * 8 + 2:s_ * 8 + 3]))
                    self.V(f)
                    vevar = self.counts["eVE"]

                    def f(s, vevar=vevar):
                        self.w(s, "eVE", vevar)
                        self.inc("eSC", s.activation(b_st[:, s_ * 8 + 4:s_ * 8 + 5], b_st[:, s_ * 8 + 3:s_ * 8 + 4], AF.Sqrt,
                                                     bias=self.v_eps[:, 0:1]))
                    self.S(f)
                    scsd = self.counts["eSC"]

                    def f(v, sl=sl, scsd=scsd, ti=ti, s_=s_):
                        self.w(v, "eSC", scsd)
                        self.w(v, "ePE", mk.get(("xntp", ti, s_ - 2),
                                                mk.get(("xntp", ti - 1, NS - 2 + s_), 0)))
                        v.reciprocal(b_st[:, s_ * 8 + 5:s_ * 8 + 6], b_st[:, s_ * 8 + 4:s_ * 8 + 5])
                        mk[("lnv", ti, s_)] = self.inc("eVE", v.tensor_scalar(b_xn[:, sl * D:sl * D + D], b_xs[:, sl * D:sl * D + D],
                                        b_st[:, s_ * 8 + 0:s_ * 8 + 1], b_st[:, s_ * 8 + 5:s_ * 8 + 6], ALU.subtract, ALU.mult))
                    self.V(f)

                    for kt in range(ND):
                        pp = (s_ * ND + kt) % 2
                        tb = tp2[pp]

                        def f(t, ti=ti, s_=s_, sl=sl, kt=kt, tb=tb):
                            self.w(t, "eVE", mk[("lnv", ti, s_)])
                            self.w(t, "eSC", mk.get(("xTev", ti, s_ * ND + kt - 2), 0))
                            mm = self.inc("ePE", t.matmul(
                                tb[:, 0:P],
                                b_xn[:, sl * D + kt * P: sl * D + (kt + 1) * P], self.idnb[:], start=True, stop=True))
                            if kt == ND - 1:
                                mk[("xntp", ti, s_)] = mm
                        self.T(f)
                        pem = self.counts["ePE"]

                        def f(s, ti=ti, s_=s_, kt=kt, tb=tb, pem=pem):
                            self.w(s, "ePE", pem)
                            s.copy(b_xT[:, kt * TT + s_ * P: kt * TT + (s_ + 1) * P],
                                   tb[:, 0:P])
                            mk[("xTev", ti, s_ * ND + kt)] = self.inc("eSC", s.nop())
                        self.S(f)
                xT_ready = self.counts["eSC"]

                for ot in range(NH):
                    wsl = ot % 2
                    assert H >= 2 * D

                    def f(sy, ti=ti, ot=ot, wsl=wsl):
                        self.w(sy, "ePE", mk.get(("c1mm", ti, ot - 2), mk.get(("done", ti - 1), 0)))
                        self.dma(sy, "s_w%d" % wsl, b_w[:, wsl * H: wsl * H + D],
                                 self.t_w1p.ap()[ot])
                        self.dma(sy, "s_w%d" % wsl, b_w[:, wsl * H + D: wsl * H + 2 * D],
                                 self.t_w2p.ap()[ot])
                    self.SY(f)
                    qwm = ("s_w%d" % wsl, self.counts["s_w%d" % wsl])

                    for hf in range(NHF):
                        sidx = ot * NHF + hf
                        pa, pb = mmps[(2 * sidx) % 4], mmps[(2 * sidx + 1) % 4]

                        def f(t, ti=ti, ot=ot, hf=hf, wsl=wsl, qwm=qwm, pa=pa, pb=pb, sidx=sidx):
                            self.wm(t, qwm)
                            if ot == 0 and hf == 0:
                                self.w(t, "eSC", xT_ready)
                            self.w(t, "eVE", mk.get(("c1h", ti, sidx - 2),
                                                     mk.get(("c1h", ti - 1, NH * NHF - 2 + sidx), 0)))
                            for kt in range(ND):
                                rhs = b_xT[:, kt * TT + hf * HF: kt * TT + hf * HF + HF]
                                t.matmul(pa[:, 0:HF], b_w[:, wsl * H + kt * P: wsl * H + (kt + 1) * P],
                                         rhs, start=(kt == 0), stop=(kt == ND - 1))
                            for kt in range(ND):
                                rhs = b_xT[:, kt * TT + hf * HF: kt * TT + hf * HF + HF]
                                t.matmul(pb[:, 0:HF], b_w[:, wsl * H + D + kt * P: wsl * H + D + (kt + 1) * P],
                                         rhs, start=(kt == 0), stop=(kt == ND - 1))
                            mm = self.inc("ePE", t.nop())
                            if hf == NHF - 1:
                                mk[("c1mm", ti, ot)] = mm
                        self.T(f)
                        pem = self.counts["ePE"]

                        def f(s, ot=ot, hf=hf, pem=pem, pa=pa, ti=ti, sidx=sidx):
                            self.w(s, "ePE", pem)
                            self.w(s, "eVE", mk.get(("c1h", ti, sidx - 1),
                                                    mk.get(("c1h", ti - 1, NH * NHF - 1 + sidx), 0)))
                            s.activation(b_t1[:, 0:HF], pa[:, 0:HF],
                                         AF.Silu, scale=self.v_w1s[:, ot:ot + 1])
                            mk[("c1silu", ti, sidx)] = self.inc("eSC", s.nop())
                        self.S(f)
                        scm = self.counts["eSC"]

                        def f(v, ot=ot, hf=hf, scm=scm, pb=pb, ti=ti, sidx=sidx):
                            self.w(v, "eSC", scm)
                            mk[("c1h", ti, sidx)] = self.inc("eVE", v.scalar_tensor_tensor(
                                b_h[:, ot * TT + hf * HF: ot * TT + hf * HF + HF],
                                pb[:, 0:HF], self.v_w2s[:, ot:ot + 1],
                                b_t1[:, 0:HF], ALU.mult, ALU.mult))
                        self.V(f)
                h_ready = self.counts["eVE"]

                prev_done = ("c1mm", NH - 1)
                for stg, wp, vs, src, dst in [("g1", self.t_h1p, self.v_h1s, b_h, b_g),
                                              ("g2", self.t_h2p, self.v_h2s, b_g, b_h)]:
                    for ot in range(NH):
                        wsl = ot % 3

                        def f(sy, ti=ti, ot=ot, wsl=wsl, wp=wp, stg=stg, prev_done=prev_done):
                            if ot < 3:
                                self.w(sy, "ePE", mk.get((prev_done[0], ti, prev_done[1]), 0))
                            self.w(sy, "ePE", mk.get((stg + "mm", ti, ot - 3), 0))
                            self.dma(sy, "s_w%d" % wsl, b_w[:, wsl * H: wsl * H + H],
                                     wp.ap()[ot])
                        self.SY(f)
                        qwm = ("s_w%d" % wsl, self.counts["s_w%d" % wsl])

                        for hf in range(NHF):
                            sidx = ot * NHF + hf
                            pa = mmps[sidx % 4]

                            def f(t, ti=ti, ot=ot, hf=hf, wsl=wsl, qwm=qwm, pa=pa, stg=stg,
                                  src=src, sidx=sidx):
                                self.wm(t, qwm)
                                if ot == 0 and hf == 0 and stg == "g1":
                                    self.w(t, "eVE", h_ready)
                                if ot == 0 and hf == 0 and stg == "g2":
                                    self.w(t, "eSC", mk.get(("g1ev", ti, NH * NHF - 1), 0))
                                self.w(t, "eSC", mk.get((stg + "ev", ti, sidx - 4), 0))
                                for kt in range(NH):
                                    rhs = src[:, kt * TT + hf * HF: kt * TT + hf * HF + HF]
                                    t.matmul(pa[:, 0:HF], b_w[:, wsl * H + kt * P: wsl * H + (kt + 1) * P],
                                             rhs, start=(kt == 0), stop=(kt == NH - 1))
                                mm = self.inc("ePE", t.nop())
                                if hf == NHF - 1:
                                    mk[(stg + "mm", ti, ot)] = mm
                            self.T(f)
                            pem = self.counts["ePE"]

                            def f(s, ot=ot, hf=hf, pem=pem, pa=pa, vs=vs, dst=dst, stg=stg,
                                  ti=ti, sidx=sidx):
                                self.w(s, "ePE", pem)
                                s.activation(dst[:, ot * TT + hf * HF: ot * TT + hf * HF + HF],
                                             pa[:, 0:HF], AF.Gelu, scale=vs[:, ot:ot + 1])
                                mk[(stg + "ev", ti, sidx)] = self.inc("eSC", s.nop())
                            self.S(f)
                        if side_work:
                            side_work.pop(0)()
                    prev_done = (stg + "mm", NH - 1)

                for dt_ in range(ND):
                    wsl = dt_ % 3

                    def f(sy, ti=ti, dt_=dt_, wsl=wsl):
                        if dt_ < 3:
                            self.w(sy, "ePE", mk.get(("g2mm", ti, NH - 1), 0))
                        self.w(sy, "ePE", mk.get(("w3mm", ti, dt_ - 3), 0))
                        self.dma(sy, "s_w%d" % wsl, b_w[:, wsl * H: wsl * H + H],
                                 self.t_w3p.ap()[dt_])
                    self.SY(f)
                    qwm = ("s_w%d" % wsl, self.counts["s_w%d" % wsl])

                    for hf in range(NHF):
                        sidx = dt_ * NHF + hf
                        pa = mmps[sidx % 4]

                        def f(t, ti=ti, dt_=dt_, hf=hf, wsl=wsl, qwm=qwm, pa=pa, sidx=sidx):
                            self.wm(t, qwm)
                            if dt_ == 0 and hf == 0:
                                self.w(t, "eSC", mk.get(("g2ev", ti, NH * NHF - 1), 0))
                            self.w(t, "eSC", mk.get(("w3ev", ti, sidx - 4), 0))
                            for kt in range(NH):
                                rhs = b_h[:, kt * TT + hf * HF: kt * TT + hf * HF + HF]
                                t.matmul(pa[:, 0:HF], b_w[:, wsl * H + kt * P: wsl * H + (kt + 1) * P],
                                         rhs, start=(kt == 0), stop=(kt == NH - 1))
                            mm = self.inc("ePE", t.nop())
                            if hf == NHF - 1:
                                mk[("w3mm", ti, dt_)] = mm
                        self.T(f)
                        pem = self.counts["ePE"]

                        def f(s, dt_=dt_, hf=hf, pem=pem, pa=pa, ti=ti, sidx=sidx):
                            self.w(s, "ePE", pem)
                            self.w(s, "ePE", mk.get(("otp", ti, sidx - 2), 0))
                            s.activation(b_o[:, (sidx % 2) * HF:(sidx % 2) * HF + HF], pa[:, 0:HF],
                                         AF.Identity, scale=self.v_w3s[:, dt_:dt_ + 1])
                            mk[("w3ev", ti, sidx)] = self.inc("eSC", s.nop())
                        self.S(f)
                        scm = self.counts["eSC"]

                        for pj in range(HF // P):
                            pidx = sidx * (HF // P) + pj
                            tfb_ = tf2[pidx % 2]

                            def f(t, dt_=dt_, pj=pj, tfb_=tfb_, scm=scm, ti=ti, sidx=sidx, pidx=pidx):
                                self.w(t, "eSC", scm)
                                self.w(t, "eSC", mk.get(("opev", ti, pidx - 2), 0))
                                mm = self.inc("ePE", t.matmul(
                                    tfb_[:, 0:P],
                                    b_o[:, (sidx % 2) * HF + pj * P:(sidx % 2) * HF + (pj + 1) * P],
                                    self.idn[:], start=True, stop=True))
                                if pj == HF // P - 1:
                                    mk[("otp", ti, sidx)] = mm
                            self.T(f)
                            pet = self.counts["ePE"]

                            def f(s, pj=pj, tfb_=tfb_, pet=pet, ti=ti, sidx=sidx, pidx=pidx):
                                self.w(s, "ePE", pet)
                                self.wm(s, mk.get(("oacc", ti, sidx - 2),
                                                  mk.get(("oacc", ti - 1, ND * NHF - 2 + sidx), 0)))
                                s.copy(b_op[:, (sidx % 2) * HF + pj * P:(sidx % 2) * HF + (pj + 1) * P],
                                       tfb_[:, 0:P])
                                mk[("opev", ti, pidx)] = self.inc("eSC", s.nop())
                            self.S(f)
                        scm2 = self.counts["eSC"]

                        def f(g, ti=ti, dt_=dt_, hf=hf, scm2=scm2, sidx=sidx):
                            self.w(g, "eSC", scm2)
                            if ti == 0 and dt_ == 0 and hf == 0:
                                self.wm(g, mk["precopy"])
                            r0 = ti * TT + hf * HF
                            src = b_op[:, (sidx % 2) * HF:(sidx % 2) * HF + HF] \
                                .rearrange("p (pj q) -> p pj q", q=P)
                            dst3 = bass.AP(t_outx, r0 * D + dt_ * P,
                                           [[D, P], [P * D, HF // P], [1, P]])
                            mk[("oacc", ti, sidx)] = self.dma(
                                g, "g_oac%d" % (sidx % 2), dst3, src, accum_op=ALU.add)
                        self.G(f)
                mk[("done", ti)] = self.counts["ePE"]
            while side_work:
                side_work.pop(0)()


    def mixer_a1(self):
        c, nc = self.c, self.nc
        D, L, O3, INNER = c.D, c.L, c.O3, c.INNER
        ND, NO3, NI = D // P, O3 // P, INNER // P
        TT = 512
        NT, NS = L // TT, TT // P
        mk = {}

        _es = contextlib.ExitStack()
        b_xs = _es.enter_context(nc.sbuf_tensor("a1_xs", [P, 2 * D], F32))
        b_x1n = _es.enter_context(nc.sbuf_tensor("a1_x1n", [P, 2 * D], F32))
        b_r2 = _es.enter_context(nc.sbuf_tensor("a1_r2", [P, 2 * D], F32))
        b_xn = _es.enter_context(nc.sbuf_tensor("a1_xn", [P, 2 * D], BF16))
        b_xT = _es.enter_context(nc.sbuf_tensor("a1_xT", [P, 2 * ND * TT], BF16))
        b_w = _es.enter_context(nc.sbuf_tensor("a1_w", [P, NO3 * D], BF16))
        b_ev = _es.enter_context(nc.sbuf_tensor("a1_ev", [P, 2 * TT], F32))
        b_evb = _es.enter_context(nc.sbuf_tensor("a1_evb", [P, 2 * TT], BF16))
        b_st = _es.enter_context(nc.sbuf_tensor("a1_st", [P, 64], F32))
        b_sq = _es.enter_context(nc.sbuf_tensor("a1_sq", [P, D], F32))
        mm0 = _es.enter_context(nc.psum_tensor("a1_p0", [P, 512], F32))
        mm1 = _es.enter_context(nc.psum_tensor("a1_p1", [P, 512], F32))
        mm2 = _es.enter_context(nc.psum_tensor("a1_p2", [P, 512], F32))
        mm3 = _es.enter_context(nc.psum_tensor("a1_p3", [P, 512], F32))
        tpa = _es.enter_context(nc.psum_tensor("a1_tpa", [P, 512], F32))
        tpb = _es.enter_context(nc.psum_tensor("a1_tpb", [P, 512], F32))
        with _es:
            tp2 = [tpa, tpb]
            mmps = [mm0, mm1, mm2, mm3]

            inw_marks = []

            def f(sy):
                CH = NO3 // 4
                for g_ in range(4):
                    dst = b_w[:, g_ * CH * D:(g_ + 1) * CH * D].rearrange("p (a k) -> p a k", k=D)
                    src = self.t_inwp.ap()[g_ * CH:(g_ + 1) * CH].transpose([1, 0, 2])
                    inw_marks.append(self.dma(sy, "s_iw%d" % g_, dst, src))
            self.SY(f)

            def emit_LN(ti):
                tsl = ti % 2
                for s_ in range(NS):
                    sl = s_ % 2

                    def f(sy, ti=ti, s_=s_, sl=sl):
                        self.w(sy, "eVE", mk.get(("ln2v", ti, s_ - 2),
                                                 mk.get(("ln2v", ti - 1, NS - 2 + s_), 0)))
                        r0 = ti * TT + s_ * P
                        self.dma(sy, "s_x%d" % sl, b_xs[:, sl * D:sl * D + D],
                                 self.t_x1.ap()[r0:r0 + P, :])
                    self.SY(f)
                    qwm = ("s_x%d" % sl, self.counts["s_x%d" % sl])

                    def f(v, sl=sl, qwm=qwm, ti=ti, s_=s_):
                        self.wm(v, qwm)
                        self.w(v, "eVE", mk.get(("ln2r", ti, s_ - 1),
                                                mk.get(("ln2r", ti - 1, NS - 1 + s_), 0)))
                        v.tensor_mul(b_sq[:], b_xs[:, sl * D:sl * D + D], b_xs[:, sl * D:sl * D + D])
                        v.reduce_sum(b_st[:, s_ * 8 + 0:s_ * 8 + 1], b_xs[:, sl * D:sl * D + D], axis=AX)
                        mk[("ln1r", ti, s_)] = self.inc("eVE", v.reduce_sum(b_st[:, s_ * 8 + 1:s_ * 8 + 2], b_sq[:], axis=AX))
                        v.tensor_scalar(b_st[:, s_ * 8 + 0:s_ * 8 + 1], b_st[:, s_ * 8 + 0:s_ * 8 + 1], 1.0 / D, None, ALU.mult)
                        v.tensor_scalar(b_st[:, s_ * 8 + 1:s_ * 8 + 2], b_st[:, s_ * 8 + 1:s_ * 8 + 2], 1.0 / D, None, ALU.mult)
                        v.tensor_scalar(b_st[:, s_ * 8 + 2:s_ * 8 + 3], b_st[:, s_ * 8 + 0:s_ * 8 + 1], b_st[:, s_ * 8 + 0:s_ * 8 + 1], None, ALU.mult)
                        self.inc("eVE", v.tensor_sub(b_st[:, s_ * 8 + 3:s_ * 8 + 4], b_st[:, s_ * 8 + 1:s_ * 8 + 2], b_st[:, s_ * 8 + 2:s_ * 8 + 3]))
                    self.V(f)
                    vevar = self.counts["eVE"]

                    def f(s, vevar=vevar, s_=s_):
                        self.w(s, "eVE", vevar)
                        self.inc("eSC", s.activation(b_st[:, s_ * 8 + 4:s_ * 8 + 5], b_st[:, s_ * 8 + 3:s_ * 8 + 4], AF.Sqrt,
                                                     bias=self.v_eps[:, 0:1]))
                    self.S(f)
                    scsd = self.counts["eSC"]

                    def f(v, sl=sl, scsd=scsd, ti=ti, s_=s_):
                        self.w(v, "eSC", scsd)
                        self.wm(v, mk.get(("r2o", ti, s_ - 2),
                                          mk.get(("r2o", ti - 1, NS - 2 + s_), 0)))
                        self.w(v, "eVE", mk.get(("ln2v", ti, s_ - 2),
                                                mk.get(("ln2v", ti - 1, NS - 2 + s_), 0)))
                        v.reciprocal(b_st[:, s_ * 8 + 5:s_ * 8 + 6], b_st[:, s_ * 8 + 4:s_ * 8 + 5])
                        v.tensor_scalar(b_x1n[:, sl * D:sl * D + D], b_xs[:, sl * D:sl * D + D],
                                        b_st[:, s_ * 8 + 0:s_ * 8 + 1], b_st[:, s_ * 8 + 5:s_ * 8 + 6], ALU.subtract, ALU.mult)
                        mk[("ln1v", ti, s_)] = self.inc("eVE", v.tensor_add(b_r2[:, sl * D:sl * D + D], b_xs[:, sl * D:sl * D + D],
                                     b_x1n[:, sl * D:sl * D + D]))
                    self.V(f)
                    ve1 = self.counts["eVE"]

                    def f(g, ti=ti, s_=s_, sl=sl, ve1=ve1):
                        self.w(g, "eVE", ve1)
                        r0 = ti * TT + s_ * P
                        mk[("r2o", ti, s_)] = self.dma(
                            g, "g_r2%d" % (s_ % 2), self.t_r2.ap()[r0:r0 + P, :],
                            b_r2[:, sl * D:sl * D + D])
                    self.G(f)

                    def f(v, sl=sl, ti=ti, s_=s_):
                        v.tensor_mul(b_sq[:], b_x1n[:, sl * D:sl * D + D], b_x1n[:, sl * D:sl * D + D])
                        v.reduce_sum(b_st[:, s_ * 8 + 0:s_ * 8 + 1], b_x1n[:, sl * D:sl * D + D], axis=AX)
                        mk[("ln2r", ti, s_)] = self.inc("eVE", v.reduce_sum(b_st[:, s_ * 8 + 1:s_ * 8 + 2], b_sq[:], axis=AX))
                        v.tensor_scalar(b_st[:, s_ * 8 + 0:s_ * 8 + 1], b_st[:, s_ * 8 + 0:s_ * 8 + 1], 1.0 / D, None, ALU.mult)
                        v.tensor_scalar(b_st[:, s_ * 8 + 1:s_ * 8 + 2], b_st[:, s_ * 8 + 1:s_ * 8 + 2], 1.0 / D, None, ALU.mult)
                        v.tensor_scalar(b_st[:, s_ * 8 + 2:s_ * 8 + 3], b_st[:, s_ * 8 + 0:s_ * 8 + 1], b_st[:, s_ * 8 + 0:s_ * 8 + 1], None, ALU.mult)
                        self.inc("eVE", v.tensor_sub(b_st[:, s_ * 8 + 3:s_ * 8 + 4], b_st[:, s_ * 8 + 1:s_ * 8 + 2], b_st[:, s_ * 8 + 2:s_ * 8 + 3]))
                    self.V(f)
                    vevar2 = self.counts["eVE"]

                    def f(s, vevar2=vevar2, s_=s_):
                        self.w(s, "eVE", vevar2)
                        self.inc("eSC", s.activation(b_st[:, s_ * 8 + 4:s_ * 8 + 5], b_st[:, s_ * 8 + 3:s_ * 8 + 4], AF.Sqrt,
                                                     bias=self.v_eps[:, 0:1]))
                    self.S(f)
                    scsd2 = self.counts["eSC"]

                    def f(v, sl=sl, scsd2=scsd2, ti=ti, s_=s_):
                        self.w(v, "eSC", scsd2)
                        self.w(v, "ePE", mk.get(("xntp", ti, s_ - 2),
                                                mk.get(("xntp", ti - 1, NS - 2 + s_), 0)))
                        v.reciprocal(b_st[:, s_ * 8 + 5:s_ * 8 + 6], b_st[:, s_ * 8 + 4:s_ * 8 + 5])
                        mk[("ln2v", ti, s_)] = self.inc("eVE", v.tensor_scalar(b_xn[:, sl * D:sl * D + D], b_x1n[:, sl * D:sl * D + D],
                                        b_st[:, s_ * 8 + 0:s_ * 8 + 1], b_st[:, s_ * 8 + 5:s_ * 8 + 6], ALU.subtract, ALU.mult))
                    self.V(f)

                    for kt in range(ND):
                        pp = (s_ * ND + kt) % 2
                        tb = tp2[pp]

                        def f(t, ti=ti, s_=s_, sl=sl, kt=kt, tb=tb):
                            self.w(t, "eVE", mk[("ln2v", ti, s_)])
                            idx = s_ * ND + kt
                            self.w(t, "eSC", mk.get(("xTev", ti, idx - 2),
                                                    mk.get(("xTev", ti - 1, NS * ND - 2 + idx), 0)))
                            mm = self.inc("ePE", t.matmul(
                                tb[:, 0:P],
                                b_xn[:, sl * D + kt * P: sl * D + (kt + 1) * P], self.idnb[:], start=True, stop=True))
                            if kt == ND - 1:
                                mk[("xntp", ti, s_)] = mm
                        self.T(f)
                        pem = self.counts["ePE"]

                        def f(s, ti=ti, s_=s_, kt=kt, tb=tb, pem=pem, tsl=tsl):
                            self.w(s, "ePE", pem)
                            s.copy(b_xT[:, tsl * ND * TT + kt * TT + s_ * P: tsl * ND * TT + kt * TT + (s_ + 1) * P],
                                   tb[:, 0:P])
                            mk[("xTev", ti, s_ * ND + kt)] = self.inc("eSC", s.nop())
                        self.S(f)
                mk[("xT_ready", ti)] = self.counts["eSC"]

            def emit_IP(ti):
                tsl = ti % 2
                for ot in range(NO3):
                    pa = mmps[ot % 4]

                    def f(t, ti=ti, ot=ot, pa=pa, tsl=tsl):
                        if ti == 0 and ot % (NO3 // 4) == 0:
                            self.wm(t, inw_marks[ot // (NO3 // 4)])
                        if ot == 0:
                            self.w(t, "eSC", mk[("xT_ready", ti)])
                        self.w(t, "eSC", mk.get(("ipev", ti, ot - 4),
                                                mk.get(("ipev", ti - 1, NO3 - 4 + ot), 0)))
                        for kt in range(ND):
                            t.matmul(pa[:, 0:TT], b_w[:, ot * D + kt * P: ot * D + (kt + 1) * P],
                                     b_xT[:, tsl * ND * TT + kt * TT: tsl * ND * TT + (kt + 1) * TT],
                                     start=(kt == 0), stop=(kt == ND - 1))
                        mk[("ipmm", ti, ot)] = self.inc("ePE", t.nop())
                    self.T(f)
                    pem = self.counts["ePE"]

                    def f(s, ti=ti, ot=ot, pa=pa, pem=pem):
                        self.w(s, "ePE", pem)
                        self.wm(s, mk.get(("ipo", ti, ot - 2),
                                          mk.get(("ipo", ti - 1, NO3 - 2 + ot), 0)))
                        bias = self.v_inb[:, ot:ot + 1]
                        if ot < NI:
                            s.activation(b_ev[:, (ot % 2) * TT:(ot % 2) * TT + TT], pa[:, 0:TT],
                                         AF.Identity, bias=bias)
                        elif ot < 2 * NI:
                            s.activation(b_evb[:, (ot % 2) * TT:(ot % 2) * TT + TT], pa[:, 0:TT],
                                         AF.Silu, bias=bias)
                        else:
                            s.activation(b_evb[:, (ot % 2) * TT:(ot % 2) * TT + TT], pa[:, 0:TT],
                                         AF.Sigmoid, bias=bias)
                        mk[("ipev", ti, ot)] = self.inc("eSC", s.nop())
                    self.S(f)
                    scm = self.counts["eSC"]

                    def f(g, ti=ti, ot=ot, scm=scm):
                        self.w(g, "eSC", scm)
                        t0 = ti * TT
                        if ot < NI:
                            dst = self.t_u.ap()[P * ot:P * (ot + 1), t0:t0 + TT]
                            srcb = b_ev[:, (ot % 2) * TT:(ot % 2) * TT + TT]
                        elif ot < 2 * NI:
                            r = ot - NI
                            dst = self.t_sv.ap()[P * r:P * (r + 1), t0:t0 + TT]
                            srcb = b_evb[:, (ot % 2) * TT:(ot % 2) * TT + TT]
                        else:
                            r = ot - 2 * NI
                            dst = self.t_sg.ap()[P * r:P * (r + 1), t0:t0 + TT]
                            srcb = b_evb[:, (ot % 2) * TT:(ot % 2) * TT + TT]
                        mk[("ipo", ti, ot)] = self.dma(g, "g_ip%d" % (ot % 2), dst, srcb)
                    self.G(f)

            emit_LN(0)
            for ti in range(NT):
                if ti + 1 < NT:
                    emit_LN(ti + 1)
                emit_IP(ti)

    def mixer_a2(self):
        c, nc = self.c, self.nc
        L, INNER = c.L, c.INNER
        NI = INNER // P
        CT = 512
        NTT = L // CT
        NPJ = CT // P
        mk = {}

        _es = contextlib.ExitStack()
        b_u = _es.enter_context(nc.sbuf_tensor("a2_u", [P, 2 * (CT + 2)], F32))
        b_c = _es.enter_context(nc.sbuf_tensor("a2_c", [P, 2 * CT], F32))
        b_s = _es.enter_context(nc.sbuf_tensor("a2_s", [P, 2 * CT], F32))
        b_ev = _es.enter_context(nc.sbuf_tensor("a2_ev", [P, 4 * P], F32))
        b_uTB = _es.enter_context(nc.sbuf_tensor("a2_uTB", [P, 2 * L], BF16))
        tpa = _es.enter_context(nc.psum_tensor("a2_tpa", [P, 512], F32))
        tpb = _es.enter_context(nc.psum_tensor("a2_tpb", [P, 512], F32))
        with _es:
            tp2 = [tpa, tpb]
            for ct in range(NI):
                for tt in range(NTT):
                    sl = tt % 2
                    UW = CT + 2

                    def f(sy, ct=ct, tt=tt, sl=sl):
                        self.w(sy, "eVE", mk.get(("cv", ct, tt - 2), mk.get(("cv", ct - 1, NTT - 2 + tt), 0)))
                        t0 = tt * CT
                        lo = max(t0 - 1, 0)
                        hi = min(t0 + CT + 1, L)
                        off = 0 if tt > 0 else 1
                        self.dma(sy, "s_u%d" % sl,
                                 b_u[:, sl * UW + off: sl * UW + off + (hi - lo)],
                                 self.t_u.ap()[P * ct:P * (ct + 1), lo:hi])
                    self.SY(f)
                    qwm = ("s_u%d" % sl, self.counts["s_u%d" % sl])

                    if tt == 0 or tt == NTT - 1:
                        def f(g, ct=ct, tt=tt, sl=sl, qwm=qwm):
                            self.wm(g, qwm)
                            if tt == 0:
                                g.memset(b_u[:, sl * UW: sl * UW + 1], 0.0)
                            if tt == NTT - 1:
                                g.memset(b_u[:, sl * UW + CT + 1: sl * UW + CT + 2], 0.0)
                            mk[("hz", ct, tt)] = self.inc("eGP", g.nop())
                        self.G(f)

                    def f(v, ct=ct, tt=tt, sl=sl, qwm=qwm):
                        self.wm(v, qwm)
                        if ("hz", ct, tt) in mk:
                            self.w(v, "eGP", mk[("hz", ct, tt)])
                        self.w(v, "eSC", mk.get(("si", ct, tt - 2), mk.get(("si", ct - 1, NTT - 2 + tt), 0)))
                        k0 = self.v_convw[:, 3 * ct + 0:3 * ct + 1]
                        k1 = self.v_convw[:, 3 * ct + 1:3 * ct + 2]
                        k2 = self.v_convw[:, 3 * ct + 2:3 * ct + 3]
                        v.tensor_scalar(b_c[:, sl * CT: sl * CT + CT], b_u[:, sl * UW + 1: sl * UW + 1 + CT],
                                        k1, None, ALU.mult)
                        v.scalar_tensor_tensor(b_c[:, sl * CT: sl * CT + CT], b_u[:, sl * UW: sl * UW + CT],
                                               k0, b_c[:, sl * CT: sl * CT + CT], ALU.mult, ALU.add)
                        mk[("cv", ct, tt)] = self.inc("eVE", v.scalar_tensor_tensor(b_c[:, sl * CT: sl * CT + CT], b_u[:, sl * UW + 2: sl * UW + 2 + CT],
                                               k2, b_c[:, sl * CT: sl * CT + CT], ALU.mult, ALU.add))
                    self.V(f)
                    vem = self.counts["eVE"]

                    def f(s, ct=ct, tt=tt, sl=sl, vem=vem):
                        self.w(s, "eVE", vem)
                        self.w(s, "ePE", mk.get(("tp", ct, tt - 2), mk.get(("tp", ct - 1, NTT - 2 + tt), 0)))
                        s.activation(b_s[:, sl * CT: sl * CT + CT], b_c[:, sl * CT: sl * CT + CT],
                                     AF.Silu, bias=self.v_convb[:, ct:ct + 1])
                        mk[("si", ct, tt)] = self.inc("eSC", s.nop())
                    self.S(f)
                    scm = self.counts["eSC"]

                    for pj in range(NPJ):
                        gidx = (ct * NTT + tt) * NPJ + pj
                        psl = gidx % 4
                        tb = tp2[gidx % 2]

                        def f(t, ct=ct, tt=tt, sl=sl, pj=pj, tb=tb, scm=scm, gidx=gidx):
                            self.w(t, "eSC", scm)
                            self.w(t, "eSC", mk.get(("tev", gidx - 2), 0))
                            mm = self.inc("ePE", t.matmul(
                                tb[:, 0:P],
                                b_s[:, sl * CT + pj * P: sl * CT + (pj + 1) * P], self.idn[:], start=True, stop=True))
                            if pj == NPJ - 1:
                                mk[("tp", ct, tt)] = mm
                        self.T(f)
                        pem = self.counts["ePE"]

                        def f(s, ct=ct, tt=tt, pj=pj, tb=tb, pem=pem, gidx=gidx):
                            self.w(s, "ePE", pem)
                            if tt == 0 and pj == 0:
                                self.wm(s, mk.get(("uto", ct - 2), 0))
                            i_ = tt * NPJ + pj
                            s.copy(b_uTB[:, (ct % 2) * L + i_ * P:(ct % 2) * L + (i_ + 1) * P],
                                   tb[:, 0:P])
                            mk[("tev", gidx)] = self.inc("eSC", s.nop())
                        self.S(f)

                if True:
                    sct = self.counts["eSC"]

                    def f(g, ct=ct, sct=sct):
                        self.w(g, "eSC", sct)
                        mk[("uto", ct)] = self.dma(
                            g, "g_uT%d" % (ct % 2), self.t_uT.ap()[ct],
                            b_uTB[:, (ct % 2) * L:(ct % 2) * L + L])
                    self.G(f)

    def scan(self):
        c, nc = self.c, self.nc
        L, INNER, TAPS = c.L, c.INNER, c.TAPS
        NI, NCH = INNER // P, c.NCH
        mk = {}
        GB = 8
        NG = P // 8
        NSL = 4
        BW = (NCH + 1) * P

        _es = contextlib.ExitStack()
        b_B = _es.enter_context(nc.sbuf_tensor("sc_B", [P, 2 * BW], BF16))
        b_w = _es.enter_context(nc.sbuf_tensor("sc_w", [P, NSL * GB * 2 * TAPS], BF16))
        b_S = _es.enter_context(nc.sbuf_tensor("sc_S", [P, 2 * NCH * P], BF16))
        b_yt = _es.enter_context(nc.sbuf_tensor("sc_yt", [P, 2 * P], BF16))
        b_sv = _es.enter_context(nc.sbuf_tensor("sc_sv", [P, 2 * L], BF16))
        b_sg = _es.enter_context(nc.sbuf_tensor("sc_sg", [P, 2 * L], BF16))
        b_z = _es.enter_context(nc.sbuf_tensor("sc_z", [P, L], BF16))
        ps0 = _es.enter_context(nc.psum_tensor("sc_p0", [P, 512], F32))
        ps1 = _es.enter_context(nc.psum_tensor("sc_p1", [P, 512], F32))
        ps2 = _es.enter_context(nc.psum_tensor("sc_p2", [P, 512], F32))
        ps3 = _es.enter_context(nc.psum_tensor("sc_p3", [P, 512], F32))
        tpa = _es.enter_context(nc.psum_tensor("sc_tpa", [P, 512], F32))
        tpb = _es.enter_context(nc.psum_tensor("sc_tpb", [P, 512], F32))
        with _es:
            tp2 = [tpa, tpb]
            ymm = [ps0, ps1, ps2, ps3]

            def f(g):
                self.inc("eGP", g.memset(b_B[:, 0:P], 0.0))
                self.inc("eGP", g.memset(b_B[:, BW:BW + P], 0.0))
            self.G(f)
            gpz = self.counts["eGP"]

            def emit_wf_load(g_, cb):
                def f(sy, cb=cb, g_=g_):
                    self.w(sy, "ePE", mk.get(("wcons", cb, g_ - NSL),
                                             mk.get(("wcons", cb - 1, NG - NSL + g_), 0)))
                    if cb == 0 and g_ < NSL:
                        self.wm(sy, self.qg_wfull)
                    src = self.t_wfull.ap()[cb * P + g_ * GB: cb * P + (g_ + 1) * GB] \
                        .transpose([1, 0, 2])
                    dst = b_w[:, (g_ % NSL) * GB * 2 * TAPS:((g_ % NSL) + 1) * GB * 2 * TAPS] \
                        .rearrange("p (c f) -> p c f", c=GB)
                    self.dma(sy, "s_wf%d" % (g_ % NSL), dst, src)
                self.SY(f)
                mk[("wld", cb, g_)] = ("s_wf%d" % (g_ % NSL), self.counts["s_wf%d" % (g_ % NSL)])

            def emit_A(cb):
                sl = cb % 2
                for g_ in range(NSL):
                    emit_wf_load(g_, cb)

                def f(sy, cb=cb, sl=sl):
                    self.w(sy, "eGP", gpz)
                    self.w(sy, "ePE", mk.get(("bfree", cb - 2), 0))
                    self.dma(sy, "s_B%d" % sl, b_B[:, sl * BW + P: sl * BW + BW],
                             self.t_uT.ap()[cb])
                self.SY(f)
                qwB = ("s_B%d" % sl, self.counts["s_B%d" % sl])

                def f(sy, cb=cb, sl=sl):
                    self.w(sy, "eVE", mk.get(("gate", cb - 2), 0))
                    self.dma(sy, "s_sv%d" % sl, b_sv[:, sl * L: sl * L + L],
                             self.t_sv.ap()[P * cb:P * (cb + 1), :])
                    self.dma(sy, "s_sv%d" % sl, b_sg[:, sl * L: sl * L + L],
                             self.t_sg.ap()[P * cb:P * (cb + 1), :])
                self.SY(f)
                mk[("svld", cb)] = ("s_sv%d" % sl, self.counts["s_sv%d" % sl])

                for gr in range(NG):
                    pa = ymm[gr % 4]

                    def f(t, cb=cb, gr=gr, sl=sl, pa=pa, qwB=qwB):
                        self.wm(t, qwB)
                        self.wm(t, mk[("wld", cb, gr)])
                        self.w(t, "eSC", mk.get(("yev", cb, gr - 4), mk.get(("yev", cb - 1, NG - 4 + gr), 0)))
                        for cr in range(8):
                            cl = gr * 8 + cr
                            wbase = (gr % NSL) * GB * 2 * TAPS + cr * 2 * TAPS
                            W0 = b_w[:, wbase: wbase + TAPS]
                            W1 = b_w[:, wbase + TAPS: wbase + 2 * TAPS]
                            X0 = b_B[:, sl * BW + P + cl: sl * BW + P + cl + (NCH - 1) * P + 1: P]
                            X1 = b_B[:, sl * BW + cl: sl * BW + cl + (NCH - 1) * P + 1: P]
                            t.matmul(pa[:, cr * NCH:(cr + 1) * NCH], W0, X0, start=True, stop=False)
                            t.matmul(pa[:, cr * NCH:(cr + 1) * NCH], W1, X1, start=False, stop=True)
                        mm = self.inc("ePE", t.nop())
                        mk[("ymm", cb, gr)] = mm
                        if gr == NG - 1:
                            mk[("bfree", cb)] = mm
                        mk[("wcons", cb, gr)] = mm
                    self.T(f)
                    pem = self.counts["ePE"]
                    if gr + NSL < NG:
                        emit_wf_load(gr + NSL, cb)

                    def f(s, cb=cb, gr=gr, sl=sl, pa=pa, pem=pem):
                        self.w(s, "ePE", pem)
                        if gr == 0:
                            self.w(s, "ePE", mk.get(("stp", cb - 2), 0))
                        src = pa[:, 0:8 * NCH].rearrange("p (c i) -> p c i", c=8)
                        dst = b_S[:, sl * NCH * P:(sl + 1) * NCH * P] \
                            .rearrange("p (i c) -> p c i", c=P)[:, gr * 8:(gr + 1) * 8, :]
                        s.copy(dst, src)
                        mk[("yev", cb, gr)] = self.inc("eSC", s.nop())
                    self.S(f)
                mk[("S_ready", cb)] = self.counts["eSC"]

            def emit_B(cb):
                sl = cb % 2
                for m_ in range(NCH):
                    tb = tp2[m_ % 2]

                    def f(t, cb=cb, m_=m_, sl=sl, tb=tb):
                        self.w(t, "eSC", mk[("S_ready", cb)])
                        self.w(t, "eVE", mk.get(("gvm", cb, m_ - 2),
                                                mk.get(("gvm", cb - 1, NCH - 2 + m_), 0)))
                        mm = self.inc("ePE", t.matmul(
                            tb[:, 0:P], b_S[:, sl * NCH * P + m_ * P: sl * NCH * P + (m_ + 1) * P],
                            self.idnb[:], start=True, stop=True))
                        if m_ == NCH - 1:
                            mk[("stp", cb)] = mm
                    self.T(f)
                    pem = self.counts["ePE"]

                    def f(v, cb=cb, m_=m_, sl=sl, tb=tb, pem=pem):
                        self.w(v, "ePE", pem)
                        if m_ == 0:
                            self.wm(v, mk[("svld", cb)])
                            self.wm(v, mk.get(("zo", cb - 1), 0))
                        mk[("gvm", cb, m_)] = self.inc("eVE", v.tensor_mul(
                            b_yt[:, (m_ % 2) * P:(m_ % 2 + 1) * P], tb[:, 0:P],
                            b_sg[:, sl * L + m_ * P: sl * L + (m_ + 1) * P]))
                        gv = self.inc("eVE", v.tensor_tensor(
                            b_z[:, m_ * P:(m_ + 1) * P],
                            b_yt[:, (m_ % 2) * P:(m_ % 2 + 1) * P],
                            b_sv[:, sl * L + m_ * P: sl * L + (m_ + 1) * P], ALU.add))
                        if m_ == NCH - 1:
                            mk[("gate", cb)] = gv
                    self.V(f)

                def f(g, cb=cb):
                    self.w(g, "eVE", mk[("gate", cb)])
                    mk[("zo", cb)] = self.dma(
                        g, "g_z", self.t_z.ap()[P * cb:P * (cb + 1), :], b_z[:])
                self.G(f)

            emit_A(0)
            for cb in range(NI):
                if cb + 1 < NI:
                    emit_A(cb + 1)
                emit_B(cb)

    def outproj(self):
        c, nc = self.c, self.nc
        D, L, INNER = c.D, c.L, c.INNER
        ND, NI = D // P, INNER // P
        TT = 512
        NT, NS = L // TT, TT // P
        mk = {}

        _es = contextlib.ExitStack()
        b_w = _es.enter_context(nc.sbuf_tensor("op_w", [P, ND * NI * P], BF16))
        b_z = _es.enter_context(nc.sbuf_tensor("op_z", [P, 2 * NI * TT], BF16))
        b_o = _es.enter_context(nc.sbuf_tensor("op_o", [P, 2 * TT], F32))
        b_op = _es.enter_context(nc.sbuf_tensor("op_op", [P, 2 * TT], F32))
        ps0 = _es.enter_context(nc.psum_tensor("op_p0", [P, 512], F32))
        ps1 = _es.enter_context(nc.psum_tensor("op_p1", [P, 512], F32))
        tfa = _es.enter_context(nc.psum_tensor("op_tfa", [P, 512], F32))
        tfb = _es.enter_context(nc.psum_tensor("op_tfb", [P, 512], F32))
        with _es:
            tf2 = [tfa, tfb]
            mmps = [ps0, ps1]

            def f(g):
                for ch in range(8):
                    r0 = (L // 8) * ch
                    mk["precopy"] = self.dma(g, "g_pre", self.t_x2.ap()[r0:r0 + L // 8, :],
                                             self.t_r2.ap()[r0:r0 + L // 8, :])
            self.G(f)

            def f(sy):
                for dt_ in range(ND):
                    self.dma(sy, "s_opw", b_w[:, dt_ * NI * P:(dt_ + 1) * NI * P],
                             self.t_outwp.ap()[dt_])
            self.SY(f)
            qww = ("s_opw", self.counts["s_opw"])

            for ti in range(NT):
                zsl = ti % 2

                def f(sy, ti=ti, zsl=zsl):
                    self.w(sy, "ePE", mk.get(("opmm", ti - 2, ND - 1), 0))
                    t0 = ti * TT
                    src = bass.AP(self.t_z, t0, [[L, P], [P * L, NI], [1, TT]])
                    dst = b_z[:, zsl * NI * TT:(zsl + 1) * NI * TT].rearrange("p (k t) -> p k t", t=TT)
                    self.dma(sy, "s_z%d" % zsl, dst, src)
                self.SY(f)
                qwz = ("s_z%d" % zsl, self.counts["s_z%d" % zsl])

                for dt_ in range(ND):
                    pa = mmps[dt_ % 2]

                    def f(t, ti=ti, dt_=dt_, zsl=zsl, pa=pa, qwz=qwz):
                        self.wm(t, qwz)
                        self.wm(t, qww)
                        self.w(t, "eSC", mk.get(("opev", ti, dt_ - 2), mk.get(("opev", ti - 1, ND - 2 + dt_), 0)))
                        for kt in range(NI):
                            t.matmul(pa[:, 0:TT], b_w[:, (dt_ * NI + kt) * P:(dt_ * NI + kt + 1) * P],
                                     b_z[:, zsl * NI * TT + kt * TT: zsl * NI * TT + (kt + 1) * TT],
                                     start=(kt == 0), stop=(kt == NI - 1))
                        mk[("opmm", ti, dt_)] = self.inc("ePE", t.nop())
                    self.T(f)
                    pem = self.counts["ePE"]

                    def f(s, ti=ti, dt_=dt_, pa=pa, pem=pem):
                        self.w(s, "ePE", pem)
                        self.w(s, "ePE", mk.get(("optp", ti, dt_ - 2), 0))
                        s.activation(b_o[:, (dt_ % 2) * TT:(dt_ % 2) * TT + TT], pa[:, 0:TT],
                                     AF.Identity, bias=self.v_outb[:, dt_:dt_ + 1])
                        mk[("opev", ti, dt_)] = self.inc("eSC", s.nop())
                    self.S(f)
                    scm = self.counts["eSC"]

                    for pj in range(NS):
                        pidx = (ti * ND + dt_) * NS + pj
                        tfb_ = tf2[pidx % 2]

                        def f(t, ti=ti, dt_=dt_, pj=pj, tfb_=tfb_, scm=scm, pidx=pidx):
                            self.w(t, "eSC", scm)
                            self.w(t, "eSC", mk.get(("oev", pidx - 2), 0))
                            mm = self.inc("ePE", t.matmul(
                                tfb_[:, 0:P],
                                b_o[:, (dt_ % 2) * TT + pj * P:(dt_ % 2) * TT + (pj + 1) * P],
                                self.idn[:], start=True, stop=True))
                            if pj == NS - 1:
                                mk[("optp", ti, dt_)] = mm
                        self.T(f)
                        pet = self.counts["ePE"]

                        def f(s, ti=ti, dt_=dt_, pj=pj, tfb_=tfb_, pet=pet, pidx=pidx):
                            self.w(s, "ePE", pet)
                            self.wm(s, mk.get(("oacc", ti, dt_ - 2),
                                              mk.get(("oacc", ti - 1, ND - 2 + dt_), 0)))
                            s.copy(b_op[:, (dt_ % 2) * TT + pj * P:(dt_ % 2) * TT + (pj + 1) * P],
                                   tfb_[:, 0:P])
                            mk[("oev", pidx)] = self.inc("eSC", s.nop())
                        self.S(f)
                    scm2 = self.counts["eSC"]

                    def f(g, ti=ti, dt_=dt_, scm2=scm2):
                        self.w(g, "eSC", scm2)
                        if ti == 0 and dt_ == 0:
                            self.wm(g, mk["precopy"])
                        r0 = ti * TT
                        dst3 = bass.AP(self.t_x2, r0 * D + dt_ * P, [[D, P], [P * D, NS], [1, P]])
                        src = b_op[:, (dt_ % 2) * TT:(dt_ % 2) * TT + TT].rearrange("p (pj q) -> p pj q", q=P)
                        mk[("oacc", ti, dt_)] = self.dma(g, "g_oac%d" % (dt_ % 2), dst3, src,
                                                         accum_op=ALU.add)
                    self.G(f)


def build_program(cfg: Cfg, stop_after=None):
    m = MK(cfg, stop_after)
    nc = m.build()
    m._stack.close()
    return nc, m


_CFG = Cfg(D=1024, L=8192, H=4096, STATE=64, TT_FFN=1024)
_NCORES = 8


def kernel(**inputs):
    from concourse.bass_utils import run_bass_kernel_spmd
    nc, m = build_program(_CFG, None)
    INNER = _CFG.INNER
    x = inputs["L0_attn_x"]
    base = {
        "norm_gamma": inputs["L0_attn_norm_gamma"],
        "norm_beta": inputs["L0_attn_norm_beta"],
        "in_w": inputs["L0_attn_in_w"],
        "in_b": inputs["L0_attn_in_b"],
        "conv_w": np.reshape(inputs["L0_attn_conv_w"], (INNER, 3)),
        "conv_b": inputs["L0_attn_conv_b"],
        "A_log": inputs["L0_attn_A_log"],
        "B_p": inputs["L0_attn_B_p"],
        "C_p": inputs["L0_attn_C_p"],
        "D_p": inputs["L0_attn_D_p"],
        "out_w": inputs["L0_attn_out_w"],
        "out_b": inputs["L0_attn_out_b"],
        "w1_w": inputs["L0_ffn_w1_w"], "w1_s": inputs["L0_ffn_w1_s"],
        "w2_w": inputs["L0_ffn_w2_w"], "w2_s": inputs["L0_ffn_w2_s"],
        "h1_w": inputs["L0_ffn_h1_w"], "h1_s": inputs["L0_ffn_h1_s"],
        "h2_w": inputs["L0_ffn_h2_w"], "h2_s": inputs["L0_ffn_h2_s"],
        "w3_w": inputs["L0_ffn_w3_w"], "w3_s": inputs["L0_ffn_w3_s"],
    }
    in_maps = [dict(base, x=np.ascontiguousarray(x[b])) for b in range(_NCORES)]
    res = run_bass_kernel_spmd(nc, in_maps, core_ids=list(range(_NCORES)))
    return np.stack([np.asarray(res.results[b]["out"]) for b in range(_NCORES)])


# version 13 (final)
# speedup vs baseline: 1.1248x; 1.0284x over previous
import contextlib
import numpy as np
import concourse.bass as bass
import concourse.mybir as mybir

F32 = mybir.dt.float32
BF16 = mybir.dt.bfloat16
AF = mybir.ActivationFunctionType
ALU = mybir.AluOpType
AX = mybir.AxisListType.X
P = 128


class Cfg:
    def __init__(self, D=1024, L=8192, H=4096, STATE=64, TT_FFN=1024):
        self.D, self.L, self.H, self.STATE = D, L, H, STATE
        self.INNER, self.O3 = 2 * D, 6 * D
        self.TAPS = 128
        self.TT_FFN = TT_FFN
        assert L % TT_FFN == 0 and L % P == 0 and L % 512 == 0
        self.NCH = L // P



class RecInst:
    def __init__(self, method, args, kwargs, is_nop=False):
        self.method, self.args, self.kwargs = method, args, kwargs
        self.incs = []
        self.is_nop = is_nop


class RecWait:
    def __init__(self, sem_name, val):
        self.sem_name, self.val = sem_name, val


class Rec:
    def __init__(self, owner):
        self._ops = []
        self._owner = owner

    def wait_ge(self, sem_handle_or_name, val):
        self._ops.append(RecWait(sem_handle_or_name, val))

    def nop(self):
        e = RecInst("nop", (), {}, is_nop=True)
        e.rec = self
        self._ops.append(e)
        return e

    def __getattr__(self, m):
        if m.startswith("_"):
            raise AttributeError(m)

        def call(*a, **k):
            e = RecInst(m, a, k)
            e.rec = self
            self._ops.append(e)
            return e
        return call

    def last_real(self):
        for e in reversed(self._ops):
            if isinstance(e, RecInst):
                return None if e.is_nop else e
        return None

    def replay(self, eng, sems):
        for e in self._ops:
            if isinstance(e, RecWait):
                eng.wait_ge(sems[e.sem_name], e.val)
            else:
                if e.is_nop and not e.incs:
                    continue
                meth = "drain" if (e.is_nop and e.incs) else e.method
                inst = getattr(eng, meth)(*(() if meth == "drain" else e.args),
                                          **({} if meth == "drain" else e.kwargs))
                for (nm, kk) in e.incs:
                    inst.then_inc(sems[nm], kk)


class MK:
    SEMS = ["ePE", "eSC", "eVE", "eGP", "eSY"]

    def __init__(self, cfg: Cfg, stop_after=None):
        self.c = cfg
        self.stop_after = stop_after
        self.nc = bass.Bass("TRN2", target_bir_lowering=False)
        self.counts = {}
        self.sems = {}
        self.dma_streams = []
        self.mk = {}
        self._entry_rec = {}
        self.rT, self.rS, self.rV = Rec(self), Rec(self), Rec(self)
        self.rG, self.rSY = Rec(self), Rec(self)
        self._stack = contextlib.ExitStack()

    def stream(self, name):
        if name not in self.sems:
            self.sems[name] = self._stack.enter_context(self.nc.semaphore(name))
            self.counts[name] = 0
            self.dma_streams.append(name)
        return name

    def dma(self, rec, stname, out, in_, **kw):
        st = self.stream(stname)
        inst = rec.dma_start(out=out, in_=in_, **kw)
        c = self.inc(st, inst, 16)
        return (st, c)

    def wm(self, eng, mark):
        if mark:
            self.w(eng, mark[0], mark[1])

    def wm2(self, eng, mark):
        if mark:
            self.w(eng, mark[0], mark[1])

    def inc(self, name, entry, k=1):
        if entry.is_nop:
            rec = entry.rec
            idx = rec._ops.index(entry)
            prev = rec._ops[idx - 1] if idx > 0 else None
            if isinstance(prev, RecInst) and not prev.is_nop and not prev.incs:
                entry = prev
        if entry.incs and not entry.is_nop:
            nxt = RecInst("drain", (), {})
            nxt.rec = entry.rec
            pos = entry.rec._ops.index(entry) + 1
            entry.rec._ops.insert(pos, nxt)
            entry = nxt
        entry.incs.append((name, k))
        self.counts[name] += k
        return self.counts[name]

    def w(self, eng, name, val):
        if val and val > 0:
            eng.wait_ge(name, val)

    def _run(self, rec, f):
        f(rec)

    def T(self, f): self._run(self.rT, f)
    def S(self, f): self._run(self.rS, f)
    def V(self, f): self._run(self.rV, f)
    def G(self, f): self._run(self.rG, f)
    def SY(self, f): self._run(self.rSY, f)

    def build(self):
        c, nc = self.c, self.nc
        D, L, H, INNER, O3, STATE, TAPS = c.D, c.L, c.H, c.INNER, c.O3, c.STATE, c.TAPS
        NH, ND, NI, NO3 = H // P, D // P, INNER // P, O3 // P
        st = self._stack

        for s in self.SEMS:
            self.sems[s] = st.enter_context(nc.semaphore(s))
            self.counts[s] = 0

        def din(name, shape):
            return nc.dram_tensor(name, shape, F32, kind="ExternalInput")
        self.t_x = din("x", [L, D])
        self.t_gamma, self.t_beta = din("norm_gamma", [D]), din("norm_beta", [D])
        self.t_inw, self.t_inb = din("in_w", [D, O3]), din("in_b", [O3])
        self.t_convw, self.t_convb = din("conv_w", [INNER, 3]), din("conv_b", [INNER])
        self.t_alog, self.t_bp = din("A_log", [INNER, STATE]), din("B_p", [INNER, STATE])
        self.t_cp, self.t_dp = din("C_p", [INNER, STATE]), din("D_p", [INNER])
        self.t_outw, self.t_outb = din("out_w", [INNER, D]), din("out_b", [D])
        self.t_w1, self.t_w1s = din("w1_w", [H, D]), din("w1_s", [H])
        self.t_w2, self.t_w2s = din("w2_w", [H, D]), din("w2_s", [H])
        self.t_h1, self.t_h1s = din("h1_w", [H, H]), din("h1_s", [H])
        self.t_h2, self.t_h2s = din("h2_w", [H, H]), din("h2_s", [H])
        self.t_w3, self.t_w3s = din("w3_w", [D, H]), din("w3_s", [D])
        self.t_out = nc.dram_tensor("out", [L, D], F32, kind="ExternalOutput")

        sa = self.stop_after

        def scratch(name, shape, dtype, dbg=False):
            return nc.dram_tensor(name, shape, dtype,
                                  kind="ExternalOutput" if dbg else "Internal")
        self.t_w1p = scratch("w1p", [NH, P, D], BF16, sa == "p0")
        self.t_w2p = scratch("w2p", [NH, P, D], BF16)
        self.t_h1p = scratch("h1p", [NH, P, H], BF16)
        self.t_h2p = scratch("h2p", [NH, P, H], BF16)
        self.t_w3p = scratch("w3p", [ND, P, H], BF16, sa == "p0")
        self.t_inwp = scratch("inwp", [NO3, P, D], BF16, sa == "p0")
        self.t_outwp = scratch("outwp", [ND, P, INNER], BF16)
        self.t_kpad = scratch("kpad", [INNER, 3 * TAPS], BF16, sa == "p0")
        self.t_wfull = scratch("wfull", [INNER, P, 2 * TAPS], BF16, sa == "p0")
        self.t_vinb = scratch("vinb_dbg", [P, NO3], F32, sa == "p0")
        self.t_x1 = scratch("x1", [L, D], F32, sa == "p1")
        self.t_r2 = scratch("r2", [L, D], F32, sa in ("a1", "a2", "scan"))
        self.t_u = scratch("u_pre", [INNER, L], F32, sa == "a1")
        self.t_sv = scratch("sv", [INNER, L], BF16, sa == "a1")
        self.t_sg = scratch("sg", [INNER, L], BF16)
        self.t_uT = scratch("uT", [INNER // P, P, c.NCH * P], BF16, sa == "a2")
        self.t_z = scratch("z", [INNER, L], BF16, sa == "scan")
        self.t_x2 = scratch("x2", [L, D], F32, sa == "oproj")

        def sb(name, shape, dtype=F32):
            return st.enter_context(nc.sbuf_tensor(name, shape, dtype))
        self.idn = sb("idn", [P, P])
        self.idnb = sb("idnb", [P, P], BF16)
        self.iota = sb("iota", [P, TAPS])
        self.v_w1s, self.v_w2s = sb("v_w1s", [P, NH]), sb("v_w2s", [P, NH])
        self.v_h1s, self.v_h2s = sb("v_h1s", [P, NH]), sb("v_h2s", [P, NH])
        self.v_w3s = sb("v_w3s", [P, ND])
        self.v_gamma, self.v_beta = sb("v_gamma", [P, ND]), sb("v_beta", [P, ND])
        self.v_inb = sb("v_inb", [P, NO3])
        self.v_convw = sb("v_convw", [P, 3 * NI])
        self.v_convb = sb("v_convb", [P, NI])
        self.v_outb = sb("v_outb", [P, ND])
        self.v_dp = sb("v_dp", [P, NI])
        self.v_eps = sb("v_eps", [P, 1])
        self.sb_na = sb("sb_na", [P, NI * STATE])
        self.sb_cb = sb("sb_cb", [P, NI * STATE])
        self.sb_ke = sb("sb_ke", [P, 4 * self.c.TAPS], BF16)
        self.sb_k = sb("sb_k", [P, self.c.TAPS])
        self.sb_k2 = sb("sb_k2", [P, self.c.TAPS])
        self.sb_z3 = sb("sb_z3", [P, self.c.TAPS + 1])

        done = False
        self.phase0()
        self.barrier()
        if sa == "p0":
            done = True
        if not done:
            self.ffn(self.t_x, self.t_x1, side_work=self.kbuild_side_work())
            self.barrier()
            done = sa == "p1"
        if not done:
            self.mixer_a1()
            self.barrier()
            done = sa == "a1"
        if not done:
            self.mixer_a2()
            self.barrier()
            done = sa == "a2"
        if not done:
            self.scan()
            self.barrier()
            done = sa == "scan"
        if not done:
            self.outproj()
            self.barrier()
            done = sa == "oproj"
        if not done:
            self.ffn(self.t_x2, self.t_out)
        self.final_wait()
        self._auto_drain()
        with nc.Block() as blk:
            blk.tensor(lambda e: self.rT.replay(e, self.sems))
            blk.scalar(lambda e: self.rS.replay(e, self.sems))
            blk.vector(lambda e: self.rV.replay(e, self.sems))
            blk.gpsimd(lambda e: self.rG.replay(e, self.sems))
            blk.sync(lambda e: self.rSY.replay(e, self.sems))
        return nc

    def _auto_drain(self):
        for rec in (self.rV, self.rS, self.rG):
            newops = []
            pend = False
            for e in rec._ops:
                if isinstance(e, RecInst) and not e.is_nop and e.method != "drain":
                    if pend:
                        d = RecInst("drain", (), {})
                        d.rec = rec
                        newops.append(d)
                    newops.append(e)
                    pend = True
                else:
                    if isinstance(e, RecInst) and e.method == "drain":
                        pend = False
                    newops.append(e)
            rec._ops[:] = newops

    def barrier(self):
        cn = self.counts
        eng = [("ePE", self.T), ("eSC", self.S), ("eVE", self.V), ("eGP", self.G), ("eSY", self.SY)]
        sync_streams = [st for st in self.dma_streams if st.startswith("s_")]
        g_streams = [st for st in self.dma_streams
                     if st.startswith("g_") and st != "g_wfull"]
        for nm, sec in eng:
            def f(e, nm=nm):
                if nm == "eSY":
                    for st in sync_streams:
                        self.w(e, st, cn[st])
                if nm == "eGP":
                    for st in g_streams:
                        self.w(e, st, cn[st])
                self.inc(nm, e.nop())
            sec(f)
        targ = {nm: cn[nm] for nm, _ in eng}
        for nm, sec in eng:
            def f(e, nm=nm):
                for o, _ in eng:
                    if o != nm:
                        e.wait_ge(o, targ[o])
                self.inc(nm, e.nop())
            sec(f)
        targ2 = {nm: cn[nm] for nm, _ in eng}
        for nm, sec in eng:
            def f(e, nm=nm):
                for o, _ in eng:
                    if o != nm:
                        e.wait_ge(o, targ2[o])
            sec(f)

    def final_wait(self):
        self.barrier()
        tot = {st: self.counts[st] for st in self.dma_streams}

        def f(g):
            for st, v in tot.items():
                self.w(g, st, v)
        self.G(f)

    def phase0(self):
        c, nc = self.c, self.nc
        D, H, INNER, O3, STATE, TAPS = c.D, c.H, c.INNER, c.O3, c.STATE, c.TAPS
        NH, ND, NI, NO3 = H // P, D // P, INNER // P, O3 // P
        mk = self.mk
        WMAX = max(O3, H)

        _es = contextlib.ExitStack()
        ps_a = _es.enter_context(nc.psum_tensor("p0ps_a", [P, 512], F32))
        ps_c = _es.enter_context(nc.psum_tensor("p0ps_c", [P, 512], F32))
        ps_d = _es.enter_context(nc.psum_tensor("p0ps_d", [P, 512], F32))
        p0_vec = _es.enter_context(nc.sbuf_tensor("p0_vec", [P, P], F32))
        p0_ld = _es.enter_context(nc.sbuf_tensor("p0_ld", [P, 2 * WMAX], F32))
        p0_bf = _es.enter_context(nc.sbuf_tensor("p0_bf", [P, 2 * WMAX], BF16))
        p0_ev = _es.enter_context(nc.sbuf_tensor("p0_ev", [P, 4 * P], BF16))
        p0_st = _es.enter_context(nc.sbuf_tensor("p0_st", [P, 2 * 32 * P], BF16))
        p0_as = _es.enter_context(nc.sbuf_tensor("p0_as", [P, NI * STATE], F32))
        p0_e = _es.enter_context(nc.sbuf_tensor("p0_e", [P, STATE * TAPS], F32))
        p0_k = _es.enter_context(nc.sbuf_tensor("p0_k", [P, TAPS], F32))
        p0_k2 = _es.enter_context(nc.sbuf_tensor("p0_k2", [P, TAPS], F32))
        p0_z = _es.enter_context(nc.sbuf_tensor("p0_z", [P, 3 * TAPS], F32))
        p0_bias = _es.enter_context(nc.sbuf_tensor("p0_bias", [P, NO3], F32))
        with _es:
            def f(g):
                self.inc("eGP", g.memset(self.idn[:], 0.0))
                self.inc("eGP", g.memset(self.idnb[:], 0.0))
                g.iota(self.iota[:], pattern=[[1, TAPS]], base=0, channel_multiplier=0,
                        allow_small_or_imprecise_dtypes=True)
                g.memset(p0_z[:], 0.0)
                g.memset(self.v_eps[:], 1e-5)
                self.inc("eGP", g.memset(p0_bias[:], 0.0))
                self.w(g, "eGP", self.counts["eGP"])
                self.inc("eGP", g.affine_select(out=self.idn[:], in_=self.idn[:],
                                compare_op=ALU.not_equal,
                                fill=1.0, base=0, pattern=[[-1, P]], channel_multiplier=1))
                self.inc("eGP", g.affine_select(out=self.idnb[:], in_=self.idnb[:],
                                compare_op=ALU.not_equal,
                                fill=1.0, base=0, pattern=[[-1, P]], channel_multiplier=1))
                self.w(g, "eGP", self.counts["eGP"])
            self.G(f)
            g_consts = self.counts["eGP"]

            vecs = [(self.t_w1s, self.v_w1s, NH, None), (self.t_w2s, self.v_w2s, NH, None),
                    (self.t_h1s, self.v_h1s, NH, None), (self.t_h2s, self.v_h2s, NH, None),
                    (self.t_w3s, self.v_w3s, ND, 0.5),
                    (self.t_gamma, self.v_gamma, ND, None), (self.t_beta, self.v_beta, ND, None),
                    (self.t_convb, self.v_convb, NI, None), (self.t_outb, self.v_outb, ND, None),
                    (self.t_dp, self.v_dp, NI, None)]
            for vi, (src, dst, n_, fold) in enumerate(vecs):
                def f(sy, src=src, n_=n_):
                    self.w(sy, "ePE", mk.get(("p0v_tp", vi - 1), 0))
                    self.dma(sy, "s_vec", p0_vec[0:n_, :],
                             src.ap().rearrange("(a b) -> a b", b=P))
                self.SY(f)
                qwm = ("s_vec", self.counts["s_vec"])

                def f(t, n_=n_, qwm=qwm, vi=vi):
                    if vi == 0:
                        self.w(t, "eGP", g_consts)
                    self.wm(t, qwm)
                    self.w(t, "eSC", mk.get(("p0v_ev", vi - 1), 0))
                    mk[("p0v_tp", vi)] = self.inc(
                        "ePE", t.matmul(ps_a[:, 0:n_], p0_vec[0:n_, :], self.idn[0:n_, 0:n_], start=True, stop=True))
                self.T(f)

                def f(s, dst=dst, n_=n_, fold=fold, vi=vi):
                    self.w(s, "ePE", mk[("p0v_tp", vi)])
                    if fold is not None:
                        s.activation(dst[:, 0:n_], ps_a[:, 0:n_], AF.Identity, scale=fold)
                        mk[("p0v_ev", vi)] = self.inc("eSC", s.nop())
                    else:
                        mk[("p0v_ev", vi)] = self.inc("eSC", s.copy(dst[:, 0:n_], ps_a[:, 0:n_]))
                self.S(f)

            else:
                NI_scan = NI
            def f(sy):
                for ct in range(NI_scan):
                    self.dma(sy, "s_const", self.v_convw[:, 3 * ct:3 * ct + 3],
                             self.t_convw.ap()[P * ct:P * ct + P, :])
                self.dma(sy, "s_const", p0_as[:].rearrange("p (ct s) -> p ct s", s=STATE),
                         self.t_alog.ap().rearrange("(ct p) s -> p ct s", p=P))
                self.dma(sy, "s_const", self.sb_na[:].rearrange("p (ct s) -> p ct s", s=STATE),
                         self.t_bp.ap().rearrange("(ct p) s -> p ct s", p=P))
                self.dma(sy, "s_const", self.sb_cb[:].rearrange("p (ct s) -> p ct s", s=STATE),
                         self.t_cp.ap().rearrange("(ct p) s -> p ct s", p=P))
            self.SY(f)
            qw_scan = ("s_const", self.counts["s_const"])

            def f(v):
                self.wm(v, qw_scan)
                self.inc("eVE", v.tensor_mul(self.sb_cb[:], self.sb_cb[:], self.sb_na[:]))
            self.V(f)
            ve_cb = self.counts["eVE"]

            def f(s):
                self.w(s, "eVE", ve_cb)
                self.inc("eSC", s.activation(self.sb_na[:], p0_as[:], AF.Exp))
            self.S(f)
            sc_na = self.counts["eSC"]

            def f(v):
                self.w(v, "eSC", sc_na)
                self.inc("eVE", v.tensor_scalar(self.sb_na[:], self.sb_na[:], -1.0, None, ALU.mult))
            self.V(f)
            ve_na = self.counts["eVE"]


            rs = {"slot": 0, "pc": 0, "st": 0}

            def restripe_T(t_src, t_dst, O, K, tag):
                NK = K // P
                for ot in range(O // P):
                    gs = rs["slot"]; rs["slot"] += 1
                    slot = gs % 2
                    gst = rs["st"]; rs["st"] += 1
                    ssl = gst % 2

                    def f(sy, ot=ot, slot=slot, gs=gs):
                        for m_ in mk.get(("rs_ld", gs - 2), []):
                            self.wm(sy, m_)
                        self.dma(sy, "s_rs%d" % slot, p0_ld[:, slot * WMAX:slot * WMAX + K],
                                 t_src.ap()[P * ot:P * ot + P, :])
                    self.SY(f)
                    qwm = ("s_rs%d" % slot, self.counts["s_rs%d" % slot])

                    def f(v, slot=slot, qwm=qwm, gs=gs):
                        self.wm(v, qwm)
                        self.wm2(v, mk.get(("rs_bf", gs - 2), 0))
                        mk[("rs_cast", gs)] = self.inc("eVE", v.tensor_copy(
                            p0_bf[:, slot * WMAX:slot * WMAX + K],
                            p0_ld[:, slot * WMAX:slot * WMAX + K]))
                        mk[("rs_ld", gs)] = [("eVE", mk[("rs_cast", gs)])]
                    self.V(f)
                    ve_m = self.counts["eVE"]

                    for kt in range(NK):
                        pc = rs["pc"]; rs["pc"] += 1
                        tb = ps_c if pc % 2 == 0 else ps_d

                        def f(t, kt=kt, slot=slot, pc=pc, tb=tb, ve_m=ve_m, gs=gs, NK=NK):
                            self.w(t, "eVE", ve_m)
                            self.w(t, "eSC", mk.get(("rs_ev", pc - 2), 0))
                            mm = self.inc("ePE", t.matmul(
                                tb[:, 0:P],
                                p0_bf[:, slot * WMAX + P * kt: slot * WMAX + P * (kt + 1)],
                                self.idnb[:], start=True, stop=True))
                            if kt == NK - 1:
                                mk[("rs_bf", gs)] = ("ePE", mm)
                        self.T(f)
                        pem = self.counts["ePE"]

                        def f(s, kt=kt, pc=pc, tb=tb, pem=pem, ssl=ssl, gst=gst, NK=NK):
                            self.w(s, "ePE", pem)
                            if kt == 0:
                                self.wm(s, mk.get(("rst_out", gst - 2), 0))
                            s.copy(p0_st[:, ssl * 32 * P + kt * P: ssl * 32 * P + (kt + 1) * P],
                                   tb[:, 0:P])
                            mk[("rs_ev", pc)] = self.inc("eSC", s.nop())
                        self.S(f)
                    scm = self.counts["eSC"]

                    def f(g, ot=ot, ssl=ssl, gst=gst, scm=scm, NK=NK):
                        self.w(g, "eSC", scm)
                        mk[("rst_out", gst)] = self.dma(
                            g, "g_rso%d" % ssl, t_dst.ap()[ot],
                            p0_st[:, ssl * 32 * P: ssl * 32 * P + K])
                    self.G(f)

            def restripe_N(t_src, t_dst, K, O, tag, gamma=None, bias_beta=False):
                NK = K // P
                for kt in range(NK):
                    gs = rs["slot"]; rs["slot"] += 1
                    slot = gs % 2

                    def f(sy, kt=kt, slot=slot, gs=gs):
                        for m_ in mk.get(("rs_ld", gs - 2), []):
                            self.wm(sy, m_)
                        self.dma(sy, "s_rs%d" % slot, p0_ld[:, slot * WMAX:slot * WMAX + O],
                                 t_src.ap()[P * kt:P * kt + P, :])
                    self.SY(f)
                    qwm = ("s_rs%d" % slot, self.counts["s_rs%d" % slot])

                    def f(v, kt=kt, slot=slot, qwm=qwm, gs=gs):
                        self.wm(v, qwm)
                        self.wm2(v, mk.get(("rs_bf", gs - 2), 0))
                        if gamma is not None:
                            ii = v.tensor_scalar(p0_bf[:, slot * WMAX:slot * WMAX + O],
                                                 p0_ld[:, slot * WMAX:slot * WMAX + O],
                                                 gamma[:, kt:kt + 1], None, ALU.mult)
                        else:
                            ii = v.tensor_copy(p0_bf[:, slot * WMAX:slot * WMAX + O],
                                               p0_ld[:, slot * WMAX:slot * WMAX + O])
                        mk[("rs_cast", gs)] = self.inc("eVE", ii)
                        mk[("rs_ld", gs)] = [("eVE", mk[("rs_cast", gs)])]
                    self.V(f)

                    if bias_beta:
                        def f(t, kt=kt, slot=slot, qwm=qwm):
                            self.wm(t, qwm)
                            self.w(t, "eVE", mk.get((tag, "bacc", kt - 1), 0))
                            for ot in range(O // P):
                                t.matmul(ps_a[:, ot:ot + 1],
                                         p0_ld[:, slot * WMAX + P * ot: slot * WMAX + P * (ot + 1)],
                                         self.v_beta[:, kt:kt + 1], start=True, stop=True)
                            mk[(tag, "bmm", kt)] = self.inc("ePE", t.nop())
                            mk[("rs_ld", gs)] = mk.get(("rs_ld", gs), []) + [("ePE", mk[(tag, "bmm", kt)])]
                        self.T(f)
                        peb = self.counts["ePE"]

                        def f(v, kt=kt, peb=peb):
                            self.w(v, "ePE", peb)
                            mk[(tag, "bacc", kt)] = self.inc("eVE", v.tensor_tensor(
                                p0_bias[:, 0:O // P], p0_bias[:, 0:O // P],
                                ps_a[:, 0:O // P], ALU.add))
                        self.V(f)

                    def f(g, kt=kt, slot=slot, gs=gs):
                        self.w(g, "eVE", mk[("rs_cast", gs)])
                        dst = bass.AP(t_dst, kt * P,
                                      [[K, P], [P * K, O // P], [1, P]])
                        src = p0_bf[:, slot * WMAX:slot * WMAX + O].rearrange("p (ot o) -> p ot o", o=P)
                        m = self.dma(g, "g_rsn%d" % slot, dst, src)
                        mk[("rs_bf", gs)] = m
                    self.G(f)

            restripe_T(self.t_w1, self.t_w1p, H, D, "w1")
            restripe_T(self.t_w2, self.t_w2p, H, D, "w2")
            restripe_T(self.t_h1, self.t_h1p, H, H, "h1")
            restripe_T(self.t_h2, self.t_h2p, H, H, "h2")
            restripe_T(self.t_w3, self.t_w3p, D, H, "w3")
            restripe_N(self.t_inw, self.t_inwp, D, O3, "inw", gamma=self.v_gamma, bias_beta=True)
            restripe_N(self.t_outw, self.t_outwp, INNER, D, "outw")


            ve_bias = mk[("inw", "bacc", ND - 1)]

            def f(sy):
                self.w(sy, "ePE", mk[("p0v_tp", len(vecs) - 1)])
                self.dma(sy, "s_vec", p0_vec[0:NO3, :],
                         self.t_inb.ap().rearrange("(a b) -> a b", b=P))
            self.SY(f)
            qw_inb = ("s_vec", self.counts["s_vec"])

            def f(t):
                self.wm(t, qw_inb)
                self.w(t, "eVE", ve_bias)
                self.inc("ePE", t.matmul(ps_a[:, 256:256 + NO3], p0_vec[0:NO3, :],
                                            self.idn[0:NO3, 0:NO3], start=True, stop=True))
            self.T(f)
            pe_inb = self.counts["ePE"]

            def f(v):
                self.w(v, "ePE", pe_inb)
                self.inc("eVE", v.tensor_tensor(self.v_inb[:], p0_bias[:, 0:NO3],
                                                ps_a[:, 256:256 + NO3], ALU.add))
            self.V(f)
            ve_vinb = self.counts["eVE"]

            if self.stop_after == "p0":
                def f(g):
                    self.w(g, "eVE", ve_vinb)
                    self.dma(g, "g_dbg", self.t_vinb.ap(), self.v_inb[:])
                self.G(f)


    def kbuild_side_work(self):
        c = self.c
        STATE, TAPS, INNER = c.STATE, c.TAPS, c.INNER
        NI = INNER // P
        mk = self.mk
        works = []

        zmark = [0]

        def mkzero():
            def f(g):
                zmark[0] = self.inc("eGP", g.memset(self.sb_z3[:], 0.0))
            self.G(f)
        works.append(mkzero)

        for ct in range(NI):
            for ng in range(STATE // 4):
                def unit(ct=ct, ng=ng):
                    def f(s, ct=ct, ng=ng):
                        if ng == 0:
                            self.wm(s, mk.get(("kout", ct - 1), 0))
                        pv = (ct, ng - 1) if ng > 0 else (ct - 1, STATE // 4 - 1)
                        self.w(s, "eVE", mk.get(("kacc",) + pv, 0))
                        for j in range(4):
                            n_ = ng * 4 + j
                            s.activation(self.sb_ke[:, j * TAPS:(j + 1) * TAPS], self.iota[:],
                                         AF.Exp, scale=self.sb_na[:, ct * STATE + n_:ct * STATE + n_ + 1])
                        mk[("kexp", ct, ng)] = self.inc("eSC", s.nop())
                    self.S(f)

                    def f(v, ct=ct, ng=ng):
                        self.w(v, "eSC", mk[("kexp", ct, ng)])
                        for j in range(4):
                            n_ = ng * 4 + j
                            e_sl = self.sb_ke[:, j * TAPS:(j + 1) * TAPS]
                            cb_sl = self.sb_cb[:, ct * STATE + n_:ct * STATE + n_ + 1]
                            if n_ == 0:
                                v.tensor_scalar(self.sb_k2[:], e_sl, cb_sl, None, ALU.mult)
                            else:
                                src, dst = (self.sb_k2, self.sb_k) if n_ % 2 == 1 else (self.sb_k, self.sb_k2)
                                v.scalar_tensor_tensor(dst[:], e_sl, cb_sl, src[:], ALU.mult, ALU.add)
                        if ng == STATE // 4 - 1:
                            fin = self.sb_k if STATE % 2 == 0 else self.sb_k2
                            v.tensor_tensor(fin[:, 0:1], fin[:, 0:1], self.v_dp[:, ct:ct + 1], ALU.add)
                        mk[("kacc", ct, ng)] = self.inc("eVE", v.nop())
                    self.V(f)

                    if ng == STATE // 4 - 1:
                        def f(g, ct=ct):
                            fin = self.sb_k if STATE % 2 == 0 else self.sb_k2
                            self.w(g, "eVE", mk[("kacc", ct, STATE // 4 - 1)])
                            self.w(g, "eGP", zmark[0])
                            kst = "g_kpad%d" % (ct % 2)
                            self.dma(g, kst, self.t_kpad.ap()[P * ct:P * ct + P, 0:TAPS - 1],
                                     self.sb_z3[:, 0:TAPS - 1])
                            self.dma(g, kst, self.t_kpad.ap()[P * ct:P * ct + P, 2 * TAPS - 1:],
                                     self.sb_z3[:, 0:TAPS + 1])
                            mk[("kout", ct)] = self.dma(
                                g, kst, self.t_kpad.ap()[P * ct:P * ct + P, TAPS - 1:2 * TAPS - 1],
                                fin[:])
                        self.G(f)
                works.append(unit)

        kp_marks = [("g_kpad0", 0), ("g_kpad1", 0)]

        def wfull_one(j):
            def f(g, j=j):
                if j == 0:
                    kp_marks[0] = ("g_kpad0", self.counts.get("g_kpad0", 0))
                    kp_marks[1] = ("g_kpad1", self.counts.get("g_kpad1", 0))
                    self.wm(g, kp_marks[0])
                    self.wm(g, kp_marks[1])
                src = bass.AP(self.t_kpad, (TAPS - 1 - j), [[3 * TAPS, INNER], [1, 2 * TAPS]])
                dst = bass.AP(self.t_wfull, j * (2 * TAPS), [[P * 2 * TAPS, INNER], [1, 2 * TAPS]])
                self.dma(g, "g_wfull", dst, src)
                if j == P - 1:
                    self.qg_wfull = ("g_wfull", self.counts["g_wfull"])
            self.G(f)
        for j in range(P):
            works.append(lambda j=j: wfull_one(j))
        return works

    def ffn(self, t_in, t_outx, side_work=()):
        side_work = list(side_work)
        c, nc = self.c, self.nc
        D, L, H = c.D, c.L, c.H
        NH, ND = H // P, D // P
        TT = c.TT_FFN
        NT, NS = L // TT, TT // P
        HF = min(512, TT)
        NHF = TT // HF
        mk = {}
        tag = "ffnA" if t_outx is self.t_x1 else "ffnB"

        _es = contextlib.ExitStack()
        b_xs = _es.enter_context(nc.sbuf_tensor(tag + "_xs", [P, 2 * D], F32))
        b_xn = _es.enter_context(nc.sbuf_tensor(tag + "_xn", [P, 2 * D], BF16))
        b_xT = _es.enter_context(nc.sbuf_tensor(tag + "_xT", [P, ND * TT], BF16))
        b_h = _es.enter_context(nc.sbuf_tensor(tag + "_h", [P, NH * TT], BF16))
        b_g = _es.enter_context(nc.sbuf_tensor(tag + "_g", [P, NH * TT], BF16))
        b_w = _es.enter_context(nc.sbuf_tensor(tag + "_w", [P, 3 * H], BF16))
        b_t1 = _es.enter_context(nc.sbuf_tensor(tag + "_t1", [P, HF], F32))
        b_o = _es.enter_context(nc.sbuf_tensor(tag + "_o", [P, 2 * HF], F32))
        b_op = _es.enter_context(nc.sbuf_tensor(tag + "_op", [P, 2 * HF], F32))
        b_st = _es.enter_context(nc.sbuf_tensor(tag + "_st", [P, 64], F32))
        b_sq = _es.enter_context(nc.sbuf_tensor(tag + "_sq", [P, D], F32))
        mm0 = _es.enter_context(nc.psum_tensor(tag + "_p0", [P, 512], F32))
        mm1 = _es.enter_context(nc.psum_tensor(tag + "_p1", [P, 512], F32))
        mm2 = _es.enter_context(nc.psum_tensor(tag + "_p2", [P, 512], F32))
        mm3 = _es.enter_context(nc.psum_tensor(tag + "_p3", [P, 512], F32))
        tpa = _es.enter_context(nc.psum_tensor(tag + "_tpa", [P, 512], F32))
        tpb = _es.enter_context(nc.psum_tensor(tag + "_tpb", [P, 512], F32))
        tfa = _es.enter_context(nc.psum_tensor(tag + "_tfa", [P, 512], F32))
        tfb = _es.enter_context(nc.psum_tensor(tag + "_tfb", [P, 512], F32))
        with _es:
            tp2 = [tpa, tpb]
            tf2 = [tfa, tfb]
            mmps = [mm0, mm1, mm2, mm3]

            def f(g):
                for ch in range(8):
                    r0 = (L // 8) * ch
                    mk["precopy"] = self.dma(g, "g_pre", t_outx.ap()[r0:r0 + L // 8, :],
                                             t_in.ap()[r0:r0 + L // 8, :])
            self.G(f)

            for ti in range(NT):
                for s_ in range(NS):
                    sl = s_ % 2

                    def f(sy, ti=ti, s_=s_, sl=sl):
                        self.w(sy, "eVE", mk.get(("lnv", ti, s_ - 2),
                                                 mk.get(("lnv", ti - 1, NS - 2 + s_), 0)))
                        r0 = ti * TT + s_ * P
                        self.dma(sy, "s_x%d" % sl, b_xs[:, sl * D:sl * D + D],
                                 t_in.ap()[r0:r0 + P, :])
                    self.SY(f)
                    qwm = ("s_x%d" % sl, self.counts["s_x%d" % sl])

                    def f(s, sl=sl, qwm=qwm, ti=ti, s_=s_):
                        self.wm(s, qwm)
                        self.w(s, "eVE", mk.get(("lnr", ti, s_ - 1),
                                                 mk.get(("lnr", ti - 1, NS - 1 + s_), 0)))
                        s.activation(b_sq[:], b_xs[:, sl * D:sl * D + D], AF.Square)
                        mk[("lnsq", ti, s_)] = self.inc("eSC", s.nop())
                    self.S(f)

                    def f(v, sl=sl, qwm=qwm, ti=ti, s_=s_):
                        self.wm(v, qwm)
                        self.w(v, "eSC", mk[("lnsq", ti, s_)])
                        v.reduce_sum(b_st[:, s_ * 8 + 0:s_ * 8 + 1], b_xs[:, sl * D:sl * D + D], axis=AX)
                        mk[("lnr", ti, s_)] = self.inc("eVE", v.reduce_sum(b_st[:, s_ * 8 + 1:s_ * 8 + 2], b_sq[:], axis=AX))
                        v.tensor_scalar(b_st[:, s_ * 8 + 0:s_ * 8 + 1], b_st[:, s_ * 8 + 0:s_ * 8 + 1], 1.0 / D, None, ALU.mult)
                        v.tensor_scalar(b_st[:, s_ * 8 + 1:s_ * 8 + 2], b_st[:, s_ * 8 + 1:s_ * 8 + 2], 1.0 / D, None, ALU.mult)
                        v.tensor_scalar(b_st[:, s_ * 8 + 2:s_ * 8 + 3], b_st[:, s_ * 8 + 0:s_ * 8 + 1], b_st[:, s_ * 8 + 0:s_ * 8 + 1], None, ALU.mult)
                        self.inc("eVE", v.tensor_sub(b_st[:, s_ * 8 + 3:s_ * 8 + 4], b_st[:, s_ * 8 + 1:s_ * 8 + 2], b_st[:, s_ * 8 + 2:s_ * 8 + 3]))
                    self.V(f)
                    vevar = self.counts["eVE"]

                    def f(s, vevar=vevar):
                        self.w(s, "eVE", vevar)
                        self.inc("eSC", s.activation(b_st[:, s_ * 8 + 4:s_ * 8 + 5], b_st[:, s_ * 8 + 3:s_ * 8 + 4], AF.Sqrt,
                                                     bias=self.v_eps[:, 0:1]))
                    self.S(f)
                    scsd = self.counts["eSC"]

                    def f(v, sl=sl, scsd=scsd, ti=ti, s_=s_):
                        self.w(v, "eSC", scsd)
                        self.w(v, "ePE", mk.get(("xntp", ti, s_ - 2),
                                                mk.get(("xntp", ti - 1, NS - 2 + s_), 0)))
                        v.reciprocal(b_st[:, s_ * 8 + 5:s_ * 8 + 6], b_st[:, s_ * 8 + 4:s_ * 8 + 5])
                        mk[("lnv", ti, s_)] = self.inc("eVE", v.tensor_scalar(b_xn[:, sl * D:sl * D + D], b_xs[:, sl * D:sl * D + D],
                                        b_st[:, s_ * 8 + 0:s_ * 8 + 1], b_st[:, s_ * 8 + 5:s_ * 8 + 6], ALU.subtract, ALU.mult))
                    self.V(f)

                    for kt in range(ND):
                        pp = (s_ * ND + kt) % 2
                        tb = tp2[pp]

                        def f(t, ti=ti, s_=s_, sl=sl, kt=kt, tb=tb):
                            self.w(t, "eVE", mk[("lnv", ti, s_)])
                            self.w(t, "eSC", mk.get(("xTev", ti, s_ * ND + kt - 2), 0))
                            mm = self.inc("ePE", t.matmul(
                                tb[:, 0:P],
                                b_xn[:, sl * D + kt * P: sl * D + (kt + 1) * P], self.idnb[:], start=True, stop=True))
                            if kt == ND - 1:
                                mk[("xntp", ti, s_)] = mm
                        self.T(f)
                        pem = self.counts["ePE"]

                        def f(s, ti=ti, s_=s_, kt=kt, tb=tb, pem=pem):
                            self.w(s, "ePE", pem)
                            s.copy(b_xT[:, kt * TT + s_ * P: kt * TT + (s_ + 1) * P],
                                   tb[:, 0:P])
                            mk[("xTev", ti, s_ * ND + kt)] = self.inc("eSC", s.nop())
                        self.S(f)
                xT_ready = self.counts["eSC"]

                for ot in range(NH):
                    wsl = ot % 2
                    assert H >= 2 * D

                    def f(sy, ti=ti, ot=ot, wsl=wsl):
                        self.w(sy, "ePE", mk.get(("c1mm", ti, ot - 2), mk.get(("done", ti - 1), 0)))
                        self.dma(sy, "s_w%d" % wsl, b_w[:, wsl * H: wsl * H + D],
                                 self.t_w1p.ap()[ot])
                        self.dma(sy, "s_w%d" % wsl, b_w[:, wsl * H + D: wsl * H + 2 * D],
                                 self.t_w2p.ap()[ot])
                    self.SY(f)
                    qwm = ("s_w%d" % wsl, self.counts["s_w%d" % wsl])

                    for hf in range(NHF):
                        sidx = ot * NHF + hf
                        pa, pb = mmps[(2 * sidx) % 4], mmps[(2 * sidx + 1) % 4]

                        def f(t, ti=ti, ot=ot, hf=hf, wsl=wsl, qwm=qwm, pa=pa, pb=pb, sidx=sidx):
                            self.wm(t, qwm)
                            if ot == 0 and hf == 0:
                                self.w(t, "eSC", xT_ready)
                            self.w(t, "eVE", mk.get(("c1h", ti, sidx - 2),
                                                     mk.get(("c1h", ti - 1, NH * NHF - 2 + sidx), 0)))
                            for kt in range(ND):
                                rhs = b_xT[:, kt * TT + hf * HF: kt * TT + hf * HF + HF]
                                t.matmul(pa[:, 0:HF], b_w[:, wsl * H + kt * P: wsl * H + (kt + 1) * P],
                                         rhs, start=(kt == 0), stop=(kt == ND - 1))
                            for kt in range(ND):
                                rhs = b_xT[:, kt * TT + hf * HF: kt * TT + hf * HF + HF]
                                t.matmul(pb[:, 0:HF], b_w[:, wsl * H + D + kt * P: wsl * H + D + (kt + 1) * P],
                                         rhs, start=(kt == 0), stop=(kt == ND - 1))
                            mm = self.inc("ePE", t.nop())
                            if hf == NHF - 1:
                                mk[("c1mm", ti, ot)] = mm
                        self.T(f)
                        pem = self.counts["ePE"]

                        def f(s, ot=ot, hf=hf, pem=pem, pa=pa, ti=ti, sidx=sidx):
                            self.w(s, "ePE", pem)
                            self.w(s, "eVE", mk.get(("c1h", ti, sidx - 1),
                                                    mk.get(("c1h", ti - 1, NH * NHF - 1 + sidx), 0)))
                            s.activation(b_t1[:, 0:HF], pa[:, 0:HF],
                                         AF.Silu, scale=self.v_w1s[:, ot:ot + 1])
                            mk[("c1silu", ti, sidx)] = self.inc("eSC", s.nop())
                        self.S(f)
                        scm = self.counts["eSC"]

                        def f(v, ot=ot, hf=hf, scm=scm, pb=pb, ti=ti, sidx=sidx):
                            self.w(v, "eSC", scm)
                            mk[("c1h", ti, sidx)] = self.inc("eVE", v.scalar_tensor_tensor(
                                b_h[:, ot * TT + hf * HF: ot * TT + hf * HF + HF],
                                pb[:, 0:HF], self.v_w2s[:, ot:ot + 1],
                                b_t1[:, 0:HF], ALU.mult, ALU.mult))
                        self.V(f)
                h_ready = self.counts["eVE"]

                prev_done = ("c1mm", NH - 1)
                for stg, wp, vs, src, dst in [("g1", self.t_h1p, self.v_h1s, b_h, b_g),
                                              ("g2", self.t_h2p, self.v_h2s, b_g, b_h)]:
                    for ot in range(NH):
                        wsl = ot % 3

                        def f(sy, ti=ti, ot=ot, wsl=wsl, wp=wp, stg=stg, prev_done=prev_done):
                            if ot < 3:
                                self.w(sy, "ePE", mk.get((prev_done[0], ti, prev_done[1]), 0))
                            self.w(sy, "ePE", mk.get((stg + "mm", ti, ot - 3), 0))
                            self.dma(sy, "s_w%d" % wsl, b_w[:, wsl * H: wsl * H + H],
                                     wp.ap()[ot])
                        self.SY(f)
                        qwm = ("s_w%d" % wsl, self.counts["s_w%d" % wsl])

                        for hf in range(NHF):
                            sidx = ot * NHF + hf
                            pa = mmps[sidx % 4]

                            def f(t, ti=ti, ot=ot, hf=hf, wsl=wsl, qwm=qwm, pa=pa, stg=stg,
                                  src=src, sidx=sidx):
                                self.wm(t, qwm)
                                if ot == 0 and hf == 0 and stg == "g1":
                                    self.w(t, "eVE", h_ready)
                                if ot == 0 and hf == 0 and stg == "g2":
                                    self.w(t, "eSC", mk.get(("g1ev", ti, NH * NHF - 1), 0))
                                self.w(t, "eSC", mk.get((stg + "ev", ti, sidx - 4), 0))
                                for kt in range(NH):
                                    rhs = src[:, kt * TT + hf * HF: kt * TT + hf * HF + HF]
                                    t.matmul(pa[:, 0:HF], b_w[:, wsl * H + kt * P: wsl * H + (kt + 1) * P],
                                             rhs, start=(kt == 0), stop=(kt == NH - 1))
                                mm = self.inc("ePE", t.nop())
                                if hf == NHF - 1:
                                    mk[(stg + "mm", ti, ot)] = mm
                            self.T(f)
                            pem = self.counts["ePE"]

                            def f(s, ot=ot, hf=hf, pem=pem, pa=pa, vs=vs, dst=dst, stg=stg,
                                  ti=ti, sidx=sidx):
                                self.w(s, "ePE", pem)
                                s.activation(dst[:, ot * TT + hf * HF: ot * TT + hf * HF + HF],
                                             pa[:, 0:HF], AF.Gelu, scale=vs[:, ot:ot + 1])
                                mk[(stg + "ev", ti, sidx)] = self.inc("eSC", s.nop())
                            self.S(f)
                        if side_work:
                            side_work.pop(0)()
                    prev_done = (stg + "mm", NH - 1)

                for dt_ in range(ND):
                    wsl = dt_ % 3

                    def f(sy, ti=ti, dt_=dt_, wsl=wsl):
                        if dt_ < 3:
                            self.w(sy, "ePE", mk.get(("g2mm", ti, NH - 1), 0))
                        self.w(sy, "ePE", mk.get(("w3mm", ti, dt_ - 3), 0))
                        self.dma(sy, "s_w%d" % wsl, b_w[:, wsl * H: wsl * H + H],
                                 self.t_w3p.ap()[dt_])
                    self.SY(f)
                    qwm = ("s_w%d" % wsl, self.counts["s_w%d" % wsl])

                    for hf in range(NHF):
                        sidx = dt_ * NHF + hf
                        pa = mmps[sidx % 4]

                        def f(t, ti=ti, dt_=dt_, hf=hf, wsl=wsl, qwm=qwm, pa=pa, sidx=sidx):
                            self.wm(t, qwm)
                            if dt_ == 0 and hf == 0:
                                self.w(t, "eSC", mk.get(("g2ev", ti, NH * NHF - 1), 0))
                            self.w(t, "eSC", mk.get(("w3ev", ti, sidx - 4), 0))
                            for kt in range(NH):
                                rhs = b_h[:, kt * TT + hf * HF: kt * TT + hf * HF + HF]
                                t.matmul(pa[:, 0:HF], b_w[:, wsl * H + kt * P: wsl * H + (kt + 1) * P],
                                         rhs, start=(kt == 0), stop=(kt == NH - 1))
                            mm = self.inc("ePE", t.nop())
                            if hf == NHF - 1:
                                mk[("w3mm", ti, dt_)] = mm
                        self.T(f)
                        pem = self.counts["ePE"]

                        def f(s, dt_=dt_, hf=hf, pem=pem, pa=pa, ti=ti, sidx=sidx):
                            self.w(s, "ePE", pem)
                            self.w(s, "ePE", mk.get(("otp", ti, sidx - 2), 0))
                            s.activation(b_o[:, (sidx % 2) * HF:(sidx % 2) * HF + HF], pa[:, 0:HF],
                                         AF.Identity, scale=self.v_w3s[:, dt_:dt_ + 1])
                            mk[("w3ev", ti, sidx)] = self.inc("eSC", s.nop())
                        self.S(f)
                        scm = self.counts["eSC"]

                        for pj in range(HF // P):
                            pidx = sidx * (HF // P) + pj
                            tfb_ = tf2[pidx % 2]

                            def f(t, dt_=dt_, pj=pj, tfb_=tfb_, scm=scm, ti=ti, sidx=sidx, pidx=pidx):
                                self.w(t, "eSC", scm)
                                self.w(t, "eSC", mk.get(("opev", ti, pidx - 2), 0))
                                mm = self.inc("ePE", t.matmul(
                                    tfb_[:, 0:P],
                                    b_o[:, (sidx % 2) * HF + pj * P:(sidx % 2) * HF + (pj + 1) * P],
                                    self.idn[:], start=True, stop=True))
                                if pj == HF // P - 1:
                                    mk[("otp", ti, sidx)] = mm
                            self.T(f)
                            pet = self.counts["ePE"]

                            def f(s, pj=pj, tfb_=tfb_, pet=pet, ti=ti, sidx=sidx, pidx=pidx):
                                self.w(s, "ePE", pet)
                                self.wm(s, mk.get(("oacc", ti, sidx - 2),
                                                  mk.get(("oacc", ti - 1, ND * NHF - 2 + sidx), 0)))
                                s.copy(b_op[:, (sidx % 2) * HF + pj * P:(sidx % 2) * HF + (pj + 1) * P],
                                       tfb_[:, 0:P])
                                mk[("opev", ti, pidx)] = self.inc("eSC", s.nop())
                            self.S(f)
                        scm2 = self.counts["eSC"]

                        def f(g, ti=ti, dt_=dt_, hf=hf, scm2=scm2, sidx=sidx):
                            self.w(g, "eSC", scm2)
                            if ti == 0 and dt_ == 0 and hf == 0:
                                self.wm(g, mk["precopy"])
                            r0 = ti * TT + hf * HF
                            src = b_op[:, (sidx % 2) * HF:(sidx % 2) * HF + HF] \
                                .rearrange("p (pj q) -> p pj q", q=P)
                            dst3 = bass.AP(t_outx, r0 * D + dt_ * P,
                                           [[D, P], [P * D, HF // P], [1, P]])
                            mk[("oacc", ti, sidx)] = self.dma(
                                g, "g_oac%d" % (sidx % 2), dst3, src, accum_op=ALU.add)
                        self.G(f)
                mk[("done", ti)] = self.counts["ePE"]
            while side_work:
                side_work.pop(0)()


    def mixer_a1(self):
        c, nc = self.c, self.nc
        D, L, O3, INNER = c.D, c.L, c.O3, c.INNER
        ND, NO3, NI = D // P, O3 // P, INNER // P
        TT = 512
        NT, NS = L // TT, TT // P
        mk = {}

        _es = contextlib.ExitStack()
        b_xs = _es.enter_context(nc.sbuf_tensor("a1_xs", [P, 2 * D], F32))
        b_x1n = _es.enter_context(nc.sbuf_tensor("a1_x1n", [P, 2 * D], F32))
        b_r2 = _es.enter_context(nc.sbuf_tensor("a1_r2", [P, 2 * D], F32))
        b_xn = _es.enter_context(nc.sbuf_tensor("a1_xn", [P, 4 * D], BF16))
        b_xT = _es.enter_context(nc.sbuf_tensor("a1_xT", [P, 2 * ND * TT], BF16))
        b_w = _es.enter_context(nc.sbuf_tensor("a1_w", [P, NO3 * D], BF16))
        b_ev = _es.enter_context(nc.sbuf_tensor("a1_ev", [P, 2 * TT], F32))
        b_evb = _es.enter_context(nc.sbuf_tensor("a1_evb", [P, 2 * TT], BF16))
        b_st = _es.enter_context(nc.sbuf_tensor("a1_st", [P, 64], F32))
        b_sq = _es.enter_context(nc.sbuf_tensor("a1_sq", [P, D], F32))
        mm0 = _es.enter_context(nc.psum_tensor("a1_p0", [P, 512], F32))
        mm1 = _es.enter_context(nc.psum_tensor("a1_p1", [P, 512], F32))
        mm2 = _es.enter_context(nc.psum_tensor("a1_p2", [P, 512], F32))
        mm3 = _es.enter_context(nc.psum_tensor("a1_p3", [P, 512], F32))
        tpa = _es.enter_context(nc.psum_tensor("a1_tpa", [P, 512], F32))
        tpb = _es.enter_context(nc.psum_tensor("a1_tpb", [P, 512], F32))
        with _es:
            tp2 = [tpa, tpb]
            mmps = [mm0, mm1, mm2, mm3]

            inw_marks = []

            def f(sy):
                CH = NO3 // 4
                for g_ in range(4):
                    dst = b_w[:, g_ * CH * D:(g_ + 1) * CH * D].rearrange("p (a k) -> p a k", k=D)
                    src = self.t_inwp.ap()[g_ * CH:(g_ + 1) * CH].transpose([1, 0, 2])
                    inw_marks.append(self.dma(sy, "s_iw%d" % g_, dst, src))
            self.SY(f)

            def unit_a(ti, s_):
                sl = s_ % 2

                def f(sy, ti=ti, s_=s_, sl=sl):
                    self.w(sy, "eVE", mk.get(("ln2v", ti, s_ - 2),
                                             mk.get(("ln2v", ti - 1, NS - 2 + s_), 0)))
                    r0 = ti * TT + s_ * P
                    self.dma(sy, "s_x%d" % sl, b_xs[:, sl * D:sl * D + D],
                             self.t_x1.ap()[r0:r0 + P, :])
                self.SY(f)
                qwm = ("s_x%d" % sl, self.counts["s_x%d" % sl])
                mk[("xld", ti, s_)] = qwm

                def f(v, sl=sl, qwm=qwm, ti=ti, s_=s_):
                    self.wm(v, qwm)
                    self.w(v, "eVE", mk.get(("ln2r", ti, s_ - 1),
                                            mk.get(("ln2r", ti - 1, NS - 1 + s_), 0)))
                    v.tensor_mul(b_sq[:], b_xs[:, sl * D:sl * D + D], b_xs[:, sl * D:sl * D + D])
                    v.reduce_sum(b_st[:, s_ * 8 + 0:s_ * 8 + 1], b_xs[:, sl * D:sl * D + D], axis=AX)
                    mk[("ln1r", ti, s_)] = self.inc("eVE", v.reduce_sum(b_st[:, s_ * 8 + 1:s_ * 8 + 2], b_sq[:], axis=AX))
                    v.tensor_scalar(b_st[:, s_ * 8 + 0:s_ * 8 + 1], b_st[:, s_ * 8 + 0:s_ * 8 + 1], 1.0 / D, None, ALU.mult)
                    v.tensor_scalar(b_st[:, s_ * 8 + 1:s_ * 8 + 2], b_st[:, s_ * 8 + 1:s_ * 8 + 2], 1.0 / D, None, ALU.mult)
                    v.tensor_scalar(b_st[:, s_ * 8 + 2:s_ * 8 + 3], b_st[:, s_ * 8 + 0:s_ * 8 + 1], b_st[:, s_ * 8 + 0:s_ * 8 + 1], None, ALU.mult)
                    self.inc("eVE", v.tensor_sub(b_st[:, s_ * 8 + 3:s_ * 8 + 4], b_st[:, s_ * 8 + 1:s_ * 8 + 2], b_st[:, s_ * 8 + 2:s_ * 8 + 3]))
                self.V(f)
                vevar = self.counts["eVE"]

                def f(s, vevar=vevar, s_=s_):
                    self.w(s, "eVE", vevar)
                    self.inc("eSC", s.activation(b_st[:, s_ * 8 + 4:s_ * 8 + 5], b_st[:, s_ * 8 + 3:s_ * 8 + 4], AF.Sqrt,
                                                 bias=self.v_eps[:, 0:1]))
                self.S(f)
                mk[("sd1", ti, s_)] = self.counts["eSC"]

            def unit_b(ti, s_):
                sl = s_ % 2
                sl4 = s_ % 4

                def f(v, sl=sl, ti=ti, s_=s_):
                    self.w(v, "eSC", mk[("sd1", ti, s_)])
                    self.wm(v, mk.get(("r2o", ti, s_ - 2),
                                      mk.get(("r2o", ti - 1, NS - 2 + s_), 0)))
                    v.reciprocal(b_st[:, s_ * 8 + 5:s_ * 8 + 6], b_st[:, s_ * 8 + 4:s_ * 8 + 5])
                    v.tensor_scalar(b_x1n[:, sl * D:sl * D + D], b_xs[:, sl * D:sl * D + D],
                                    b_st[:, s_ * 8 + 0:s_ * 8 + 1], b_st[:, s_ * 8 + 5:s_ * 8 + 6], ALU.subtract, ALU.mult)
                    mk[("ln1v", ti, s_)] = self.inc("eVE", v.tensor_add(b_r2[:, sl * D:sl * D + D], b_xs[:, sl * D:sl * D + D],
                                 b_x1n[:, sl * D:sl * D + D]))
                self.V(f)
                ve1 = self.counts["eVE"]

                def f(g, ti=ti, s_=s_, sl=sl, ve1=ve1):
                    self.w(g, "eVE", ve1)
                    r0 = ti * TT + s_ * P
                    mk[("r2o", ti, s_)] = self.dma(
                        g, "g_r2%d" % (s_ % 2), self.t_r2.ap()[r0:r0 + P, :],
                        b_r2[:, sl * D:sl * D + D])
                self.G(f)

                def f(v, sl=sl, ti=ti, s_=s_):
                    v.tensor_mul(b_sq[:], b_x1n[:, sl * D:sl * D + D], b_x1n[:, sl * D:sl * D + D])
                    v.reduce_sum(b_st[:, s_ * 8 + 0:s_ * 8 + 1], b_x1n[:, sl * D:sl * D + D], axis=AX)
                    mk[("ln2r", ti, s_)] = self.inc("eVE", v.reduce_sum(b_st[:, s_ * 8 + 1:s_ * 8 + 2], b_sq[:], axis=AX))
                    v.tensor_scalar(b_st[:, s_ * 8 + 0:s_ * 8 + 1], b_st[:, s_ * 8 + 0:s_ * 8 + 1], 1.0 / D, None, ALU.mult)
                    v.tensor_scalar(b_st[:, s_ * 8 + 1:s_ * 8 + 2], b_st[:, s_ * 8 + 1:s_ * 8 + 2], 1.0 / D, None, ALU.mult)
                    v.tensor_scalar(b_st[:, s_ * 8 + 2:s_ * 8 + 3], b_st[:, s_ * 8 + 0:s_ * 8 + 1], b_st[:, s_ * 8 + 0:s_ * 8 + 1], None, ALU.mult)
                    self.inc("eVE", v.tensor_sub(b_st[:, s_ * 8 + 3:s_ * 8 + 4], b_st[:, s_ * 8 + 1:s_ * 8 + 2], b_st[:, s_ * 8 + 2:s_ * 8 + 3]))
                self.V(f)
                vevar2 = self.counts["eVE"]

                def f(s, vevar2=vevar2, s_=s_):
                    self.w(s, "eVE", vevar2)
                    self.inc("eSC", s.activation(b_st[:, s_ * 8 + 4:s_ * 8 + 5], b_st[:, s_ * 8 + 3:s_ * 8 + 4], AF.Sqrt,
                                                 bias=self.v_eps[:, 0:1]))
                self.S(f)
                scsd2 = self.counts["eSC"]

                def f(v, sl=sl, sl4=sl4, scsd2=scsd2, ti=ti, s_=s_):
                    self.w(v, "eSC", scsd2)
                    self.w(v, "ePE", mk.get(("xntp", ti, s_ - 4),
                                            mk.get(("xntp", ti - 1, NS - 4 + s_), 0)))
                    v.reciprocal(b_st[:, s_ * 8 + 5:s_ * 8 + 6], b_st[:, s_ * 8 + 4:s_ * 8 + 5])
                    mk[("ln2v", ti, s_)] = self.inc("eVE", v.tensor_scalar(b_xn[:, sl4 * D:sl4 * D + D], b_x1n[:, sl * D:sl * D + D],
                                    b_st[:, s_ * 8 + 0:s_ * 8 + 1], b_st[:, s_ * 8 + 5:s_ * 8 + 6], ALU.subtract, ALU.mult))
                self.V(f)

            def emit_LN_tp(ti):
                tsl = ti % 2
                for s_ in range(NS):
                    sl4 = s_ % 4
                    for kt in range(ND):
                        idx = s_ * ND + kt
                        tb = tp2[idx % 2]

                        def f(t, ti=ti, s_=s_, sl4=sl4, kt=kt, tb=tb, idx=idx):
                            self.w(t, "eVE", mk[("ln2v", ti, s_)])
                            peng, pcnt = mk.get(("xTev", ti, idx - 2),
                                                mk.get(("xTev", ti - 1, NS * ND - 2 + idx), ("eSC", 0)))
                            self.w(t, peng, pcnt)
                            mm = self.inc("ePE", t.matmul(
                                tb[:, 0:P],
                                b_xn[:, sl4 * D + kt * P: sl4 * D + (kt + 1) * P], self.idnb[:], start=True, stop=True))
                            if kt == ND - 1:
                                mk[("xntp", ti, s_)] = mm
                        self.T(f)
                        pem = self.counts["ePE"]
                        dstsl = lambda kt=kt, s_=s_, tsl=tsl: None

                        if idx % 2 == 0:
                            def f(s, ti=ti, s_=s_, kt=kt, tb=tb, pem=pem, tsl=tsl, idx=idx):
                                self.w(s, "ePE", pem)
                                s.copy(b_xT[:, tsl * ND * TT + kt * TT + s_ * P: tsl * ND * TT + kt * TT + (s_ + 1) * P],
                                       tb[:, 0:P])
                                mk[("xTev", ti, idx)] = ("eSC", self.inc("eSC", s.nop()))
                            self.S(f)
                        else:
                            def f(v, ti=ti, s_=s_, kt=kt, tb=tb, pem=pem, tsl=tsl, idx=idx):
                                self.w(v, "ePE", pem)
                                v.tensor_copy(b_xT[:, tsl * ND * TT + kt * TT + s_ * P: tsl * ND * TT + kt * TT + (s_ + 1) * P],
                                              tb[:, 0:P])
                                mk[("xTev", ti, idx)] = ("eVE", self.inc("eVE", v.nop()))
                            self.V(f)
                mk[("xT_ready", ti)] = (self.counts["eSC"], self.counts["eVE"])

            def emit_IP_ot(ti, ot):
                tsl = ti % 2
                pa = mmps[ot % 4]

                def f(t, ti=ti, ot=ot, pa=pa, tsl=tsl):
                    if ti == 0 and ot % (NO3 // 4) == 0:
                        self.wm(t, inw_marks[ot // (NO3 // 4)])
                    if ot == 0:
                        sc_c, ve_c = mk[("xT_ready", ti)]
                        self.w(t, "eSC", sc_c)
                        self.w(t, "eVE", ve_c)
                    self.w(t, "eSC", mk.get(("ipev", ti, ot - 4),
                                            mk.get(("ipev", ti - 1, NO3 - 4 + ot), 0)))
                    for kt in range(ND):
                        t.matmul(pa[:, 0:TT], b_w[:, ot * D + kt * P: ot * D + (kt + 1) * P],
                                 b_xT[:, tsl * ND * TT + kt * TT: tsl * ND * TT + (kt + 1) * TT],
                                 start=(kt == 0), stop=(kt == ND - 1))
                    mk[("ipmm", ti, ot)] = self.inc("ePE", t.nop())
                self.T(f)
                pem = self.counts["ePE"]

                def f(s, ti=ti, ot=ot, pa=pa, pem=pem):
                    self.w(s, "ePE", pem)
                    self.wm(s, mk.get(("ipo", ti, ot - 2),
                                      mk.get(("ipo", ti - 1, NO3 - 2 + ot), 0)))
                    bias = self.v_inb[:, ot:ot + 1]
                    if ot < NI:
                        s.activation(b_ev[:, (ot % 2) * TT:(ot % 2) * TT + TT], pa[:, 0:TT],
                                     AF.Identity, bias=bias)
                    elif ot < 2 * NI:
                        s.activation(b_evb[:, (ot % 2) * TT:(ot % 2) * TT + TT], pa[:, 0:TT],
                                     AF.Silu, bias=bias)
                    else:
                        s.activation(b_evb[:, (ot % 2) * TT:(ot % 2) * TT + TT], pa[:, 0:TT],
                                     AF.Sigmoid, bias=bias)
                    mk[("ipev", ti, ot)] = self.inc("eSC", s.nop())
                self.S(f)
                scm = self.counts["eSC"]

                def f(g, ti=ti, ot=ot, scm=scm):
                    self.w(g, "eSC", scm)
                    t0 = ti * TT
                    if ot < NI:
                        dst = self.t_u.ap()[P * ot:P * (ot + 1), t0:t0 + TT]
                        srcb = b_ev[:, (ot % 2) * TT:(ot % 2) * TT + TT]
                    elif ot < 2 * NI:
                        r = ot - NI
                        dst = self.t_sv.ap()[P * r:P * (r + 1), t0:t0 + TT]
                        srcb = b_evb[:, (ot % 2) * TT:(ot % 2) * TT + TT]
                    else:
                        r = ot - 2 * NI
                        dst = self.t_sg.ap()[P * r:P * (r + 1), t0:t0 + TT]
                        srcb = b_evb[:, (ot % 2) * TT:(ot % 2) * TT + TT]
                    mk[("ipo", ti, ot)] = self.dma(g, "g_ip%d" % (ot % 2), dst, srcb)
                self.G(f)

            for s_ in range(NS):
                unit_a(0, s_)
                unit_b(0, s_)
            emit_LN_tp(0)
            for ti in range(NT):
                pend = []
                if ti + 1 < NT:
                    for s_ in range(NS):
                        pend.append(lambda ti=ti, s_=s_: unit_a(ti + 1, s_))
                        pend.append(lambda ti=ti, s_=s_: unit_b(ti + 1, s_))
                gap = max(1, NO3 // (len(pend) + 1)) if pend else NO3
                for ot in range(NO3):
                    emit_IP_ot(ti, ot)
                    if pend and ot % gap == gap - 1:
                        pend.pop(0)()
                while pend:
                    pend.pop(0)()
                if ti + 1 < NT:
                    emit_LN_tp(ti + 1)

    def mixer_a2(self):
        c, nc = self.c, self.nc
        L, INNER = c.L, c.INNER
        NI = INNER // P
        CT = 512
        NTT = L // CT
        NPJ = CT // P
        mk = {}

        _es = contextlib.ExitStack()
        b_u = _es.enter_context(nc.sbuf_tensor("a2_u", [P, 2 * (CT + 2)], F32))
        b_c = _es.enter_context(nc.sbuf_tensor("a2_c", [P, 2 * CT], F32))
        b_s = _es.enter_context(nc.sbuf_tensor("a2_s", [P, 2 * CT], BF16))
        b_ev = _es.enter_context(nc.sbuf_tensor("a2_ev", [P, 4 * P], F32))
        b_uTB = _es.enter_context(nc.sbuf_tensor("a2_uTB", [P, 2 * L], BF16))
        tpa = _es.enter_context(nc.psum_tensor("a2_tpa", [P, 512], F32))
        tpb = _es.enter_context(nc.psum_tensor("a2_tpb", [P, 512], F32))
        with _es:
            tp2 = [tpa, tpb]
            for ct in range(NI):
                for tt in range(NTT):
                    sl = tt % 2
                    UW = CT + 2

                    def f(sy, ct=ct, tt=tt, sl=sl):
                        self.w(sy, "eVE", mk.get(("cv", ct, tt - 2), mk.get(("cv", ct - 1, NTT - 2 + tt), 0)))
                        t0 = tt * CT
                        lo = max(t0 - 1, 0)
                        hi = min(t0 + CT + 1, L)
                        off = 0 if tt > 0 else 1
                        self.dma(sy, "s_u%d" % sl,
                                 b_u[:, sl * UW + off: sl * UW + off + (hi - lo)],
                                 self.t_u.ap()[P * ct:P * (ct + 1), lo:hi])
                    self.SY(f)
                    qwm = ("s_u%d" % sl, self.counts["s_u%d" % sl])

                    if tt == 0 or tt == NTT - 1:
                        def f(g, ct=ct, tt=tt, sl=sl, qwm=qwm):
                            self.wm(g, qwm)
                            if tt == 0:
                                g.memset(b_u[:, sl * UW: sl * UW + 1], 0.0)
                            if tt == NTT - 1:
                                g.memset(b_u[:, sl * UW + CT + 1: sl * UW + CT + 2], 0.0)
                            mk[("hz", ct, tt)] = self.inc("eGP", g.nop())
                        self.G(f)

                    def f(v, ct=ct, tt=tt, sl=sl, qwm=qwm):
                        self.wm(v, qwm)
                        if ("hz", ct, tt) in mk:
                            self.w(v, "eGP", mk[("hz", ct, tt)])
                        self.w(v, "eSC", mk.get(("si", ct, tt - 2), mk.get(("si", ct - 1, NTT - 2 + tt), 0)))
                        k0 = self.v_convw[:, 3 * ct + 0:3 * ct + 1]
                        k1 = self.v_convw[:, 3 * ct + 1:3 * ct + 2]
                        k2 = self.v_convw[:, 3 * ct + 2:3 * ct + 3]
                        v.tensor_scalar(b_c[:, sl * CT: sl * CT + CT], b_u[:, sl * UW + 1: sl * UW + 1 + CT],
                                        k1, None, ALU.mult)
                        v.scalar_tensor_tensor(b_c[:, sl * CT: sl * CT + CT], b_u[:, sl * UW: sl * UW + CT],
                                               k0, b_c[:, sl * CT: sl * CT + CT], ALU.mult, ALU.add)
                        mk[("cv", ct, tt)] = self.inc("eVE", v.scalar_tensor_tensor(b_c[:, sl * CT: sl * CT + CT], b_u[:, sl * UW + 2: sl * UW + 2 + CT],
                                               k2, b_c[:, sl * CT: sl * CT + CT], ALU.mult, ALU.add))
                    self.V(f)
                    vem = self.counts["eVE"]

                    def f(s, ct=ct, tt=tt, sl=sl, vem=vem):
                        self.w(s, "eVE", vem)
                        self.w(s, "ePE", mk.get(("tp", ct, tt - 2), mk.get(("tp", ct - 1, NTT - 2 + tt), 0)))
                        s.activation(b_s[:, sl * CT: sl * CT + CT], b_c[:, sl * CT: sl * CT + CT],
                                     AF.Silu, bias=self.v_convb[:, ct:ct + 1])
                        mk[("si", ct, tt)] = self.inc("eSC", s.nop())
                    self.S(f)
                    scm = self.counts["eSC"]

                    for pj in range(NPJ):
                        gidx = (ct * NTT + tt) * NPJ + pj
                        psl = gidx % 4
                        tb = tp2[gidx % 2]

                        def f(t, ct=ct, tt=tt, sl=sl, pj=pj, tb=tb, scm=scm, gidx=gidx):
                            self.w(t, "eSC", scm)
                            self.w(t, "eSC", mk.get(("tev", gidx - 2), 0))
                            mm = self.inc("ePE", t.matmul(
                                tb[:, 0:P],
                                b_s[:, sl * CT + pj * P: sl * CT + (pj + 1) * P], self.idnb[:], start=True, stop=True))
                            if pj == NPJ - 1:
                                mk[("tp", ct, tt)] = mm
                        self.T(f)
                        pem = self.counts["ePE"]

                        def f(s, ct=ct, tt=tt, pj=pj, tb=tb, pem=pem, gidx=gidx):
                            self.w(s, "ePE", pem)
                            if tt == 0 and pj == 0:
                                self.wm(s, mk.get(("uto", ct - 2), 0))
                            i_ = tt * NPJ + pj
                            s.copy(b_uTB[:, (ct % 2) * L + i_ * P:(ct % 2) * L + (i_ + 1) * P],
                                   tb[:, 0:P])
                            mk[("tev", gidx)] = self.inc("eSC", s.nop())
                        self.S(f)

                if True:
                    sct = self.counts["eSC"]

                    def f(g, ct=ct, sct=sct):
                        self.w(g, "eSC", sct)
                        mk[("uto", ct)] = self.dma(
                            g, "g_uT%d" % (ct % 2), self.t_uT.ap()[ct],
                            b_uTB[:, (ct % 2) * L:(ct % 2) * L + L])
                    self.G(f)

    def scan(self):
        c, nc = self.c, self.nc
        L, INNER, TAPS = c.L, c.INNER, c.TAPS
        NI, NCH = INNER // P, c.NCH
        mk = {}
        GB = 8
        NG = P // 8
        NSL = 4
        BW = (NCH + 1) * P

        _es = contextlib.ExitStack()
        b_B = _es.enter_context(nc.sbuf_tensor("sc_B", [P, 2 * BW], BF16))
        b_w = _es.enter_context(nc.sbuf_tensor("sc_w", [P, NSL * GB * 2 * TAPS], BF16))
        b_S = _es.enter_context(nc.sbuf_tensor("sc_S", [P, 2 * NCH * P], BF16))
        b_yt = _es.enter_context(nc.sbuf_tensor("sc_yt", [P, 2 * P], BF16))
        b_sv = _es.enter_context(nc.sbuf_tensor("sc_sv", [P, 2 * L], BF16))
        b_sg = _es.enter_context(nc.sbuf_tensor("sc_sg", [P, 2 * L], BF16))
        b_z = _es.enter_context(nc.sbuf_tensor("sc_z", [P, L], BF16))
        ps0 = _es.enter_context(nc.psum_tensor("sc_p0", [P, 512], F32))
        ps1 = _es.enter_context(nc.psum_tensor("sc_p1", [P, 512], F32))
        ps2 = _es.enter_context(nc.psum_tensor("sc_p2", [P, 512], F32))
        ps3 = _es.enter_context(nc.psum_tensor("sc_p3", [P, 512], F32))
        tpa = _es.enter_context(nc.psum_tensor("sc_tpa", [P, 512], F32))
        tpb = _es.enter_context(nc.psum_tensor("sc_tpb", [P, 512], F32))
        with _es:
            tp2 = [tpa, tpb]
            ymm = [ps0, ps1, ps2, ps3]

            def f(g):
                self.inc("eGP", g.memset(b_B[:, 0:P], 0.0))
                self.inc("eGP", g.memset(b_B[:, BW:BW + P], 0.0))
            self.G(f)
            gpz = self.counts["eGP"]

            def emit_wf_load(g_, cb):
                def f(sy, cb=cb, g_=g_):
                    self.w(sy, "ePE", mk.get(("wcons", cb, g_ - NSL),
                                             mk.get(("wcons", cb - 1, NG - NSL + g_), 0)))
                    if cb == 0 and g_ < NSL:
                        self.wm(sy, self.qg_wfull)
                    src = self.t_wfull.ap()[cb * P + g_ * GB: cb * P + (g_ + 1) * GB] \
                        .transpose([1, 0, 2])
                    dst = b_w[:, (g_ % NSL) * GB * 2 * TAPS:((g_ % NSL) + 1) * GB * 2 * TAPS] \
                        .rearrange("p (c f) -> p c f", c=GB)
                    self.dma(sy, "s_wf%d" % (g_ % NSL), dst, src)
                self.SY(f)
                mk[("wld", cb, g_)] = ("s_wf%d" % (g_ % NSL), self.counts["s_wf%d" % (g_ % NSL)])

            def emit_A(cb):
                sl = cb % 2
                for g_ in range(NSL):
                    emit_wf_load(g_, cb)

                def f(sy, cb=cb, sl=sl):
                    self.w(sy, "eGP", gpz)
                    self.w(sy, "ePE", mk.get(("bfree", cb - 2), 0))
                    self.dma(sy, "s_B%d" % sl, b_B[:, sl * BW + P: sl * BW + BW],
                             self.t_uT.ap()[cb])
                self.SY(f)
                qwB = ("s_B%d" % sl, self.counts["s_B%d" % sl])

                def f(sy, cb=cb, sl=sl):
                    self.w(sy, "eVE", mk.get(("gate", cb - 2), 0))
                    self.dma(sy, "s_sv%d" % sl, b_sv[:, sl * L: sl * L + L],
                             self.t_sv.ap()[P * cb:P * (cb + 1), :])
                    self.dma(sy, "s_sv%d" % sl, b_sg[:, sl * L: sl * L + L],
                             self.t_sg.ap()[P * cb:P * (cb + 1), :])
                self.SY(f)
                mk[("svld", cb)] = ("s_sv%d" % sl, self.counts["s_sv%d" % sl])

                for gr in range(NG):
                    pa = ymm[gr % 4]

                    def f(t, cb=cb, gr=gr, sl=sl, pa=pa, qwB=qwB):
                        self.wm(t, qwB)
                        self.wm(t, mk[("wld", cb, gr)])
                        self.w(t, "eSC", mk.get(("yev", cb, gr - 4), mk.get(("yev", cb - 1, NG - 4 + gr), 0)))
                        for cr in range(8):
                            cl = gr * 8 + cr
                            wbase = (gr % NSL) * GB * 2 * TAPS + cr * 2 * TAPS
                            W0 = b_w[:, wbase: wbase + TAPS]
                            W1 = b_w[:, wbase + TAPS: wbase + 2 * TAPS]
                            X0 = b_B[:, sl * BW + P + cl: sl * BW + P + cl + (NCH - 1) * P + 1: P]
                            X1 = b_B[:, sl * BW + cl: sl * BW + cl + (NCH - 1) * P + 1: P]
                            t.matmul(pa[:, cr * NCH:(cr + 1) * NCH], W0, X0, start=True, stop=False)
                            t.matmul(pa[:, cr * NCH:(cr + 1) * NCH], W1, X1, start=False, stop=True)
                        mm = self.inc("ePE", t.nop())
                        mk[("ymm", cb, gr)] = mm
                        if gr == NG - 1:
                            mk[("bfree", cb)] = mm
                        mk[("wcons", cb, gr)] = mm
                    self.T(f)
                    pem = self.counts["ePE"]
                    if gr + NSL < NG:
                        emit_wf_load(gr + NSL, cb)

                    def f(s, cb=cb, gr=gr, sl=sl, pa=pa, pem=pem):
                        self.w(s, "ePE", pem)
                        if gr == 0:
                            self.w(s, "ePE", mk.get(("stp", cb - 2), 0))
                        src = pa[:, 0:8 * NCH].rearrange("p (c i) -> p c i", c=8)
                        dst = b_S[:, sl * NCH * P:(sl + 1) * NCH * P] \
                            .rearrange("p (i c) -> p c i", c=P)[:, gr * 8:(gr + 1) * 8, :]
                        s.copy(dst, src)
                        mk[("yev", cb, gr)] = self.inc("eSC", s.nop())
                    self.S(f)
                mk[("S_ready", cb)] = self.counts["eSC"]

            def emit_B(cb):
                sl = cb % 2
                for m_ in range(NCH):
                    tb = tp2[m_ % 2]

                    def f(t, cb=cb, m_=m_, sl=sl, tb=tb):
                        self.w(t, "eSC", mk[("S_ready", cb)])
                        self.w(t, "eVE", mk.get(("gvm", cb, m_ - 2),
                                                mk.get(("gvm", cb - 1, NCH - 2 + m_), 0)))
                        mm = self.inc("ePE", t.matmul(
                            tb[:, 0:P], b_S[:, sl * NCH * P + m_ * P: sl * NCH * P + (m_ + 1) * P],
                            self.idnb[:], start=True, stop=True))
                        if m_ == NCH - 1:
                            mk[("stp", cb)] = mm
                    self.T(f)
                    pem = self.counts["ePE"]

                    def f(v, cb=cb, m_=m_, sl=sl, tb=tb, pem=pem):
                        self.w(v, "ePE", pem)
                        if m_ == 0:
                            self.wm(v, mk[("svld", cb)])
                            self.wm(v, mk.get(("zo", cb - 1), 0))
                        mk[("gvm", cb, m_)] = self.inc("eVE", v.tensor_mul(
                            b_yt[:, (m_ % 2) * P:(m_ % 2 + 1) * P], tb[:, 0:P],
                            b_sg[:, sl * L + m_ * P: sl * L + (m_ + 1) * P]))
                        gv = self.inc("eVE", v.tensor_tensor(
                            b_z[:, m_ * P:(m_ + 1) * P],
                            b_yt[:, (m_ % 2) * P:(m_ % 2 + 1) * P],
                            b_sv[:, sl * L + m_ * P: sl * L + (m_ + 1) * P], ALU.add))
                        if m_ == NCH - 1:
                            mk[("gate", cb)] = gv
                    self.V(f)

                def f(g, cb=cb):
                    self.w(g, "eVE", mk[("gate", cb)])
                    mk[("zo", cb)] = self.dma(
                        g, "g_z", self.t_z.ap()[P * cb:P * (cb + 1), :], b_z[:])
                self.G(f)

            emit_A(0)
            for cb in range(NI):
                if cb + 1 < NI:
                    emit_A(cb + 1)
                emit_B(cb)

    def outproj(self):
        c, nc = self.c, self.nc
        D, L, INNER = c.D, c.L, c.INNER
        ND, NI = D // P, INNER // P
        TT = 512
        NT, NS = L // TT, TT // P
        mk = {}

        _es = contextlib.ExitStack()
        b_w = _es.enter_context(nc.sbuf_tensor("op_w", [P, ND * NI * P], BF16))
        b_z = _es.enter_context(nc.sbuf_tensor("op_z", [P, 2 * NI * TT], BF16))
        b_o = _es.enter_context(nc.sbuf_tensor("op_o", [P, 2 * TT], F32))
        b_op = _es.enter_context(nc.sbuf_tensor("op_op", [P, 2 * TT], F32))
        ps0 = _es.enter_context(nc.psum_tensor("op_p0", [P, 512], F32))
        ps1 = _es.enter_context(nc.psum_tensor("op_p1", [P, 512], F32))
        tfa = _es.enter_context(nc.psum_tensor("op_tfa", [P, 512], F32))
        tfb = _es.enter_context(nc.psum_tensor("op_tfb", [P, 512], F32))
        with _es:
            tf2 = [tfa, tfb]
            mmps = [ps0, ps1]

            def f(g):
                for ch in range(8):
                    r0 = (L // 8) * ch
                    mk["precopy"] = self.dma(g, "g_pre", self.t_x2.ap()[r0:r0 + L // 8, :],
                                             self.t_r2.ap()[r0:r0 + L // 8, :])
            self.G(f)

            def f(sy):
                for dt_ in range(ND):
                    self.dma(sy, "s_opw", b_w[:, dt_ * NI * P:(dt_ + 1) * NI * P],
                             self.t_outwp.ap()[dt_])
            self.SY(f)
            qww = ("s_opw", self.counts["s_opw"])

            for ti in range(NT):
                zsl = ti % 2

                def f(sy, ti=ti, zsl=zsl):
                    self.w(sy, "ePE", mk.get(("opmm", ti - 2, ND - 1), 0))
                    t0 = ti * TT
                    src = bass.AP(self.t_z, t0, [[L, P], [P * L, NI], [1, TT]])
                    dst = b_z[:, zsl * NI * TT:(zsl + 1) * NI * TT].rearrange("p (k t) -> p k t", t=TT)
                    self.dma(sy, "s_z%d" % zsl, dst, src)
                self.SY(f)
                qwz = ("s_z%d" % zsl, self.counts["s_z%d" % zsl])

                for dt_ in range(ND):
                    pa = mmps[dt_ % 2]

                    def f(t, ti=ti, dt_=dt_, zsl=zsl, pa=pa, qwz=qwz):
                        self.wm(t, qwz)
                        self.wm(t, qww)
                        self.w(t, "eSC", mk.get(("opev", ti, dt_ - 2), mk.get(("opev", ti - 1, ND - 2 + dt_), 0)))
                        for kt in range(NI):
                            t.matmul(pa[:, 0:TT], b_w[:, (dt_ * NI + kt) * P:(dt_ * NI + kt + 1) * P],
                                     b_z[:, zsl * NI * TT + kt * TT: zsl * NI * TT + (kt + 1) * TT],
                                     start=(kt == 0), stop=(kt == NI - 1))
                        mk[("opmm", ti, dt_)] = self.inc("ePE", t.nop())
                    self.T(f)
                    pem = self.counts["ePE"]

                    def f(s, ti=ti, dt_=dt_, pa=pa, pem=pem):
                        self.w(s, "ePE", pem)
                        self.w(s, "ePE", mk.get(("optp", ti, dt_ - 2), 0))
                        s.activation(b_o[:, (dt_ % 2) * TT:(dt_ % 2) * TT + TT], pa[:, 0:TT],
                                     AF.Identity, bias=self.v_outb[:, dt_:dt_ + 1])
                        mk[("opev", ti, dt_)] = self.inc("eSC", s.nop())
                    self.S(f)
                    scm = self.counts["eSC"]

                    for pj in range(NS):
                        pidx = (ti * ND + dt_) * NS + pj
                        tfb_ = tf2[pidx % 2]

                        def f(t, ti=ti, dt_=dt_, pj=pj, tfb_=tfb_, scm=scm, pidx=pidx):
                            self.w(t, "eSC", scm)
                            self.w(t, "eSC", mk.get(("oev", pidx - 2), 0))
                            mm = self.inc("ePE", t.matmul(
                                tfb_[:, 0:P],
                                b_o[:, (dt_ % 2) * TT + pj * P:(dt_ % 2) * TT + (pj + 1) * P],
                                self.idn[:], start=True, stop=True))
                            if pj == NS - 1:
                                mk[("optp", ti, dt_)] = mm
                        self.T(f)
                        pet = self.counts["ePE"]

                        def f(s, ti=ti, dt_=dt_, pj=pj, tfb_=tfb_, pet=pet, pidx=pidx):
                            self.w(s, "ePE", pet)
                            self.wm(s, mk.get(("oacc", ti, dt_ - 2),
                                              mk.get(("oacc", ti - 1, ND - 2 + dt_), 0)))
                            s.copy(b_op[:, (dt_ % 2) * TT + pj * P:(dt_ % 2) * TT + (pj + 1) * P],
                                   tfb_[:, 0:P])
                            mk[("oev", pidx)] = self.inc("eSC", s.nop())
                        self.S(f)
                    scm2 = self.counts["eSC"]

                    def f(g, ti=ti, dt_=dt_, scm2=scm2):
                        self.w(g, "eSC", scm2)
                        if ti == 0 and dt_ == 0:
                            self.wm(g, mk["precopy"])
                        r0 = ti * TT
                        dst3 = bass.AP(self.t_x2, r0 * D + dt_ * P, [[D, P], [P * D, NS], [1, P]])
                        src = b_op[:, (dt_ % 2) * TT:(dt_ % 2) * TT + TT].rearrange("p (pj q) -> p pj q", q=P)
                        mk[("oacc", ti, dt_)] = self.dma(g, "g_oac%d" % (dt_ % 2), dst3, src,
                                                         accum_op=ALU.add)
                    self.G(f)


def build_program(cfg: Cfg, stop_after=None):
    m = MK(cfg, stop_after)
    nc = m.build()
    m._stack.close()
    return nc, m


_CFG = Cfg(D=1024, L=8192, H=4096, STATE=64, TT_FFN=1024)
_NCORES = 8


def kernel(**inputs):
    from concourse.bass_utils import run_bass_kernel_spmd
    nc, m = build_program(_CFG, None)
    INNER = _CFG.INNER
    x = inputs["L0_attn_x"]
    base = {
        "norm_gamma": inputs["L0_attn_norm_gamma"],
        "norm_beta": inputs["L0_attn_norm_beta"],
        "in_w": inputs["L0_attn_in_w"],
        "in_b": inputs["L0_attn_in_b"],
        "conv_w": np.reshape(inputs["L0_attn_conv_w"], (INNER, 3)),
        "conv_b": inputs["L0_attn_conv_b"],
        "A_log": inputs["L0_attn_A_log"],
        "B_p": inputs["L0_attn_B_p"],
        "C_p": inputs["L0_attn_C_p"],
        "D_p": inputs["L0_attn_D_p"],
        "out_w": inputs["L0_attn_out_w"],
        "out_b": inputs["L0_attn_out_b"],
        "w1_w": inputs["L0_ffn_w1_w"], "w1_s": inputs["L0_ffn_w1_s"],
        "w2_w": inputs["L0_ffn_w2_w"], "w2_s": inputs["L0_ffn_w2_s"],
        "h1_w": inputs["L0_ffn_h1_w"], "h1_s": inputs["L0_ffn_h1_s"],
        "h2_w": inputs["L0_ffn_h2_w"], "h2_s": inputs["L0_ffn_h2_s"],
        "w3_w": inputs["L0_ffn_w3_w"], "w3_s": inputs["L0_ffn_w3_s"],
    }
    in_maps = [dict(base, x=np.ascontiguousarray(x[b])) for b in range(_NCORES)]
    res = run_bass_kernel_spmd(nc, in_maps, core_ids=list(range(_NCORES)))
    return np.stack([np.asarray(res.results[b]["out"]) for b in range(_NCORES)])
